# version 22
# baseline (speedup 1.0000x reference)
import contextlib
import numpy as np
import ml_dtypes

import concourse.bass as bass
import concourse.mybir as mybir
from concourse.bass_utils import run_bass_kernel_spmd

F32 = mybir.dt.float32
BF16 = mybir.dt.bfloat16
I32 = mybir.dt.int32
AF = mybir.ActivationFunctionType
ALU = mybir.AluOpType
AX = mybir.AxisListType.X

NCORES = 8
B, T, D = 4, 4096, 1024
TOK = B * T
SHARD = TOK // NCORES
E = 2048
SST = 16
FFN_I = 2816
LN_EPS = 1e-5

BF = ml_dtypes.bfloat16


def _bf16(x):
    return np.ascontiguousarray(np.asarray(x)).astype(BF)


def _f32(x):
    return np.ascontiguousarray(np.asarray(x), dtype=np.float32)


class Prog:
    def __init__(self):
        self.nc = bass.Bass("TRN2", target_bir_lowering=False)
        self.es = contextlib.ExitStack()
        self._scopes = [self.es]
        self.pfx = ""
        self.defer_pops = False
        self._deferred = []

    def push_scope(self):
        s = contextlib.ExitStack()
        self._scopes.append(s)
        return s

    def pop_scope(self):
        s = self._scopes.pop()
        if self.defer_pops:
            self._deferred.append(s)
        else:
            s.close()

    def close_deferred(self):
        while self._deferred:
            self._deferred.pop().close()

    def inp(self, name, shape, dtype=F32):
        return self.nc.dram_tensor(self.pfx + name, list(shape), dtype, kind="ExternalInput").ap()

    def out(self, name, shape, dtype=F32):
        return self.nc.dram_tensor(self.pfx + name, list(shape), dtype, kind="ExternalOutput").ap()

    def dram(self, name, shape, dtype=F32, shared=False):
        if shared:
            return self.nc.dram_tensor(self.pfx + name, list(shape), dtype,
                                       kind="Internal", addr_space="Shared").ap()
        return self.nc.dram_tensor(self.pfx + name, list(shape), dtype, kind="Internal").ap()

    def sb(self, name, shape, dtype=F32):
        return self._scopes[-1].enter_context(self.nc.sbuf_tensor(self.pfx + name, list(shape), dtype))

    def ps(self, name, shape, dtype=F32):
        return self._scopes[-1].enter_context(self.nc.psum_tensor(self.pfx + name, list(shape), dtype))

    DMA_SEM_NAMES = {
        "s_x", "s_w", "s_wl0", "s_wl1", "s_wo0", "s_wo1", "s_st", "s_ld",
        "s_ldi", "s_wg0", "s_wg1", "s_wu0", "s_wu1", "s_wd", "s_lt0", "s_lt1",
        "s_lt2", "s_lt3", "s_lt4", "s_lt5", "s_x0", "s_x1", "s_st0", "s_st1",
        "s_mk0", "s_mk1", "s_x30", "s_x31", "s_qd", "s_ws0", "s_ws1",
        "s_wst0", "s_wst1", "s_xo", "s_fw0", "s_fw1", "s_sl0", "s_sl1",
        "s_sl2", "s_sl3", "s_sl4", "s_sl5",
    }

    def sem(self, name):
        if getattr(self, "pool_mode", False):
            is_dma = name in self.DMA_SEM_NAMES
            pool, key = (self.pool_dma, "d") if is_dma else (self.pool_eng, "e")
            cur = self.pool_cursors[key]
            assert cur < len(pool), f"sem pool {key} exhausted ({name})"
            self.pool_cursors[key] = cur + 1
            return pool[cur]
        return self.es.enter_context(self.nc.semaphore(self.pfx + name))

    def sem_fixed(self, name):
        return self.es.enter_context(self.nc.semaphore(name))

    def init_pools(self, n_dma=20, n_eng=26):
        self.pool_dma = [self.es.enter_context(self.nc.semaphore(f"poolD{i}"))
                         for i in range(n_dma)]
        self.pool_eng = [self.es.enter_context(self.nc.semaphore(f"poolE{i}"))
                         for i in range(n_eng)]
        nums = sorted(s.num for s in self.pool_dma + self.pool_eng)
        assert nums == list(range(nums[0], nums[0] + len(nums))), nums
        self.pool_range = range(nums[0], nums[0] + len(nums))
        self.pool_mode = True

    def begin_phase_sems(self):
        self.pool_cursors = {"d": 0, "e": 0}

    def block(self):
        return self.es.enter_context(self.nc.Block())




class PhasedBlk:

    ENGINES = ("sync", "gpsimd", "vector", "scalar", "tensor")

    def __init__(self, blk, s_ph, s_clr, phase, pool_range, entry=None):
        self._blk = blk
        self._s_ph = s_ph
        self._s_clr = s_clr
        self._phase = phase
        self._range = pool_range
        self._entry = 5 * phase if entry is None else entry

    def __getattr__(self, name):
        if name not in self.ENGINES:
            raise AttributeError(name)
        real = getattr(self._blk, name)

        def deco(fn):
            def body(eng):
                if self._entry > 0:
                    eng.wait_ge(self._s_ph, self._entry)
                if name == "gpsimd":
                    if self._phase > 0:
                        eng.dma_reset(self._range)
                        eng.sem_clear(self._range)
                    eng.engine_nop().then_inc(self._s_clr, 1)
                else:
                    eng.wait_ge(self._s_clr, self._phase + 1)
                fn(eng)
                if name == "sync":
                    eng.nop().then_inc(self._s_ph, 1)
                elif name == "gpsimd":
                    eng.engine_nop().then_inc(self._s_ph, 1)
                else:
                    eng.drain().then_inc(self._s_ph, 1)
            body.__name__ = fn.__name__
            return real(body)
        return deco


class PlainBlk:

    ENGINES = PhasedBlk.ENGINES

    def __init__(self, blk, s_ph):
        self._blk = blk
        self._s_ph = s_ph

    def __getattr__(self, name):
        if name not in self.ENGINES:
            raise AttributeError(name)
        real = getattr(self._blk, name)

        def deco(fn):
            def body(eng):
                fn(eng)
                if name == "sync":
                    eng.nop().then_inc(self._s_ph, 1)
                elif name == "gpsimd":
                    eng.engine_nop().then_inc(self._s_ph, 1)
                else:
                    eng.drain().then_inc(self._s_ph, 1)
            body.__name__ = fn.__name__
            return real(body)
        return deco



def build_l0a(p=None, blk=None, io=None):
    fused = io is not None
    CHUNK = 512
    NT = CHUNK // 128
    NCH = SHARD // CHUNK
    NF = 2 * E // 128
    NE = E // 128
    ND = D // 128
    NHD = D // 512

    if not fused:
        p = Prog()
    else:
        p.push_scope()
        p.pfx = "a_"
    x_in = p.inp("x", [SHARD, D])
    w_inT = p.inp("w_inT", [D, 2 * E])
    w_bT = p.inp("w_bT", [E, SST])
    w_cT = p.inp("w_cT", [SST, E])
    w_outT = p.inp("w_outT", [E, D])
    b_in_c = p.inp("b_in_c", [128, NF])
    d_c = p.inp("d_c", [128, NE])
    b_out_r = p.inp("b_out_r", [1, D])
    ident = io["ident"] if fused else p.inp("ident", [128, 128], BF16)
    ones128 = io["ones128"] if fused else p.inp("ones128", [1, 128], BF16)
    x2_out = io["x2"] if fused else p.out("x2", [SHARD, D])

    xrs = [p.sb(f"xr{i}", [128, NT * D]) for i in range(2)]
    ln = p.sb("ln", [128, NT * D], BF16)
    lnT = p.sb("lnT", [128, ND * CHUNK], BF16)
    xprojT = p.sb("xprojT", [128, NE * CHUNK], BF16)
    zsilT = p.sb("zsilT", [128, NE * CHUNK], BF16)
    yT = p.sb("yT", [128, NE * CHUNK], BF16)
    wA = [p.sb(f"wA{i}", [128, D], F32) for i in range(2)]
    wAb = [p.sb(f"wAb{i}", [128, D], BF16) for i in range(2)]
    wstg = [p.sb(f"wstg{i}", [128, D], F32) for i in range(2)]
    wOb = p.sb("wOb", [128, NE * D], BF16)
    wb_sb = p.sb("wb_sb", [128, NE * SST], F32)
    wb_bf = p.sb("wb_bf", [128, NE * SST], BF16)
    wc_sb = p.sb("wc_sb", [SST, E], F32)
    wc_bf = p.sb("wc_bf", [SST, E], BF16)
    bin_sb = p.sb("bin_sb", [128, NF], F32)
    d_sb = p.sb("d_sb", [128, NE], F32)
    bout_sb = p.sb("bout_sb", [1, D], F32)
    bout_bf = p.sb("bout_bf", [1, D], BF16)
    id_sb = p.sb("id_sb", [128, 128], BF16)
    ones_sb = p.sb("ones_sb", [1, 128], BF16)
    hsT = p.sb("hsT", [SST, CHUNK], F32)
    hsTb = p.sb("hsTb", [SST, CHUNK], BF16)
    carry = p.sb("carry", [SST, 1], F32)
    c09 = p.sb("c09", [SST, 1], F32)
    m_t = p.sb("m_t", [128, NT], F32)
    t1_t = p.sb("t1_t", [128, NT], F32)
    sq_t = p.sb("sq_t", [128, NT], F32)
    var_t = p.sb("var_t", [128, NT], F32)
    rstd_t = p.sb("rstd_t", [128, NT], F32)
    scr = p.sb("scr", [128, D], BF16)

    pT = [p.ps(f"pT{i}", [128, 128], BF16) for i in range(2)]
    pMM = [p.ps(f"pMM{i}", [128, 512], F32) for i in range(2)]
    pB = p.ps("pB", [SST, CHUNK], F32)
    pC = [p.ps(f"pC{i}", [128, CHUNK], F32) for i in range(2)]

    s_x = [p.sem("s_x0"), p.sem("s_x1")]
    s_w = p.sem("s_w")
    s_wl = [p.sem("s_wl0"), p.sem("s_wl1")]
    s_wc = p.sem("s_wc")
    s_wo = [p.sem("s_wo0"), p.sem("s_wo1")]
    s_woc = p.sem("s_woc")
    s_sq = p.sem("s_sq")
    s_var = p.sem("s_var")
    s_rstd = p.sem("s_rstd")
    s_ln = p.sem("s_ln")
    s_lnT = p.sem("s_lnT")
    s_lnTe = p.sem("s_lnTe")
    s_mm = p.sem("s_mm")
    s_ev = p.sem("s_ev")
    s_b = p.sem("s_b")
    s_h = p.sem("s_h")
    s_c = p.sem("s_c")
    s_y = p.sem("s_y")
    s_o = p.sem("s_o")
    s_r = p.sem("s_r")
    s_st = [p.sem("s_st0"), p.sem("s_st1")]

    if not fused:
        blk = p.block()

    @blk.sync
    def _(sync):
        for c in range(NCH):
            if c > 1:
                sync.wait_ge(s_r, (c - 1) * NT * 2)
                sync.wait_ge(s_st[c % 2], 16 * (c // 2) * NT)
            sync.dma_start(
                out=xrs[c % 2][:].rearrange("p (tt d) -> p tt d", d=D),
                in_=x_in[c * CHUNK:(c + 1) * CHUNK, :].rearrange("(tt p) d -> p tt d", p=128),
            ).then_inc(s_x[c % 2], 16)
            for fc in range(NF):
                g = c * NF + fc
                if g >= 2:
                    sync.wait_ge(s_wc, g - 1)
                sync.dma_start(
                    out=wA[fc % 2][:].rearrange("p (kt f) -> p kt f", f=128),
                    in_=w_inT[:, fc * 128:(fc + 1) * 128].rearrange("(kt p) f -> p kt f", p=128),
                ).then_inc(s_wl[fc % 2], 16)

    @blk.gpsimd
    def _(g):
        g.dma_start(out=wb_sb[:].rearrange("p (et s) -> p et s", s=SST),
                    in_=w_bT.rearrange("(et p) s -> p et s", p=128)).then_inc(s_w, 16)
        g.dma_start(out=wc_sb[:], in_=w_cT).then_inc(s_w, 16)
        g.dma_start(out=bin_sb[:], in_=b_in_c).then_inc(s_w, 16)
        g.dma_start(out=d_sb[:], in_=d_c).then_inc(s_w, 16)
        g.dma_start(out=bout_sb[:], in_=b_out_r).then_inc(s_w, 16)
        g.dma_start(out=id_sb[:], in_=ident).then_inc(s_w, 16)
        g.dma_start(out=ones_sb[:], in_=ones128).then_inc(s_w, 16)
        for ec in range(NE):
            if ec >= 2:
                g.wait_ge(s_woc, ec - 1)
            g.dma_start(
                out=wstg[ec % 2][:],
                in_=w_outT[ec * 128:(ec + 1) * 128, :],
            ).then_inc(s_wo[ec % 2], 16)
        for c in range(NCH):
            for mt in range(NT):
                g.wait_ge(s_r, (c * NT + mt + 1) * 2)
                g.dma_start(
                    out=x2_out[c * CHUNK + mt * 128: c * CHUNK + (mt + 1) * 128, :],
                    in_=xrs[c % 2][:, mt * D:(mt + 1) * D],
                ).then_inc(s_st[c % 2], 16)
        g.wait_ge(s_st[0], 16 * ((NCH + 1) // 2) * NT)
        g.wait_ge(s_st[1], 16 * (NCH // 2) * NT)

    @blk.vector
    def _(vector):
        vector.memset(carry[:], 0.0)
        vector.memset(c09[:], 0.9)
        vector.drain()
        for ec in range(NE):
            vector.wait_ge(s_wo[ec % 2], 16 * (ec // 2 + 1))
            vector.tensor_copy(wOb[:, ec * D:(ec + 1) * D], wstg[ec % 2][:])
            vector.drain()
            vector.engine_nop().then_inc(s_woc, 1)
        vector.wait_ge(s_w, 16 * 7)
        vector.tensor_scalar(wb_bf[:], wb_sb[:], 0.1, None, ALU.mult)
        vector.tensor_copy(wc_bf[:], wc_sb[:])
        vector.tensor_copy(bout_bf[:], bout_sb[:])
        vector.drain()
        for c in range(NCH):
            vector.wait_ge(s_x[c % 2], 16 * (c // 2 + 1))
            for tt in range(NT):
                vector.reduce_sum(m_t[:, tt:tt + 1], xrs[c % 2][:, tt * D:(tt + 1) * D], AX)
            vector.drain()
            vector.tensor_scalar(m_t[:], m_t[:], 1.0 / D, None, ALU.mult)
            vector.drain()
            vector.tensor_tensor(t1_t[:], m_t[:], m_t[:], ALU.mult)
            vector.wait_ge(s_sq, c * NT + NT)
            vector.tensor_scalar(var_t[:], sq_t[:], 1.0 / D, None, ALU.mult)
            vector.drain()
            vector.tensor_tensor(var_t[:], var_t[:], t1_t[:], ALU.subtract)
            vector.drain()
            vector.tensor_scalar(var_t[:], var_t[:], LN_EPS, None, ALU.add)
            vector.drain()
            vector.engine_nop().then_inc(s_var, 1)
            vector.wait_ge(s_rstd, c + 1)
            for tt in range(NT):
                vector.tensor_scalar(
                    ln[:, tt * D:(tt + 1) * D], xrs[c % 2][:, tt * D:(tt + 1) * D],
                    m_t[:, tt:tt + 1], rstd_t[:, tt:tt + 1], ALU.subtract, ALU.mult,
                ).then_inc(s_ln, 1)
            vector.drain()
            for fc in range(NF):
                g = c * NF + fc
                par = fc % 2
                vector.wait_ge(s_wl[fc % 2], 16 * (g // 2 + 1))
                if g >= 2:
                    vector.wait_ge(s_mm, g - 1)
                vector.tensor_copy(wAb[par][:], wA[par][:])
                vector.drain()
                vector.engine_nop().then_inc(s_wc, 1)
            vector.wait_ge(s_b, c + 1)
            vector.tensor_tensor_scan(
                hsT[:], c09[:, 0:1].broadcast_to([SST, CHUNK]), pB[:],
                carry[:, 0:1], ALU.mult, ALU.add)
            vector.drain()
            vector.tensor_copy(carry[:, 0:1], hsT[:, CHUNK - 1:CHUNK])
            vector.tensor_copy(hsTb[:], hsT[:])
            vector.drain()
            vector.engine_nop().then_inc(s_h, 1)
            vector.wait_ge(s_ev, c * NF + NF)
            for ec in range(NE):
                g = c * NE + ec
                vector.wait_ge(s_c, g + 1)
                sl = slice(ec * CHUNK, (ec + 1) * CHUNK)
                vector.scalar_tensor_tensor(
                    yT[:, sl], xprojT[:, sl], d_sb[:, ec:ec + 1], pC[g % 2][:],
                    ALU.mult, ALU.add)
                vector.drain()
                vector.tensor_tensor(yT[:, sl], yT[:, sl], zsilT[:, sl], ALU.mult)
                vector.drain()
                vector.engine_nop().then_inc(s_y, 1)
            for mt in range(NT):
                for nh in range(2):
                    g = (c * NT + mt) * 2 + nh
                    vector.wait_ge(s_o, g + 1)
                    sl = slice(mt * D + nh * 512, mt * D + (nh + 1) * 512)
                    vector.tensor_tensor(xrs[c % 2][:, sl], xrs[c % 2][:, sl], pMM[g % 2][:], ALU.add)
                    vector.drain()
                    vector.engine_nop().then_inc(s_r, 1)

    @blk.scalar
    def _(scalar):
        for c in range(NCH):
            scalar.wait_ge(s_x[c % 2], 16 * (c // 2 + 1))
            for tt in range(NT):
                scalar.activation(scr[:], xrs[c % 2][:, tt * D:(tt + 1) * D], AF.Square,
                                  accum_out=sq_t[:, tt:tt + 1]).then_inc(s_sq, 1)
                scalar.drain()
            scalar.wait_ge(s_var, c + 1)
            scalar.activation(rstd_t[:], var_t[:], AF.Ln)
            scalar.drain()
            scalar.activation(rstd_t[:], rstd_t[:], AF.Exp, scale=-0.5).then_inc(s_rstd, 1)
            for i in range(NT * ND):
                scalar.wait_ge(s_lnT, c * NT * ND + i + 1)
                tt, dt = divmod(i, ND)
                scalar.copy(lnT[:, dt * CHUNK + tt * 128: dt * CHUNK + (tt + 1) * 128],
                            pT[i % 2][:]).then_inc(s_lnTe, 1)
            for fc in range(NF):
                g = c * NF + fc
                scalar.wait_ge(s_mm, g + 1)
                dst = xprojT if fc < NE else zsilT
                eci = fc if fc < NE else fc - NE
                scalar.activation(dst[:, eci * CHUNK:(eci + 1) * CHUNK], pMM[fc % 2][:],
                                  AF.Silu, bias=bin_sb[:, fc:fc + 1]).then_inc(s_ev, 1)
                scalar.drain()

    @blk.tensor
    def _(tensor):
        tensor.wait_ge(s_w, 16 * 7)
        for c in range(NCH):
            for i in range(NT * ND):
                tt, dt = divmod(i, ND)
                tensor.wait_ge(s_ln, c * NT + tt + 1)
                g = c * NT * ND + i
                if g >= 2:
                    tensor.wait_ge(s_lnTe, g - 1)
                tensor.transpose(pT[i % 2][:], ln[:, tt * D + dt * 128: tt * D + (dt + 1) * 128],
                                 id_sb[:]).then_inc(s_lnT, 1)
            tensor.wait_ge(s_lnTe, (c + 1) * NT * ND)
            if c > 0:
                tensor.wait_ge(s_r, c * NT * 2)
            for fc in range(NF):
                g = c * NF + fc
                tensor.wait_ge(s_wc, g + 1)
                if g >= 2:
                    tensor.wait_ge(s_ev, g - 1)
                par = fc % 2
                for kt in range(ND):
                    mm = tensor.matmul(
                        pMM[par][:],
                        wAb[par][:, kt * 128:(kt + 1) * 128],
                        lnT[:, kt * CHUNK: kt * CHUNK + CHUNK],
                        start=(kt == 0), stop=(kt == ND - 1),
                    )
                    if kt == ND - 1:
                        mm.then_inc(s_mm, 1)
            tensor.wait_ge(s_ev, c * NF + NE)
            if c > 0:
                tensor.wait_ge(s_h, c)
            for ec in range(NE):
                mm = tensor.matmul(
                    pB[:],
                    wb_bf[:, ec * SST:(ec + 1) * SST],
                    xprojT[:, ec * CHUNK:(ec + 1) * CHUNK],
                    start=(ec == 0), stop=(ec == NE - 1),
                )
                if ec == NE - 1:
                    mm.then_inc(s_b, 1)
            tensor.wait_ge(s_h, c + 1)
            for ec in range(NE):
                g = c * NE + ec
                if g >= 2:
                    tensor.wait_ge(s_y, g - 1)
                tensor.matmul(
                    pC[g % 2][:],
                    wc_bf[:, ec * 128:(ec + 1) * 128],
                    hsTb[:],
                    start=True, stop=True,
                ).then_inc(s_c, 1)
            tensor.wait_ge(s_y, (c + 1) * NE)
            tensor.wait_ge(s_woc, NE)
            tensor.wait_ge(s_ev, (c + 1) * NF)
            for mt in range(NT):
                for nh in range(NHD):
                    g = (c * NT + mt) * NHD + nh
                    if g >= 2:
                        tensor.wait_ge(s_r, g - 1)
                    par = g % 2
                    for ec in range(NE):
                        tensor.matmul(
                            pMM[par][:],
                            yT[:, ec * CHUNK + mt * 128: ec * CHUNK + (mt + 1) * 128],
                            wOb[:, ec * D + nh * 512: ec * D + nh * 512 + 512],
                            start=(ec == 0), stop=False,
                        )
                    tensor.matmul(
                        pMM[par][:],
                        ones_sb[:],
                        bout_bf[:, nh * 512:(nh + 1) * 512],
                        start=False, stop=True,
                    ).then_inc(s_o, 1)

    if fused:
        p.pop_scope()
        p.pfx = ""
    return p


def host_prep_l0a(inputs):
    com = {
        "w_inT": _f32(inputs["L0_attn_W_in"].T),
        "w_bT": _f32(inputs["L0_attn_W_B"].T),
        "w_cT": _f32(inputs["L0_attn_W_C"].T),
        "w_outT": _f32(inputs["L0_attn_W_out"].T),
        "b_in_c": _f32(inputs["L0_attn_b_in"].reshape(2 * E // 128, 128).T),
        "d_c": _f32(inputs["L0_attn_D"].reshape(E // 128, 128).T),
        "b_out_r": _f32(inputs["L0_attn_b_out"].reshape(1, D)),
        "ident": np.eye(128, dtype=BF),
        "ones128": np.ones((1, 128), dtype=BF),
    }
    return com


def run_l0a(inputs):
    x = _f32(inputs["L0_attn_x"]).reshape(TOK, D)
    com = host_prep_l0a(inputs)
    p = build_l0a()
    in_maps = []
    for k in range(NCORES):
        m = dict(com)
        m["x"] = x[k * SHARD:(k + 1) * SHARD]
        in_maps.append(m)
    res = run_bass_kernel_spmd(p.nc, in_maps, core_ids=list(range(NCORES)))
    x2 = np.concatenate([np.asarray(r["x2"]) for r in res.results], axis=0)
    return x2.reshape(B, T, D)


def kernel(**inputs):
    return kernel_fused(inputs)


def kernel_unfused(**inputs):
    x2 = run_l0a(inputs).reshape(TOK, D)
    wq = run_l0f_wq(inputs)
    x3 = run_l0f(inputs, np.ascontiguousarray(x2), wq)
    ln3, xgs = run_c0(inputs, x3)
    q1, q3 = run_c1(inputs, xgs)
    x4 = run_c2(inputs, ln3, q1, q3, x3)
    out = run_l1f(inputs, x4)
    return _f32(out.reshape(B, T, D))



class QuantScratch:
    def __init__(self, p, name, F, nblk):
        self.F = F
        self.nblk = nblk
        self.bm = p.sb(f"{name}_bm", [128, nblk], F32)
        self.r = p.sb(f"{name}_r", [128, nblk], F32)
        self.sh = p.sb(f"{name}_sh", [128, nblk], F32)
        self.xr = p.sb(f"{name}_xr", [128, F], BF16)
        self.u = p.sb(f"{name}_u", [128, F], BF16)
        self.sg = p.sb(f"{name}_sg", [128, F], BF16)
        self.a = p.sb(f"{name}_a", [128, F], BF16)
        self.b = p.sb(f"{name}_b", [128, F], BF16)
        self.c = p.sb(f"{name}_c", [128, F], BF16)


def quant_v_pre(vector, q, x_ap, F):
    nblk = F // 16
    bm, r, sh, xr = q.bm[:, 0:nblk], q.r[:, 0:nblk], q.sh[:, 0:nblk], q.xr[:, 0:F]
    vector.tensor_reduce(
        bm.rearrange("p (n o) -> p n o", o=1),
        x_ap.rearrange("p (n b) -> p n b", b=16),
        AX, ALU.max, apply_absolute_value=True)
    vector.drain()
    vector.tensor_scalar(bm, bm, 1.0 / 6.0, 1e-12, ALU.mult, ALU.max)
    vector.drain()
    vector.reciprocal(r, bm)
    vector.tensor_scalar(sh, bm, 0.5, None, ALU.mult)
    vector.drain()
    rb = r.rearrange("p (n o) -> p n o", o=1).broadcast_to([128, nblk, 16])
    vector.tensor_tensor(xr.rearrange("p (n b) -> p n b", b=16), x_ap.rearrange("p (n b) -> p n b", b=16), rb, ALU.mult)
    vector.drain()


def quant_s_mid(scalar, q, F):
    scalar.activation(q.u[:, 0:F], q.xr[:, 0:F], AF.Abs)
    return scalar.activation(q.sg[:, 0:F], q.xr[:, 0:F], AF.Sign)


def quant_v_chain(vector, q, out_ap, F):
    nblk = F // 16
    u = q.u[:, 0:F]
    a, b_, cc, sg = q.a[:, 0:F], q.b[:, 0:F], q.c[:, 0:F], q.sg[:, 0:F]
    vector.tensor_single_scalar(a, u, 0.25, ALU.is_gt)
    vector.tensor_single_scalar(b_, u, 5.0, ALU.is_gt)
    vector.drain()
    vector.scalar_tensor_tensor(a, u, 0.75, a, ALU.is_gt, ALU.add)
    vector.scalar_tensor_tensor(cc, u, 3.5, b_, ALU.is_gt, ALU.add)
    vector.drain()
    vector.scalar_tensor_tensor(a, u, 1.25, a, ALU.is_gt, ALU.add)
    vector.scalar_tensor_tensor(cc, u, 2.5, cc, ALU.is_gt, ALU.add)
    vector.drain()
    vector.scalar_tensor_tensor(a, u, 1.75, a, ALU.is_gt, ALU.add)
    vector.tensor_tensor(b_, cc, b_, ALU.add)
    vector.drain()
    vector.scalar_tensor_tensor(cc, b_, 2.0, a, ALU.mult, ALU.add)
    vector.drain()
    vector.tensor_tensor(cc, cc, sg, ALU.mult)
    vector.drain()
    shb = q.sh[:, 0:nblk].rearrange("p (n o) -> p n o", o=1).broadcast_to([128, nblk, 16])
    vector.tensor_tensor(out_ap.rearrange("p (n b) -> p n b", b=16), cc.rearrange("p (n b) -> p n b", b=16), shb, ALU.mult)
    vector.drain()



WG_ROWS = FFN_I // NCORES
WD_ROWS = D // NCORES


def build_l0f_wq(p=None, blk=None, io=None):
    fused = io is not None
    if not fused:
        p = Prog()
    else:
        p.push_scope()
        p.pfx = "q_"
    wg_in = p.inp("wg", [WG_ROWS, D])
    wu_in = p.inp("wu", [WG_ROWS, D])
    wd_in = p.inp("wd", [WD_ROWS, FFN_I])
    ident = io["ident"] if fused else p.inp("ident", [128, 128], BF16)
    wgT_out = io["wgstg"] if fused else p.out("wgT", [D, WG_ROWS], BF16)
    wuT_out = io["wustg"] if fused else p.out("wuT", [D, WG_ROWS], BF16)
    wdT_out = io["wdstg"] if fused else p.out("wdT", [FFN_I, WD_ROWS], BF16)

    FH = 1024
    xts = [p.sb(f"xt{i}", [128, FH], F32) for i in range(2)]
    xq = p.sb("xq", [128, FH], BF16)
    q = QuantScratch(p, "q", FH, FH // 16)
    q1_ = QuantScratch.__new__(QuantScratch)
    q1_.__dict__.update(q.__dict__)
    q1_.xr = p.sb("q_xr1", [128, FH], BF16)
    q1_.sh = p.sb("q_sh1", [128, FH // 16], F32)
    qqs = [q, q1_]
    id_sb = p.sb("id_sb", [128, 128], BF16)
    outT = p.sb("outT", [128, 128 * 8], BF16)
    pT = [p.ps("pT0", [128, 128], BF16)] * 2

    s_ld2 = [p.sem("s_x0"), p.sem("s_x1")]
    s_v1 = p.sem("s_v1")
    s_s1 = p.sem("s_s1")
    s_q = p.sem("s_q")
    s_tr = p.sem("s_tr")
    s_te = p.sem("s_te")
    s_st = p.sem("s_st")

    tiles = []
    for wsrc, wout in ((wg_in, wgT_out), (wu_in, wuT_out)):
        for i, rows in enumerate((128, 128, 96)):
            tiles.append((wsrc, i * 128, rows, D, 0, wout))
    for c0, Fq in ((0, 1024), (1024, 1024), (2048, 768)):
        tiles.append((wd_in, 0, 128, Fq, c0, wdT_out))

    s_init = p.sem("s_init")
    s_ldi = p.sem("s_ldi")

    if not fused:
        blk = p.block()

    @blk.sync
    def _(sync):
        sync.wait_ge(s_init, 1)
        sync.dma_start(out=id_sb[:], in_=ident).then_inc(s_ldi, 16)
        for i, (src, r0, rows, F, c0, _) in enumerate(tiles):
            if i >= 2:
                sync.wait_ge(s_v1, i - 1)
            sync.dma_start(out=xts[i % 2][0:rows, 0:F],
                           in_=src[r0:r0 + rows, c0:c0 + F]).then_inc(s_ld2[i % 2], 16)

    @blk.vector
    def _(vector):
        vector.memset(xts[0][:], 0.0)
        vector.memset(xts[1][:], 0.0)
        vector.drain()
        vector.engine_nop().then_inc(s_init, 1)

        def wq_pre(i):
            vector.wait_ge(s_ld2[i % 2], 16 * (i // 2 + 1))
            quant_v_pre(vector, qqs[i % 2], xts[i % 2][:, 0:tiles[i][3]], tiles[i][3])
            vector.engine_nop().then_inc(s_v1, 1)

        wq_pre(0)
        for i, (src, r0, rows, F, c0, _) in enumerate(tiles):
            if i + 1 < len(tiles):
                wq_pre(i + 1)
            vector.wait_ge(s_s1, i + 1)
            if i >= 1:
                vector.wait_ge(s_tr, sum(t[3] // 128 for t in tiles[:i]))
            quant_v_chain(vector, qqs[i % 2], xq[:, 0:F], F)
            vector.engine_nop().then_inc(s_q, 1)

    @blk.scalar
    def _(scalar):
        for i, (src, r0, rows, F, c0, wout) in enumerate(tiles):
            scalar.wait_ge(s_v1, i + 1)
            if i >= 1:
                scalar.wait_ge(s_q, i)
            quant_s_mid(scalar, qqs[i % 2], F).then_inc(s_s1, 1)
            ntr = F // 128
            nst = sum((t[3] // 128 + 10) // 11 for t in tiles[:i])
            if i >= 1:
                scalar.wait_ge(s_st, 16 * nst)
            for j in range(ntr):
                if j > 0 and j % 11 == 0:
                    scalar.wait_ge(s_st, 16 * (nst + j // 11))
                scalar.wait_ge(s_tr, sum(t[3] // 128 for t in tiles[:i]) + j + 1)
                scalar.copy(outT[0:128, (j % 11) * 128:(j % 11) * 128 + rows],
                            pT[j % 2][:, 0:rows]).then_inc(s_te, 1)

    @blk.tensor
    def _(tensor):
        tensor.wait_ge(s_ldi, 16)
        n = 0
        for i, (src, r0, rows, F, c0, wout) in enumerate(tiles):
            tensor.wait_ge(s_q, i + 1)
            ntr = F // 128
            for j in range(ntr):
                if n >= 1:
                    tensor.wait_ge(s_te, n)
                tensor.transpose(pT[0][:, 0:rows], xq[0:rows, j * 128:(j + 1) * 128],
                                 id_sb[0:rows, 0:rows]).then_inc(s_tr, 1)
                n += 1

    @blk.gpsimd
    def _(g):
        nst_total = 0
        for i, (src, r0, rows, F, c0, wout) in enumerate(tiles):
            ntr = F // 128
            base = sum(t[3] // 128 for t in tiles[:i])
            for g0 in range(0, ntr, 11):
                ng = min(11, ntr - g0)
                g.wait_ge(s_te, base + g0 + ng)
                g.dma_start(
                    out=wout[c0 + g0 * 128:c0 + (g0 + ng) * 128, r0:r0 + rows].rearrange("(j c) r -> c j r", c=128),
                    in_=outT[0:128, 0:ng * 128].rearrange("p (j f) -> p j f", f=128)[:, :, 0:rows],
                ).then_inc(s_st, 16)
                nst_total += 1
        g.wait_ge(s_st, 16 * nst_total)
        if fused:
            s_cc = io["s_cc"]
            g.collective_compute("AllGather", ALU.bypass,
                                 replica_groups=[list(range(NCORES))],
                                 ins=[io["wgstg"].opt()], outs=[io["wgg"].opt()]).then_inc(s_cc, 1)
            g.collective_compute("AllGather", ALU.bypass,
                                 replica_groups=[list(range(NCORES))],
                                 ins=[io["wustg"].opt()], outs=[io["wug"].opt()]).then_inc(s_cc, 1)
            g.collective_compute("AllGather", ALU.bypass,
                                 replica_groups=[list(range(NCORES))],
                                 ins=[io["wdstg"].opt()], outs=[io["wdg"].opt()]).then_inc(s_cc, 1)
            g.wait_ge(s_cc, 3)

    if fused:
        p.pop_scope()
        p.pfx = ""
    return p



def build_l0f(p=None, blk=None, io=None):
    fused = io is not None
    CHUNK = 512
    NT = CHUNK // 128
    NCH = SHARD // CHUNK
    ND = D // 128
    NI = FFN_I // 128
    IC_W = [512] * 5 + [256]
    NIC = len(IC_W)

    if not fused:
        p = Prog()
    else:
        p.push_scope()
        p.pfx = "f_"
    if fused:
        x_in = io["x2"]
        wgg, wug, wdg = io["wgg"], io["wug"], io["wdg"]
        x3_out = io["x3"]
    else:
        x_in = p.inp("x2", [SHARD, D])
        wgT_in = p.inp("wgT", [D, FFN_I], BF16)
        wuT_in = p.inp("wuT", [D, FFN_I], BF16)
        wdT_in = p.inp("wdT", [FFN_I, D], BF16)
        x3_out = p.out("x3", [SHARD, D])
    ident = io["ident"] if fused else p.inp("ident", [128, 128], BF16)

    xr = p.sb("xr", [128, NT * D])
    ln2 = p.sb("ln2", [128, NT * D], BF16)
    xq = p.sb("xq", [128, NT * D], BF16)
    xqT = p.sb("xqT", [128, ND * CHUNK], BF16)
    h = p.sb("h", [128, NT * FFN_I], BF16)
    hqT = p.sb("hqT", [128, NI * CHUNK], BF16)
    wgS = [p.sb(f"wgS{i}", [128, ND * 512], BF16) for i in range(2)]
    wuS = [p.sb(f"wuS{i}", [128, ND * 512], BF16) for i in range(2)]
    wdS = p.sb("wdS", [128, NI * D], BF16)
    id_sb = p.sb("id_sb", [128, 128], BF16)
    tsil = p.sb("tsil", [128, 512], F32)
    m_t = p.sb("m_t", [128, NT], F32)
    t1_t = p.sb("t1_t", [128, NT], F32)
    sq_t = p.sb("sq_t", [128, NT], F32)
    var_t = p.sb("var_t", [128, NT], F32)
    rstd_t = p.sb("rstd_t", [128, NT], F32)
    scr = p.sb("scr", [128, D], BF16)

    class QS:
        pass
    _bm = p.sb("q_bm", [128, FFN_I // 16], F32)
    _r = p.sb("q_r", [128, FFN_I // 16], F32)
    _a = p.sb("q_a", [128, FFN_I], BF16)
    _b = p.sb("q_b", [128, FFN_I], BF16)
    _c = p.sb("q_c", [128, FFN_I], BF16)
    _u = p.sb("q_u", [128, FFN_I], BF16)
    _sg = p.sb("q_sg", [128, FFN_I], BF16)
    qs = []
    for i in range(2):
        q_ = QS()
        q_.bm, q_.r, q_.a, q_.b, q_.c = _bm, _r, _a, _b, _c
        q_.u, q_.sg = _u, _sg
        q_.sh = p.sb(f"q_sh{i}", [128, FFN_I // 16], F32)
        q_.xr = p.sb(f"q_xr{i}", [128, FFN_I], BF16)
        qs.append(q_)

    pG = [p.ps(f"pG{i}", [128, 512], F32) for i in range(2)]
    pU = [p.ps(f"pU{i}", [128, 512], F32) for i in range(2)]
    pDD = [p.ps(f"pDD{i}", [128, 512], F32) for i in range(2)]
    pT = [p.ps(f"pT{i}", [128, 128], BF16) for i in range(2)]

    s_x = p.sem("s_x")
    s_wg = [p.sem("s_wg0"), p.sem("s_wg1")]
    s_wu = [p.sem("s_wu0"), p.sem("s_wu1")]
    s_wd = p.sem("s_wd")
    s_sq = p.sem("s_sq")
    s_var = p.sem("s_var")
    s_rstd = p.sem("s_rstd")
    s_ln = p.sem("s_ln")
    s_qv1 = p.sem("s_qv1")
    s_qs1 = p.sem("s_qs1")
    s_qc = p.sem("s_qc")
    s_txq = p.sem("s_txq")
    s_txe = p.sem("s_txe")
    s_gu = p.sem("s_gu")
    s_sil = p.sem("s_sil")
    s_hm = p.sem("s_hm")
    s_dmm = p.sem("s_dmm")
    s_res = p.sem("s_res")
    s_st = p.sem("s_st")

    NTR_X = NT * ND
    NTR_H = NT * NI
    NTR = NTR_X + NTR_H
    NGU = NIC * NT
    NDN = NT * 2

    if not fused:
        blk = p.block()

    def _pieces(off, w):
        out = []
        i = off
        while i < off + w:
            r = i // WG_ROWS
            c0 = i - r * WG_ROWS
            c1 = min(WG_ROWS, c0 + (off + w - i))
            out.append((r, c0, c1))
            i += c1 - c0
        return out

    @blk.sync
    def _(sync):
        for c in range(NCH):
            if c > 0:
                sync.wait_ge(s_res, c * NDN)
                sync.wait_ge(s_st, 16 * c * NT)
            sync.dma_start(
                out=xr[:].rearrange("p (tt d) -> p tt d", d=D),
                in_=x_in[c * CHUNK:(c + 1) * CHUNK, :].rearrange("(tt p) d -> p tt d", p=128),
            ).then_inc(s_x, 16)
            off = 0
            for ic in range(NIC):
                w = IC_W[ic]
                par = ic % 2
                g = c * NIC + ic
                if g >= 2:
                    sync.wait_ge(s_gu, (g - 1) * NT)
                if fused:
                    dst0 = 0
                    for (r, c0, c1) in _pieces(off, w):
                        pw_ = c1 - c0
                        sync.dma_start(
                            out=wgS[par][:].rearrange("p (kt f) -> p kt f", f=512)[:, :, dst0:dst0 + pw_],
                            in_=wgg[r * D:(r + 1) * D, c0:c1].rearrange("(kt p) f -> p kt f", p=128),
                        ).then_inc(s_wg[par], 16)
                        sync.dma_start(
                            out=wuS[par][:].rearrange("p (kt f) -> p kt f", f=512)[:, :, dst0:dst0 + pw_],
                            in_=wug[r * D:(r + 1) * D, c0:c1].rearrange("(kt p) f -> p kt f", p=128),
                        ).then_inc(s_wu[par], 16)
                        dst0 += pw_
                else:
                    sync.dma_start(
                        out=wgS[par][:].rearrange("p (kt f) -> p kt f", f=512)[:, :, 0:w],
                        in_=wgT_in[:, off:off + w].rearrange("(kt p) f -> p kt f", p=128),
                    ).then_inc(s_wg[par], 16)
                    sync.dma_start(
                        out=wuS[par][:].rearrange("p (kt f) -> p kt f", f=512)[:, :, 0:w],
                        in_=wuT_in[:, off:off + w].rearrange("(kt p) f -> p kt f", p=128),
                    ).then_inc(s_wu[par], 16)
                off += w

    @blk.gpsimd
    def _(g):
        g.dma_start(out=id_sb[:], in_=ident).then_inc(s_wd, 16)
        if fused:
            for r in range(NCORES):
                g.dma_start(
                    out=wdS[:].rearrange("p (kt d) -> p kt d", d=D)[:, :, r * 128:(r + 1) * 128],
                    in_=wdg[r * FFN_I:(r + 1) * FFN_I, :].rearrange("(kt p) c -> p kt c", p=128),
                ).then_inc(s_wd, 16)
        else:
            g.dma_start(out=wdS[:].rearrange("p (kt d) -> p kt d", d=D),
                        in_=wdT_in.rearrange("(kt p) d -> p kt d", p=128)).then_inc(s_wd, 16)
        for c in range(NCH):
            for mt in range(NT):
                g.wait_ge(s_res, c * NDN + (mt + 1) * 2)
                g.dma_start(
                    out=x3_out[c * CHUNK + mt * 128: c * CHUNK + (mt + 1) * 128, :],
                    in_=xr[:, mt * D:(mt + 1) * D],
                ).then_inc(s_st, 16)
        g.wait_ge(s_st, 16 * NCH * NT)

    @blk.vector
    def _(vector):
        for c in range(NCH):
            vector.wait_ge(s_x, 16 * (c + 1))
            for tt in range(NT):
                vector.reduce_sum(m_t[:, tt:tt + 1], xr[:, tt * D:(tt + 1) * D], AX)
            vector.drain()
            vector.tensor_scalar(m_t[:], m_t[:], 1.0 / D, None, ALU.mult)
            vector.drain()
            vector.tensor_tensor(t1_t[:], m_t[:], m_t[:], ALU.mult)
            vector.wait_ge(s_sq, c * NT + NT)
            vector.tensor_scalar(var_t[:], sq_t[:], 1.0 / D, None, ALU.mult)
            vector.drain()
            vector.tensor_tensor(var_t[:], var_t[:], t1_t[:], ALU.subtract)
            vector.drain()
            vector.tensor_scalar(var_t[:], var_t[:], LN_EPS, None, ALU.add)
            vector.drain()
            vector.engine_nop().then_inc(s_var, 1)
            vector.wait_ge(s_rstd, c + 1)
            for tt in range(NT):
                vector.tensor_scalar(
                    ln2[:, tt * D:(tt + 1) * D], xr[:, tt * D:(tt + 1) * D],
                    m_t[:, tt:tt + 1], rstd_t[:, tt:tt + 1], ALU.subtract, ALU.mult,
                ).then_inc(s_ln, 1)
            vector.drain()
            def v_q_pre(ts_, ap, F):
                if ts_ >= 2:
                    vector.wait_ge(s_qs1, ts_ - 1)
                quant_v_pre(vector, qs[ts_ % 2], ap, F)
                vector.engine_nop().then_inc(s_qv1, 1)

            def v_q_chain(ts_, ap, F):
                vector.wait_ge(s_qs1, ts_ + 1)
                quant_v_chain(vector, qs[ts_ % 2], ap, F)
                vector.engine_nop().then_inc(s_qc, 1)

            for tt in range(NT):
                ts_ = c * 8 + tt
                v_q_pre(ts_, ln2[:, tt * D:(tt + 1) * D], D)
                if tt >= 1:
                    v_q_chain(ts_ - 1, xq[:, (tt - 1) * D:tt * D], D)
            v_q_chain(c * 8 + NT - 1, xq[:, (NT - 1) * D:NT * D], D)
            for ic in range(NIC):
                w = IC_W[ic]
                ioff = sum(IC_W[:ic])
                for mt in range(NT):
                    slot = c * NGU + ic * NT + mt
                    vector.wait_ge(s_sil, slot + 1)
                    vector.tensor_tensor(
                        h[:, mt * FFN_I + ioff: mt * FFN_I + ioff + w],
                        tsil[:, 0:w], pU[slot % 2][:, 0:w], ALU.mult)
                    vector.drain()
                    vector.engine_nop().then_inc(s_hm, 1)
            for mt in range(NT):
                ts_ = c * 8 + NT + mt
                v_q_pre(ts_, h[:, mt * FFN_I:(mt + 1) * FFN_I], FFN_I)
                if mt >= 1:
                    v_q_chain(ts_ - 1, h[:, (mt - 1) * FFN_I:mt * FFN_I], FFN_I)
            v_q_chain(c * 8 + 2 * NT - 1, h[:, (NT - 1) * FFN_I:NT * FFN_I], FFN_I)
            for mt in range(NT):
                for nh in range(2):
                    slot = c * NDN + mt * 2 + nh
                    vector.wait_ge(s_dmm, slot + 1)
                    sl = slice(mt * D + nh * 512, mt * D + (nh + 1) * 512)
                    vector.tensor_tensor(xr[:, sl], xr[:, sl], pDD[slot % 2][:], ALU.add)
                    vector.drain()
                    vector.engine_nop().then_inc(s_res, 1)

    @blk.scalar
    def _(scalar):
        for c in range(NCH):
            scalar.wait_ge(s_x, 16 * (c + 1))
            for tt in range(NT):
                scalar.activation(scr[:], xr[:, tt * D:(tt + 1) * D], AF.Square,
                                  accum_out=sq_t[:, tt:tt + 1]).then_inc(s_sq, 1)
                scalar.drain()
            scalar.wait_ge(s_var, c + 1)
            scalar.activation(rstd_t[:], var_t[:], AF.Ln)
            scalar.drain()
            scalar.activation(rstd_t[:], rstd_t[:], AF.Exp, scale=-0.5).then_inc(s_rstd, 1)
            for tt in range(NT):
                ts_ = c * 8 + tt
                scalar.wait_ge(s_qv1, ts_ + 1)
                if ts_ >= 1:
                    scalar.wait_ge(s_qc, ts_)
                quant_s_mid(scalar, qs[ts_ % 2], D).then_inc(s_qs1, 1)
            for i in range(NTR_X):
                n = c * NTR + i
                scalar.wait_ge(s_txq, n + 1)
                tt, dt = divmod(i, ND)
                scalar.copy(xqT[:, dt * CHUNK + tt * 128: dt * CHUNK + (tt + 1) * 128],
                            pT[n % 2][:]).then_inc(s_txe, 1)
            for ic in range(NIC):
                w = IC_W[ic]
                for mt in range(NT):
                    slot = c * NGU + ic * NT + mt
                    scalar.wait_ge(s_gu, slot + 1)
                    if slot >= 1:
                        scalar.wait_ge(s_hm, slot)
                    scalar.activation(tsil[:, 0:w], pG[slot % 2][:, 0:w], AF.Silu).then_inc(s_sil, 1)
                    scalar.drain()
            for mt in range(NT):
                ts_ = c * 8 + NT + mt
                scalar.wait_ge(s_qv1, ts_ + 1)
                if ts_ >= 1:
                    scalar.wait_ge(s_qc, ts_)
                quant_s_mid(scalar, qs[ts_ % 2], FFN_I).then_inc(s_qs1, 1)
            for i in range(NTR_H):
                n = c * NTR + NTR_X + i
                scalar.wait_ge(s_txq, n + 1)
                mt, kt = divmod(i, NI)
                scalar.copy(hqT[:, kt * CHUNK + mt * 128: kt * CHUNK + (mt + 1) * 128],
                            pT[n % 2][:]).then_inc(s_txe, 1)

    @blk.tensor
    def _(tensor):
        tensor.wait_ge(s_wd, 16 * (1 + NCORES) if fused else 32)
        for c in range(NCH):
            for i in range(NTR_X):
                tt, dt = divmod(i, ND)
                n = c * NTR + i
                tensor.wait_ge(s_qc, c * 8 + tt + 1)
                if n >= 2:
                    tensor.wait_ge(s_txe, n - 1)
                tensor.transpose(pT[n % 2][:], xq[:, tt * D + dt * 128: tt * D + (dt + 1) * 128],
                                 id_sb[:]).then_inc(s_txq, 1)
            tensor.wait_ge(s_txe, c * NTR + NTR_X)
            for ic in range(NIC):
                w = IC_W[ic]
                par = ic % 2
                g = c * NIC + ic
                if fused:
                    npieces = 0
                    for gi in range(g + 1):
                        if gi % 2 == g % 2:
                            ici = gi % NIC
                            npieces += len(_pieces(sum(IC_W[:ici]), IC_W[ici]))
                    tensor.wait_ge(s_wg[par], 16 * npieces)
                    tensor.wait_ge(s_wu[par], 16 * npieces)
                else:
                    tensor.wait_ge(s_wg[par], 16 * (g // 2 + 1))
                    tensor.wait_ge(s_wu[par], 16 * (g // 2 + 1))
                for mt in range(NT):
                    slot = c * NGU + ic * NT + mt
                    if slot >= 2:
                        tensor.wait_ge(s_sil, slot - 1)
                        tensor.wait_ge(s_hm, slot - 1)
                    for kt in range(ND):
                        tensor.matmul(
                            pG[slot % 2][:, 0:w],
                            xqT[:, kt * CHUNK + mt * 128: kt * CHUNK + (mt + 1) * 128],
                            wgS[par][:, kt * 512: kt * 512 + w],
                            start=(kt == 0), stop=(kt == ND - 1))
                    for kt in range(ND):
                        mm = tensor.matmul(
                            pU[slot % 2][:, 0:w],
                            xqT[:, kt * CHUNK + mt * 128: kt * CHUNK + (mt + 1) * 128],
                            wuS[par][:, kt * 512: kt * 512 + w],
                            start=(kt == 0), stop=(kt == ND - 1))
                        if kt == ND - 1:
                            mm.then_inc(s_gu, 1)
            for i in range(NTR_H):
                mt, kt = divmod(i, NI)
                n = c * NTR + NTR_X + i
                tensor.wait_ge(s_qc, c * 8 + NT + mt + 1)
                tensor.wait_ge(s_txe, n - 1)
                tensor.transpose(pT[n % 2][:], h[:, mt * FFN_I + kt * 128: mt * FFN_I + (kt + 1) * 128],
                                 id_sb[:]).then_inc(s_txq, 1)
            tensor.wait_ge(s_txe, (c + 1) * NTR)
            for mt in range(NT):
                for nh in range(2):
                    slot = c * NDN + mt * 2 + nh
                    if slot >= 2:
                        tensor.wait_ge(s_res, slot - 1)
                    for kt in range(NI):
                        mm = tensor.matmul(
                            pDD[slot % 2][:],
                            hqT[:, kt * CHUNK + mt * 128: kt * CHUNK + (mt + 1) * 128],
                            wdS[:, kt * D + nh * 512: kt * D + nh * 512 + 512],
                            start=(kt == 0), stop=(kt == NI - 1))
                        if kt == NI - 1:
                            mm.then_inc(s_dmm, 1)

    if fused:
        p.pop_scope()
        p.pfx = ""
    return p


def run_l0f(inputs, x2, wq):
    p = build_l0f()
    ident = np.eye(128, dtype=BF)
    in_maps = []
    for k in range(NCORES):
        in_maps.append({
            "x2": np.ascontiguousarray(x2[k * SHARD:(k + 1) * SHARD]),
            "wgT": np.ascontiguousarray(wq["wgT"]), "wuT": np.ascontiguousarray(wq["wuT"]),
            "wdT": np.ascontiguousarray(wq["wdT"]), "ident": ident})
    res = run_bass_kernel_spmd(p.nc, in_maps, core_ids=list(range(NCORES)))
    return np.concatenate([np.asarray(r["x3"]) for r in res.results], axis=0)


def run_l0f_wq(inputs):
    p = build_l0f_wq()
    wg, wu, wd = (_f32(inputs[k]) for k in ("L0_ffn_w_gate", "L0_ffn_w_up", "L0_ffn_w_down"))
    ident = np.eye(128, dtype=BF)
    in_maps = []
    for k in range(NCORES):
        in_maps.append({
            "wg": wg[k * WG_ROWS:(k + 1) * WG_ROWS], "wu": wu[k * WG_ROWS:(k + 1) * WG_ROWS],
            "wd": wd[k * WD_ROWS:(k + 1) * WD_ROWS], "ident": ident})
    res = run_bass_kernel_spmd(p.nc, in_maps, core_ids=list(range(NCORES)))
    return {
        "wgT": np.concatenate([np.asarray(r["wgT"]) for r in res.results], axis=1),
        "wuT": np.concatenate([np.asarray(r["wuT"]) for r in res.results], axis=1),
        "wdT": np.concatenate([np.asarray(r["wdT"]) for r in res.results], axis=1),
    }



def build_c0(p=None, blk=None, io=None):
    fused = io is not None
    CHUNK = 512
    NT = CHUNK // 128
    NCH = SHARD // CHUNK
    ND = D // 128

    if not fused:
        p = Prog()
    else:
        p.push_scope()
        p.pfx = "c_"
    x_in = io["x3"] if fused else p.inp("x3", [SHARD, D])
    wih = p.inp("wihT", [D, 80])
    brow = p.inp("brow", [2, 80])
    ident = io["ident"] if fused else p.inp("ident", [128, 128], BF16)
    identf = io["identf"] if fused else p.inp("identf", [128, 128], F32)
    ones128 = io["ones128"] if fused else p.inp("ones128", [1, 128], BF16)
    ln3_out = io["ln3stg"] if fused else p.out("ln3", [SHARD, D], BF16)
    xgT_out = io["xgstg"] if fused else p.out("xgT", [80, SHARD], BF16)

    xr = p.sb("xr", [128, NT * D])
    ln = p.sb("ln", [128, NT * D], BF16)
    lnT = p.sb("lnT", [128, ND * CHUNK], BF16)
    wih_sb = p.sb("wih_sb", [128, ND * 80], F32)
    wih_bf = p.sb("wih_bf", [128, ND * 80], BF16)
    brow_sb = [p.sb("brow_sb0", [1, 80], F32), p.sb("brow_sb1", [1, 80], F32)]
    brow_bf = [p.sb("brow_bf0", [1, 80], BF16), p.sb("brow_bf1", [1, 80], BF16)]
    id_sb = p.sb("id_sb", [128, 128], BF16)
    idf_sb = p.sb("idf_sb", [128, 128], F32)
    ones_sb = p.sb("ones_sb", [1, 128], BF16)
    xgtm = p.sb("xgtm", [128, NT * 80], F32)
    xgT = p.sb("xgT_sb", [80, SHARD], BF16)
    m_t = p.sb("m_t", [128, NT], F32)
    t1_t = p.sb("t1_t", [128, NT], F32)
    sq_t = p.sb("sq_t", [128, NT], F32)
    var_t = p.sb("var_t", [128, NT], F32)
    rstd_t = p.sb("rstd_t", [128, NT], F32)
    scr = p.sb("scr", [128, D], BF16)

    pT = [p.ps(f"pT{i}", [128, 128], BF16) for i in range(2)]
    pXG = [p.ps(f"pXG{i}", [128, 80], F32) for i in range(2)]
    pXT = [p.ps(f"pXT{i}", [80, 128], F32) for i in range(2)]

    s_x = p.sem("s_x")
    s_w = p.sem("s_w")
    s_wc = p.sem("s_wc")
    s_sq = p.sem("s_sq")
    s_var = p.sem("s_var")
    s_rstd = p.sem("s_rstd")
    s_ln = p.sem("s_ln")
    s_lnT = p.sem("s_lnT")
    s_lnTe = p.sem("s_lnTe")
    s_xg = p.sem("s_xg")
    s_xge = p.sem("s_xge")
    s_xt = p.sem("s_xt")
    s_xte = p.sem("s_xte")
    s_st = p.sem("s_st")

    if not fused:
        blk = p.block()

    @blk.sync
    def _(sync):
        sync.dma_start(out=wih_sb[:].rearrange("p (kt f) -> p kt f", f=80),
                       in_=wih.rearrange("(kt p) f -> p kt f", p=128)).then_inc(s_w, 16)
        sync.dma_start(out=brow_sb[0][:], in_=brow[0:1, :]).then_inc(s_w, 16)
        sync.dma_start(out=brow_sb[1][:], in_=brow[1:2, :]).then_inc(s_w, 16)
        sync.dma_start(out=id_sb[:], in_=ident).then_inc(s_w, 16)
        sync.dma_start(out=idf_sb[:], in_=identf).then_inc(s_w, 16)
        sync.dma_start(out=ones_sb[:], in_=ones128).then_inc(s_w, 16)
        for c in range(NCH):
            if c > 0:
                sync.wait_ge(s_xge, c * NT)
            sync.dma_start(
                out=xr[:].rearrange("p (tt d) -> p tt d", d=D),
                in_=x_in[c * CHUNK:(c + 1) * CHUNK, :].rearrange("(tt p) d -> p tt d", p=128),
            ).then_inc(s_x, 16)

    @blk.gpsimd
    def _(g):
        for c in range(NCH):
            for tt in range(NT):
                g.wait_ge(s_ln, c * NT + tt + 1)
                g.dma_start(
                    out=ln3_out[c * CHUNK + tt * 128: c * CHUNK + (tt + 1) * 128, :],
                    in_=ln[:, tt * D:(tt + 1) * D],
                ).then_inc(s_st, 16)
        g.wait_ge(s_xte, NCH * NT)
        g.dma_start(out=xgT_out, in_=xgT[:]).then_inc(s_st, 16)
        g.wait_ge(s_st, 16 * (NCH * NT + 1))
        if fused:
            s_cc = io["s_cc"]
            g.collective_compute("AllGather", ALU.bypass,
                                 replica_groups=[list(range(NCORES))],
                                 ins=[io["xgstg"].opt()], outs=[io["xgg"].opt()]).then_inc(s_cc, 1)
            g.collective_compute("AllGather", ALU.bypass,
                                 replica_groups=[list(range(NCORES))],
                                 ins=[io["ln3stg"].opt()], outs=[io["ln3g"].opt()]).then_inc(s_cc, 1)
            g.collective_compute("AllGather", ALU.bypass,
                                 replica_groups=[list(range(NCORES))],
                                 ins=[io["x3"].opt()], outs=[io["x3g"].opt()]).then_inc(s_cc, 1)

    @blk.vector
    def _(vector):
        vector.wait_ge(s_w, 16 * 6)
        vector.tensor_copy(wih_bf[:], wih_sb[:])
        vector.tensor_copy(brow_bf[0][:], brow_sb[0][:])
        vector.tensor_copy(brow_bf[1][:], brow_sb[1][:])
        vector.drain()
        vector.engine_nop().then_inc(s_wc, 1)
        for c in range(NCH):
            vector.wait_ge(s_x, 16 * (c + 1))
            for tt in range(NT):
                vector.reduce_sum(m_t[:, tt:tt + 1], xr[:, tt * D:(tt + 1) * D], AX)
            vector.drain()
            vector.tensor_scalar(m_t[:], m_t[:], 1.0 / D, None, ALU.mult)
            vector.drain()
            vector.tensor_tensor(t1_t[:], m_t[:], m_t[:], ALU.mult)
            vector.wait_ge(s_sq, c * NT + NT)
            vector.tensor_scalar(var_t[:], sq_t[:], 1.0 / D, None, ALU.mult)
            vector.drain()
            vector.tensor_tensor(var_t[:], var_t[:], t1_t[:], ALU.subtract)
            vector.drain()
            vector.tensor_scalar(var_t[:], var_t[:], LN_EPS, None, ALU.add)
            vector.drain()
            vector.engine_nop().then_inc(s_var, 1)
            vector.wait_ge(s_rstd, c + 1)
            if c > 0:
                vector.wait_ge(s_st, 16 * c * NT)
            for tt in range(NT):
                vector.tensor_scalar(
                    ln[:, tt * D:(tt + 1) * D], xr[:, tt * D:(tt + 1) * D],
                    m_t[:, tt:tt + 1], rstd_t[:, tt:tt + 1], ALU.subtract, ALU.mult,
                ).then_inc(s_ln, 1)
            vector.drain()
            for mt in range(NT):
                slot = c * NT + mt
                vector.wait_ge(s_xg, slot + 1)
                par = slot % 2
                vector.tensor_copy(xgtm[:, mt * 80:mt * 80 + 20], pXG[par][:, 0:20])
                vector.tensor_scalar(xgtm[:, mt * 80 + 20:mt * 80 + 80], pXG[par][:, 20:80],
                                     0.5, None, ALU.mult)
                vector.drain()
                vector.engine_nop().then_inc(s_xge, 1)

    @blk.scalar
    def _(scalar):
        for c in range(NCH):
            scalar.wait_ge(s_x, 16 * (c + 1))
            for tt in range(NT):
                scalar.activation(scr[:], xr[:, tt * D:(tt + 1) * D], AF.Square,
                                  accum_out=sq_t[:, tt:tt + 1]).then_inc(s_sq, 1)
                scalar.drain()
            scalar.wait_ge(s_var, c + 1)
            scalar.activation(rstd_t[:], var_t[:], AF.Ln)
            scalar.drain()
            scalar.activation(rstd_t[:], rstd_t[:], AF.Exp, scale=-0.5).then_inc(s_rstd, 1)
            for i in range(NT * ND):
                scalar.wait_ge(s_lnT, c * NT * ND + i + 1)
                tt, dt = divmod(i, ND)
                scalar.copy(lnT[:, dt * CHUNK + tt * 128: dt * CHUNK + (tt + 1) * 128],
                            pT[i % 2][:]).then_inc(s_lnTe, 1)
            for mt in range(NT):
                slot = c * NT + mt
                scalar.wait_ge(s_xt, slot + 1)
                scalar.copy(xgT[:, (c * NT + mt) * 128:(c * NT + mt + 1) * 128],
                            pXT[slot % 2][:]).then_inc(s_xte, 1)

    @blk.tensor
    def _(tensor):
        tensor.wait_ge(s_w, 16 * 6)
        tensor.wait_ge(s_wc, 1)
        for c in range(NCH):
            for i in range(NT * ND):
                tt, dt = divmod(i, ND)
                tensor.wait_ge(s_ln, c * NT + tt + 1)
                g = c * NT * ND + i
                if g >= 2:
                    tensor.wait_ge(s_lnTe, g - 1)
                tensor.transpose(pT[i % 2][:], ln[:, tt * D + dt * 128: tt * D + (dt + 1) * 128],
                                 id_sb[:]).then_inc(s_lnT, 1)
            tensor.wait_ge(s_lnTe, (c + 1) * NT * ND)
            for mt in range(NT):
                slot = c * NT + mt
                if slot >= 2:
                    tensor.wait_ge(s_xge, slot - 1)
                par = slot % 2
                for kt in range(ND):
                    tensor.matmul(
                        pXG[par][:],
                        lnT[:, kt * CHUNK + mt * 128: kt * CHUNK + (mt + 1) * 128],
                        wih_bf[:, kt * 80:(kt + 1) * 80],
                        start=(kt == 0), stop=False)
                tensor.matmul(pXG[par][:], ones_sb[:], brow_bf[0][:], start=False, stop=False)
                tensor.matmul(pXG[par][:], ones_sb[:], brow_bf[1][:],
                              start=False, stop=True).then_inc(s_xg, 1)
            for mt in range(NT):
                slot = c * NT + mt
                tensor.wait_ge(s_xge, slot + 1)
                if slot >= 2:
                    tensor.wait_ge(s_xte, slot - 1)
                tensor.transpose(pXT[slot % 2][:], xgtm[:, mt * 80:(mt + 1) * 80],
                                 idf_sb[:]).then_inc(s_xt, 1)

    if fused:
        p.pop_scope()
        p.pfx = ""
    return p



LST_T = T
NTILE = T // 128


def build_c1(p=None, blk=None, io=None):
    fused = io is not None
    if not fused:
        p = Prog()
    else:
        p.push_scope()
        p.pfx = "m_"
    if not fused:
        xg_mega = p.inp("xg_mega", [80, 4 * (LST_T + 1)], BF16)
    lwxh = p.inp("lwx", [20, 80])
    lwxx = p.inp("lwxx", [80, 80])
    wms = p.inp("wms", [20, 36])
    bms = p.inp("bms", [1, 36])
    invj = p.inp("invj", [32, 128])
    tp1sT = p.inp("tp1sT", [128, 32])
    identf = io["identf"] if fused else p.inp("identf", [128, 128], F32)
    q1_out = io["q1s"] if fused else p.out("q1", [4, LST_T])
    q3_out = io["q3s"] if fused else p.out("q3", [4, LST_T])
    if fused:
        sel_in = p.inp("sel", [32, 16])
        q16s_out = io["q16s"]
    if fused:
        col = lambda t, b: b * (LST_T + 1) + t
    else:
        col = lambda t, b: 4 * t + b
    RHS_COLS = 4 * (LST_T + 1)

    rhs = p.sb("rhs", [20, 4 * (LST_T + 1)], BF16)
    rhx = p.sb("rhx", [80, 4 * (LST_T + 1)], BF16)
    lwx_sb = p.sb("lwx_sb", [20, 80], F32)
    lwxx_sb = p.sb("lwxx_sb", [80, 80], F32)
    lwbh = p.sb("lwbh", [20, 80], BF16)
    lwbx = p.sb("lwbx", [80, 80], BF16)
    wms_sb = p.sb("wms_sb", [20, 36], F32)
    wmsb = p.sb("wmsb", [20, 36], BF16)
    bms_sb = p.sb("bms_sb", [1, 36], F32)
    bmsb = p.sb("bmsb", [1, 36], BF16)
    invj_sb = p.sb("invj_sb", [32, 128], F32)
    tp1_sb = p.sb("tp1_sb", [128, 32], F32)
    idf_sb = p.sb("idf_sb", [128, 128], F32)
    ones512 = p.sb("ones512", [1, 512], BF16)
    tau0 = p.sb("tau0", [20, 16], F32)
    tau1 = p.sb("tau1", [20, 16], F32)
    tau2 = p.sb("tau2", [20, 16], F32)
    tau3 = p.sb("tau3", [20, 16], F32)
    u_t = p.sb("u_t", [20, 4], F32)
    v_t = p.sb("v_t", [20, 4], F32)
    s2 = p.sb("s2", [20, 4], F32)
    th = p.sb("th", [20, 4], F32)
    pj_sb = p.sb("pj_sb", [36, 512], F32)
    grids = [p.sb(f"grid{i}", [128, 32 * 36], F32) for i in range(4)]
    t2g = p.sb("t2g", [128, 32], F32)
    bbg = p.sb("bbg", [128, 32], F32)
    aS = p.sb("aS", [32, 128], F32)
    bbS = p.sb("bbS", [32, 128], F32)
    sigS = p.sb("sigS", [32, 128], F32)
    muL = p.sb("muL", [32, 128], F32)
    PS = p.sb("PS", [32, 128], F32)
    ABt0 = p.sb("ABt0", [1, 32], F32)
    ABt1 = p.sb("ABt1", [1, 32], F32)
    mue = p.sb("mue", [1, 32], F32)
    muin_r = p.sb("muin_r", [1, 32], F32)
    muin = p.sb("muin", [32, 1], F32)
    kS = p.sb("kS", [32, 128], F32)
    q1S = p.sb("q1S", [32, 128], F32)
    q3S = p.sb("q3S", [32, 128], F32)
    if fused:
        sel_sb = p.sb("sel_sb", [32, 16], F32)
        q16sb = p.sb("q16sb", [128, 32], F32)
        pQ = p.ps("pQ", [128, 8], F32)
        s_q16 = p.sem("s_q16")
        wstg = [p.sb(f"wstg{i}", [128, D], F32) for i in range(2)]
        wstgb = [p.sb(f"wstgb{i}", [128, D], BF16) for i in range(2)]
        s_ws = [p.sem(f"s_ws{i}") for i in range(2)]
        s_wsc = p.sem("s_wsc")
        s_wst = [p.sem("s_wst0"), p.sem("s_wst1")]
        wstack, wscr, s_wpre = io["wstack"], io["wscr"], io["s_wpre"]

    ps0 = p.ps("ps0", [20, 16], F32)
    ps1 = p.ps("ps1", [20, 16], F32)
    pPJ = p.ps("pPJ", [36, 512], F32)
    pTR = p.ps("pTR", [128, 128], F32)

    s_ld = p.sem("s_ld")
    s_ldi = p.sem("s_ldi")
    s_wp = p.sem("s_wp")
    s_T = p.sem("s_T")
    s_S1 = p.sem("s_S1")
    s_V1 = p.sem("s_V1")
    s_S2 = p.sem("s_S2")
    s_V2 = p.sem("s_V2")
    s_pj = p.sem("s_pj")
    s_pje = p.sem("s_pje")
    s_tr = p.sem("s_tr")
    s_tre = p.sem("s_tre")
    s_gv = p.sem("s_gv")
    s_st = p.sem("s_st")

    taus = [tau0, tau1, tau2, tau3]
    pss = [ps0, ps1]
    NCHP = LST_T // 512

    ins_T_mm, ins_T_tr, ins_V_relu, ins_S_sig, ins_S_cp = {}, {}, {}, {}, {}
    for ch_ in range(7):
        for b_ in range(4):
            ci_ = ch_ * 4 + b_
            tb_ = (ch_ + 1) * 512 + 4 * b_
            ins_T_mm[tb_] = (b_, ch_, ci_)
            ins_V_relu[tb_ + 2] = ci_
            ins_S_sig[tb_ + 2] = ci_
            ins_S_cp.setdefault(tb_ + 3, []).extend([(b_, ch_, ci_, 0), (b_, ch_, ci_, 1)])
            ins_S_cp.setdefault(tb_ + 4, []).extend([(b_, ch_, ci_, 2), (b_, ch_, ci_, 3)])
            ins_T_tr.setdefault(tb_ + 3, []).extend([(b_, ch_, ci_, 0), (b_, ch_, ci_, 1)])
            ins_T_tr.setdefault(tb_ + 4, []).extend([(b_, ch_, ci_, 2), (b_, ch_, ci_, 3)])

    def proj_rv(b, ch):
        if fused:
            return rhs[0:20, b * (LST_T + 1) + ch * 512 + 1: b * (LST_T + 1) + (ch + 1) * 512 + 1]
        return rhs[0:20, :].rearrange("p (s c) -> p s c", c=4)[:, ch * 512 + 1:(ch + 1) * 512 + 1, b]

    if not fused:
        blk = p.block()

    @blk.sync
    def _(sync):
        sync.dma_start(out=lwx_sb[:], in_=lwxh).then_inc(s_ld, 16)
        sync.dma_start(out=lwxx_sb[:], in_=lwxx).then_inc(s_ld, 16)
        sync.dma_start(out=wms_sb[:], in_=wms).then_inc(s_ld, 16)
        sync.dma_start(out=bms_sb[:], in_=bms).then_inc(s_ld, 16)
        sync.dma_start(out=invj_sb[:], in_=invj).then_inc(s_ld, 16)
        sync.dma_start(out=tp1_sb[:], in_=tp1sT).then_inc(s_ld, 16)
        sync.dma_start(out=idf_sb[:], in_=identf).then_inc(s_ld, 16)
        if fused:
            sync.dma_start(out=sel_sb[:], in_=sel_in).then_inc(s_ld, 16)
            sync.wait_ge(io["s_cc"], 1)
            for k in range(NCORES):
                b, t0 = k // 2, (k % 2) * SHARD
                sync.dma_start(out=rhx[:, col(t0, b):col(t0, b) + SHARD],
                               in_=io["xgg"][k * 80:(k + 1) * 80, :]).then_inc(s_ldi, 16)
            for i in range(64):
                if i >= 2:
                    sync.wait_ge(s_wsc, i - 1)
                sync.dma_start(out=wstg[i % 2][:],
                               in_=wstack[i * 128:(i + 1) * 128, :]).then_inc(s_ws[i % 2], 16)
        else:
            sync.dma_start(out=rhx[:, 0:4 * LST_T], in_=xg_mega[:, 0:4 * LST_T]).then_inc(s_ld, 16)

    @blk.vector
    def _(vector):
        vector.wait_ge(s_ld, 16 * 8)
        vector.memset(rhs[0:20, :], 0.0)
        vector.memset(s2[:], 0.0)
        vector.memset(pj_sb[:], 0.0)
        vector.memset(ones512[:], 1.0)
        vector.tensor_scalar(lwx_sb[0:20, 0:20], lwx_sb[0:20, 0:20], 0.5, None, ALU.mult)
        vector.tensor_scalar(lwx_sb[0:20, 20:80], lwx_sb[0:20, 20:80], 0.25, None, ALU.mult)
        vector.drain()
        vector.tensor_copy(lwbh[:], lwx_sb[:])
        vector.tensor_copy(lwbx[:], lwxx_sb[:])
        vector.tensor_scalar(wms_sb[:], wms_sb[:], 0.5, None, ALU.mult)
        vector.drain()
        vector.tensor_copy(wmsb[:], wms_sb[:])
        vector.tensor_copy(bmsb[:], bms_sb[:])
        vector.drain()
        vector.engine_nop().then_inc(s_wp, 1)
        vector.engine_nop().then_inc(s_V2, 1)
        for t in range(LST_T):
            tau = taus[t % 4]
            vector.wait_ge(s_S1, t + 1)
            vector.scalar_tensor_tensor(u_t[:], tau[:, 4:8], 1.0, tau[:, 0:4], ALU.add, ALU.mult)
            vector.scalar_tensor_tensor(v_t[:], tau[:, 8:12], 1.0, s2[:], ALU.add, ALU.mult)
            vector.drain()
            vector.scalar_tensor_tensor(s2[:], v_t[:], 0.5, u_t[:], ALU.mult, ALU.add).then_inc(s_V1, 1)
            vector.wait_ge(s_S2, t + 1)
            if fused:
                _hd = rhs[0:20, :].rearrange("p (b tt) -> p b tt", b=4)[:, :, t + 1]
            else:
                _hd = rhs[0:20, 4 * (t + 1):4 * (t + 2)]
            vector.scalar_tensor_tensor(_hd,
                                        tau[:, 12:16], 1.0, th[:], ALU.add, ALU.mult).then_inc(s_V2, 1)
            vector.drain()
            if fused and t >= 128 and (t - 128) % 32 == 0 and (t - 128) // 32 < 64:
                i = (t - 128) // 32
                par = i % 2
                vector.wait_ge(s_ws[par], 16 * (i // 2 + 1))
                if i >= 2:
                    vector.wait_ge(s_wst[par], 16 * (i // 2))
                vector.tensor_copy(wstgb[par][:], wstg[par][:]).then_inc(s_wsc, 1)
            if t in ins_V_relu:
                ci = ins_V_relu[t]
                vector.wait_ge(s_pj, ci + 1)
                vector.tensor_relu(pj_sb[0:3, :], pPJ[0:3, :])
                vector.drain()
                vector.engine_nop().then_inc(s_pje, 1)
        for b in range(4):
            ci = 28 + b
            vector.wait_ge(s_pj, ci + 1)
            vector.tensor_relu(pj_sb[0:3, :], pPJ[0:3, :])
            vector.drain()
            vector.engine_nop().then_inc(s_pje, 1)
        for b in range(4):
            vector.wait_ge(s_tre, 116 + 4 * b)
            mu1g = grids[b][:].rearrange("p (c v) -> p c v", v=36)[:, :, 1]
            mu2g = grids[b][:].rearrange("p (c v) -> p c v", v=36)[:, :, 2]
            vector.tensor_tensor(t2g[:], mu2g, tp1_sb[:], ALU.mult)
            vector.drain()
            vector.scalar_tensor_tensor(bbg[:], mu1g, 1.0 / LST_T, t2g[:], ALU.mult, ALU.add)
            vector.drain()
            vector.engine_nop().then_inc(s_gv, 1)
            vector.wait_ge(s_tre, 128 + 6 * b + 3)
            vector.tensor_tensor_scan(muL[:], aS[:], bbS[:], 0.0, ALU.mult, ALU.add)
            vector.drain()
            vector.tensor_tensor_scan(PS[:], aS[:], aS[:], 1.0, ALU.mult, ALU.bypass)
            vector.drain()
            vector.engine_nop().then_inc(s_gv, 1)
            vector.wait_ge(s_tre, 128 + 6 * b + 5)
            vector.tensor_tensor_scan(mue[:], ABt0[:], ABt1[:], 0.0, ALU.mult, ALU.add)
            vector.drain()
            vector.memset(muin_r[:], 0.0)
            vector.drain()
            vector.tensor_copy(muin_r[0:1, 1:32], mue[0:1, 0:31])
            vector.drain()
            vector.engine_nop().then_inc(s_gv, 1)
            vector.wait_ge(s_tre, 128 + 6 * b + 6)
            vector.scalar_tensor_tensor(muL[:], PS[:], muin[:, 0:1], muL[:], ALU.mult, ALU.add)
            vector.drain()
            vector.reciprocal(kS[:], sigS[:])
            vector.drain()
            vector.tensor_scalar(kS[:], kS[:], 0.7071067811865476, None, ALU.mult)
            vector.drain()
            vector.wait_ge(s_st, 32 * b)
            if fused and b > 0:
                vector.wait_ge(s_q16, 2 * b)
            vector.tensor_tensor(q1S[:], kS[:], invj_sb[:], ALU.mult)
            vector.tensor_tensor(q3S[:], muL[:], kS[:], ALU.mult)
            vector.drain()
            vector.engine_nop().then_inc(s_gv, 1)
        if fused:
            vector.wait_ge(s_wst[0], 16 * 32)
            vector.wait_ge(s_wst[1], 16 * 32)
            vector.engine_nop().then_inc(s_wpre, 1)

    @blk.scalar
    def _(scalar):
        for t in range(LST_T):
            tau = taus[t % 4]
            ps = pss[t % 2]
            scalar.wait_ge(s_T, t + 1)
            if t >= 4:
                scalar.wait_ge(s_V2, t - 2)
            scalar.activation(tau[:], ps[:], AF.Tanh).then_inc(s_S1, 1)
            scalar.wait_ge(s_V1, t + 1)
            scalar.activation(th[:], s2[:], AF.Tanh, scale=0.5).then_inc(s_S2, 1)
            if t in ins_S_sig:
                ci = ins_S_sig[t]
                scalar.wait_ge(s_pj, ci + 1)
                scalar.activation(pj_sb[32:33, :], pPJ[32:33, :], AF.Sigmoid).then_inc(s_pje, 1)
            if t in ins_S_cp:
                for (b_, ch_, ci, sub) in ins_S_cp[t]:
                    n = ci * 4 + sub
                    scalar.wait_ge(s_tr, n + 1)
                    i_g = ch_ * 4 + sub
                    scalar.copy(grids[b_][:, i_g * 36:i_g * 36 + 36],
                                pTR[:, 0:36]).then_inc(s_tre, 1)
        for b in range(4):
            ci = 28 + b
            scalar.wait_ge(s_pj, ci + 1)
            scalar.activation(pj_sb[32:33, :], pPJ[32:33, :], AF.Sigmoid).then_inc(s_pje, 1)
            for sub in range(4):
                n = ci * 4 + sub
                scalar.wait_ge(s_tr, n + 1)
                i_g = 7 * 4 + sub
                scalar.copy(grids[b][:, i_g * 36:i_g * 36 + 36],
                            pTR[:, 0:36]).then_inc(s_tre, 1)
        for b in range(4):
            for i, dst in enumerate((aS, bbS, sigS)):
                scalar.wait_ge(s_tr, 128 + 6 * b + i + 1)
                scalar.copy(dst[:], pTR[0:32, :]).then_inc(s_tre, 1)
            scalar.wait_ge(s_tr, 128 + 6 * b + 4)
            scalar.copy(ABt0[:], pTR[0:1, 0:32]).then_inc(s_tre, 1)
            scalar.wait_ge(s_tr, 128 + 6 * b + 5)
            scalar.copy(ABt1[:], pTR[0:1, 0:32]).then_inc(s_tre, 1)
            scalar.wait_ge(s_tr, 128 + 6 * b + 6)
            scalar.copy(muin[:], pTR[0:32, 0:1]).then_inc(s_tre, 1)
            if fused:
                scalar.wait_ge(s_q16, 2 * b + 1)
                scalar.copy(q16sb[:, b * 4:(b + 1) * 4], pQ[:, 0:4])
                scalar.copy(q16sb[:, 16 + b * 4:16 + (b + 1) * 4], pQ[:, 4:8]).then_inc(s_q16, 1)

    @blk.tensor
    def _(tensor):
        tensor.wait_ge(s_ld, 16 * 8)
        if fused:
            tensor.wait_ge(s_ldi, 16 * NCORES)
        tensor.wait_ge(s_wp, 1)
        tensor.wait_ge(s_V2, 1)

        def rvh(t):
            if fused:
                return rhs[0:20, :].rearrange("p (b tt) -> p b tt", b=4)[:, :, t]
            return rhs[0:20, 4 * t:4 * (t + 1)]

        def rvx(t):
            if fused:
                return rhx[:, :].rearrange("p (b tt) -> p b tt", b=4)[:, :, t]
            return rhx[:, 4 * t:4 * (t + 1)]

        for G in range(4):
            tensor.matmul(pss[0][:, G * 4:(G + 1) * 4], lwbx[:, G * 20:(G + 1) * 20], rvx(0),
                          start=(G == 0), stop=False, skip_group_check=True)
        for t in range(LST_T):
            ps = pss[t % 2]
            tensor.wait_ge(s_V2, t + 1)
            for G in range(4):
                mm = tensor.matmul(ps[:, G * 4:(G + 1) * 4], lwbh[:, G * 20:(G + 1) * 20], rvh(t),
                                   start=False, stop=(G == 3), skip_group_check=True)
                if G == 3:
                    mm.then_inc(s_T, 1)
            if t + 1 < LST_T:
                if t >= 1:
                    tensor.wait_ge(s_S1, t)
                psn = pss[(t + 1) % 2]
                for G in range(4):
                    tensor.matmul(psn[:, G * 4:(G + 1) * 4], lwbx[:, G * 20:(G + 1) * 20], rvx(t + 1),
                                  start=(G == 0), stop=False, skip_group_check=True)
            if t in ins_T_mm:
                b_, ch_, ci = ins_T_mm[t]
                if ci >= 1:
                    tensor.wait_ge(s_pje, 2 * ci)
                tensor.matmul(pPJ[:], wmsb[:], proj_rv(b_, ch_), start=True, stop=False)
                tensor.matmul(pPJ[:], bmsb[:], ones512[:], start=False, stop=True).then_inc(s_pj, 1)
            if t in ins_T_tr:
                for (b_, ch_, ci, sub) in ins_T_tr[t]:
                    n = ci * 4 + sub
                    tensor.wait_ge(s_pje, 2 * ci + 2)
                    if n >= 1:
                        tensor.wait_ge(s_tre, n)
                    tensor.transpose(pTR[:, 0:36], pj_sb[:, sub * 128:(sub + 1) * 128],
                                     idf_sb[0:36, 0:36]).then_inc(s_tr, 1)
        tensor.wait_ge(s_V2, LST_T + 1)
        for b in range(4):
            ci = 28 + b
            tensor.wait_ge(s_pje, 2 * ci)
            tensor.matmul(pPJ[:], wmsb[:], proj_rv(b, 7), start=True, stop=False)
            tensor.matmul(pPJ[:], bmsb[:], ones512[:], start=False, stop=True).then_inc(s_pj, 1)
            tensor.wait_ge(s_pje, 2 * ci + 2)
            for sub in range(4):
                n = ci * 4 + sub
                tensor.wait_ge(s_tre, n)
                tensor.transpose(pTR[:, 0:36], pj_sb[:, sub * 128:(sub + 1) * 128],
                                 idf_sb[0:36, 0:36]).then_inc(s_tr, 1)
        for b in range(4):
            tensor.wait_ge(s_gv, 4 * b + 1)
            ag = grids[b][:].rearrange("p (c v) -> p c v", v=36)[:, :, 0]
            sgg = grids[b][:].rearrange("p (c v) -> p c v", v=36)[:, :, 32]
            for i, src in enumerate((ag, bbg[:], sgg)):
                n = 128 + 6 * b + i
                tensor.wait_ge(s_tre, n)
                tensor.transpose(pTR[0:32, :], src, idf_sb[:]).then_inc(s_tr, 1)
            tensor.wait_ge(s_gv, 4 * b + 2)
            tensor.wait_ge(s_tre, 128 + 6 * b + 3)
            tensor.transpose(pTR[0:1, 0:32], PS[:, 127:128], idf_sb[0:32, 0:32]).then_inc(s_tr, 1)
            tensor.wait_ge(s_tre, 128 + 6 * b + 4)
            tensor.transpose(pTR[0:1, 0:32], muL[:, 127:128], idf_sb[0:32, 0:32]).then_inc(s_tr, 1)
            tensor.wait_ge(s_gv, 4 * b + 3)
            tensor.wait_ge(s_tre, 128 + 6 * b + 5)
            tensor.transpose(pTR[0:32, 0:1], muin_r[:], idf_sb[0:1, 0:1]).then_inc(s_tr, 1)
            if fused:
                tensor.wait_ge(s_gv, 4 * b + 4)
                if b > 0:
                    tensor.wait_ge(s_q16, 2 * b)
                tensor.matmul(pQ[:, 0:4], q1S[:], sel_sb[:, b * 4:(b + 1) * 4],
                              start=True, stop=True, skip_group_check=True)
                tensor.matmul(pQ[:, 4:8], q3S[:], sel_sb[:, b * 4:(b + 1) * 4],
                              start=True, stop=True, skip_group_check=True).then_inc(s_q16, 1)

    @blk.gpsimd
    def _(g):
        if fused:
            for i in range(64):
                g.wait_ge(s_wsc, i + 1)
                g.dma_start(out=wscr[i * 128:(i + 1) * 128, :],
                            in_=wstgb[i % 2][:]).then_inc(s_wst[i % 2], 16)
        for b in range(4):
            g.wait_ge(s_gv, 4 * b + 4)
            g.dma_start(out=q1_out[b:b + 1, :].rearrange("o (pp f) -> (o pp) f", f=128),
                        in_=q1S[:]).then_inc(s_st, 16)
            g.dma_start(out=q3_out[b:b + 1, :].rearrange("o (pp f) -> (o pp) f", f=128),
                        in_=q3S[:]).then_inc(s_st, 16)
        if fused:
            g.wait_ge(s_q16, 8)
            g.dma_start(out=io["q16s"], in_=q16sb[:]).then_inc(s_st, 16)
            g.wait_ge(s_st, 16 * 9)
        else:
            g.wait_ge(s_st, 16 * 8)

    if fused:
        p.pop_scope()
        p.pfx = ""
    return p


PERM_GIFO = np.concatenate([np.arange(40, 60), np.arange(0, 20), np.arange(20, 40), np.arange(60, 80)])


def host_prep_c0(inputs):
    wih = _f32(inputs["L1_attn_W_ih"])[PERM_GIFO]
    return {
        "wihT": _f32(wih.T),
        "brow": _f32(np.stack([inputs["L1_attn_b_ih"][PERM_GIFO], inputs["L1_attn_b_hh"][PERM_GIFO]])),
        "ident": np.eye(128, dtype=BF),
        "identf": np.eye(128, dtype=np.float32),
        "ones128": np.ones((1, 128), dtype=BF),
    }


def run_c0(inputs, x3):
    p = build_c0()
    com = host_prep_c0(inputs)
    in_maps = []
    for k in range(NCORES):
        m = dict(com)
        m["x3"] = np.ascontiguousarray(x3[k * SHARD:(k + 1) * SHARD])
        in_maps.append(m)
    res = run_bass_kernel_spmd(p.nc, in_maps, core_ids=list(range(NCORES)))
    ln3 = np.concatenate([np.asarray(r["ln3"]) for r in res.results], axis=0)
    xgs = [np.asarray(r["xgT"]) for r in res.results]
    return ln3, xgs


def host_prep_c1(inputs, xgs):
    xg_mega = np.zeros((80, 4 * (T + 1)), dtype=BF)
    for k in range(NCORES):
        b, t0 = k // 2, (k % 2) * SHARD
        cols = 4 * (t0 + np.arange(SHARD)) + b
        xg_mega[:, cols] = xgs[k]
    W_hh = _f32(inputs["L1_attn_W_hh"])
    lwx = np.zeros((20, 80), dtype=np.float32)
    orig_off = {"i": 0, "f": 20, "g": 40, "o": 60}
    col_base = {"g": 0, "i": 20, "f": 40, "o": 60}
    for gname in ("g", "i", "f", "o"):
        for j in range(20):
            lwx[0:20, col_base[gname] + j] = W_hh[orig_off[gname] + j, :]
    lwxx = np.eye(80, dtype=np.float32)
    wms = np.zeros((20, 36), dtype=np.float32)
    wms[:, 0:3] = _f32(inputs["L1_attn_W_mu"]).T
    wms[:, 32] = _f32(inputs["L1_attn_W_sig"])[0]
    bms = np.zeros((1, 36), dtype=np.float32)
    bms[0, 0:3] = _f32(inputs["L1_attn_b_mu"])
    bms[0, 32] = float(inputs["L1_attn_b_sig"][0])
    pj = np.arange(32)[:, None] * 128 + np.arange(128)[None, :]
    invj = (1.0 / (pj + 1.0)).astype(np.float32)
    tc = np.arange(32)[None, :] * 128 + np.arange(128)[:, None]
    tp1sT = ((tc + 1.0) / T).astype(np.float32)
    return {
        "xg_mega": xg_mega, "lwx": lwx, "lwxx": lwxx, "wms": wms, "bms": bms,
        "invj": invj, "tp1sT": tp1sT, "identf": np.eye(128, dtype=np.float32),
    }


def run_c1(inputs, xgs):
    p = build_c1()
    com = host_prep_c1(inputs, xgs)
    in_maps = [dict(com) for _ in range(NCORES)]
    res = run_bass_kernel_spmd(p.nc, in_maps, core_ids=list(range(NCORES)))
    return np.asarray(res.results[0]["q1"]), np.asarray(res.results[0]["q3"])



NSB_PAT = (1, 2, 3, 4)
SB_OF_ACC = [0, 1, 1, 2, 2, 2, 3, 3, 3, 3]
NSLOT_B = 80
NSLOT = 4 * NSLOT_B


def build_c2(p=None, blk=None, io=None):
    fused = io is not None
    if not fused:
        p = Prog()
    else:
        p.push_scope()
        p.pfx = "g_"
    masks = p.inp("masks", [NSLOT, 128, 128], BF16)
    itc_in = p.inp("itc", [128, NSLOT])
    iota_in = p.inp("iota", [128, 128])
    if fused:
        xtab_in = p.inp("xtab", [1, 16], I32)
        ln3g, x3g = io["ln3g"], io["x3g"]
        q16s_d = io["q16s"]
        x4row_out = io["x4s"]
        ident = io["ident"]
        identf = io["identf"]
    else:
        ln3s = p.inp("ln3s", [NSLOT, 128, D], BF16)
        q1c_in = p.inp("q1c", [128, NSLOT])
        q3c_in = p.inp("q3c", [128, NSLOT])
        x3row_in = p.inp("x3row", [16, 128, D])
        ident = p.inp("ident", [128, 128], BF16)
        x4row_out = p.out("x4row", [16, 128, D])

    NLB = 6
    ln3b = [p.sb(f"ln3b{i}", [128, D], BF16) for i in range(NLB)]
    if fused:
        xtab = p.sb("xtab_sb", [1, 16], I32)
        q16 = p.sb("q16", [128, 32], F32)
        s_qd = p.sem("s_qd")

    mkb = [p.sb(f"mkb{i}", [128, 256], BF16) for i in range(2)]
    q1c = p.sb("q1c_sb", [128, NSLOT], F32)
    q3c = p.sb("q3c_sb", [128, NSLOT], F32)
    itc = p.sb("itc_sb", [128, NSLOT], F32)
    iota2 = p.sb("iota2_sb", [128, 256], F32)
    id_sb = p.sb("id_sb", [128, 128], BF16)
    q2col = p.sb("q2col", [128, 2], F32)
    argT = p.sb("argT", [128, 256], F32)
    sqV = [p.sb(f"sqV{i}", [128, 256], F32) for i in range(2)]
    wraw = [p.sb(f"wraw{i}", [128, 256], BF16) for i in range(2)]
    wm = [p.sb(f"wm{i}", [128, 256], BF16) for i in range(2)]
    wmT = [p.sb(f"wmT{i}", [128, 128], BF16) for i in range(2)]
    wsq_scr = p.sb("wsq_scr", [128, 128], BF16)
    ngrid = p.sb("ngrid", [128, NSLOT], F32)
    accb = p.sb("accb", [128, D], F32)
    x3b = [p.sb(f"x3b{i}", [128, D], F32) for i in range(2)]
    x4b = p.sb("x4b", [128, D], F32)
    n2 = p.sb("n2", [128, 1], F32)
    rn = p.sb("rn", [128, 1], F32)

    pW = [p.ps(f"pW{i}", [128, 128], BF16) for i in range(2)]
    pCtx = [p.ps(f"pCtx{i}", [128, D], F32) for i in range(2)]

    s_ld = p.sem("s_ld")
    s_lt = [p.sem(f"s_lt{i}") for i in range(NLB)]
    s_mk = [p.sem(f"s_mk{i}") for i in range(2)]
    s_x3 = [p.sem(f"s_x3{i}") for i in range(2)]
    s_q2 = p.sem("s_q2")
    s_we = p.sem("s_we")
    s_wm = p.sem("s_wm")
    s_tw = p.sem("s_tw")
    s_twe = p.sem("s_twe")
    s_mm = p.sem("s_mm")
    s_acc = p.sem("s_acc")
    s_n2 = p.sem("s_n2")
    s_rn = p.sem("s_rn")
    s_fin = p.sem("s_fin")
    s_st = p.sem("s_st")

    slots = []
    for b in range(4):
        for sb_b in range(10):
            acc = SB_OF_ACC[sb_b]
            first_sb = (sb_b == 0 or SB_OF_ACC[sb_b - 1] != acc)
            last_sb = (sb_b == 9 or SB_OF_ACC[sb_b + 1] != acc)
            sb_local = sb_b - SB_OF_ACC.index(acc)
            for k in range(8):
                slots.append(dict(b=b, sb=b * 10 + sb_b, k=k, acc=acc,
                                  first=first_sb, last=last_sb,
                                  gidx=b * 4 + acc, it=sb_local * 8 + k))
    NGRP = 16

    def _ln3row(b, sl):
        trow = sl["it"] * 128
        kown = b * 2 + (0 if trow < SHARD else 1)
        return kown * SHARD + (trow - (kown % 2) * SHARD)

    if not fused:
        blk = p.block()

    @blk.sync
    def _(sync):
        if fused:
            sync.dma_start(out=itc[:], in_=itc_in).then_inc(s_ld, 16)
            sync.dma_start(out=iota2[:, 0:128], in_=iota_in).then_inc(s_ld, 16)
            sync.dma_start(out=iota2[:, 128:256], in_=iota_in).then_inc(s_ld, 16)
            sync.dma_start(out=id_sb[:], in_=ident).then_inc(s_ld, 16)
            sync.dma_start(out=q16[:], in_=q16s_d).then_inc(s_qd, 16)
            sync.wait_ge(io["s_cc"], 2)
            for s in range(0, NSLOT, 2):
                par = s % NLB
                if s >= NLB:
                    sync.wait_ge(s_mm, s - NLB + 1)
                sl = slots[s]
                row = _ln3row(sl["b"], sl)
                sync.dma_start(out=ln3b[par][:], in_=ln3g[row:row + 128, :]).then_inc(s_lt[par], 16)
        else:
            sync.dma_start(out=q1c[:], in_=q1c_in).then_inc(s_ld, 16)
            sync.dma_start(out=q3c[:], in_=q3c_in).then_inc(s_ld, 16)
            sync.dma_start(out=itc[:], in_=itc_in).then_inc(s_ld, 16)
            sync.dma_start(out=iota2[:, 0:128], in_=iota_in).then_inc(s_ld, 16)
            sync.dma_start(out=iota2[:, 128:256], in_=iota_in).then_inc(s_ld, 16)
            sync.dma_start(out=id_sb[:], in_=ident).then_inc(s_ld, 16)
            for s in range(0, NSLOT, 2):
                par = s % NLB
                if s >= NLB:
                    sync.wait_ge(s_mm, s - NLB + 1)
                sync.dma_start(out=ln3b[par][:], in_=ln3s[s]).then_inc(s_lt[par], 16)

    @blk.gpsimd
    def _(g):
        with contextlib.ExitStack() as ges:
            if fused:
                s_xt2 = io["s_xt2"]
                g.dma_start(out=xtab[:], in_=xtab_in).then_inc(s_xt2, 16)
                g.wait_ge(s_xt2, 16)
                g.wait_ge(io["s_cc"], 3)
                xofs = io["xofs_reg"]
            for s in range(NSLOT):
                sl = slots[s]
                if s % 2 == 0:
                    pp = s // 2
                    if pp >= 2:
                        g.wait_ge(s_wm, pp - 1)
                    g.dma_start(out=mkb[pp % 2][:].rearrange("p (t f) -> p t f", f=128),
                                in_=masks[s:s + 2].rearrange("t p f -> p t f")).then_inc(s_mk[pp % 2], 16)
                if s % 2 == 1:
                    lpar = s % NLB
                    if s >= NLB:
                        g.wait_ge(s_mm, s - NLB + 1)
                    if fused:
                        lrow = _ln3row(sl["b"], sl)
                        g.dma_start(out=ln3b[lpar][:],
                                    in_=io["ln3g"][lrow:lrow + 128, :]).then_inc(s_lt[lpar], 16)
                    else:
                        g.dma_start(out=ln3b[lpar][:], in_=ln3s[s]).then_inc(s_lt[lpar], 16)
                if sl["last"] and sl["k"] == 0:
                    gp = sl["gidx"]
                    xpar = gp % 2
                    if gp >= 2:
                        g.wait_ge(s_fin, gp - 1)
                    if fused:
                        g.reg_load(xofs, xtab[0:1, gp:gp + 1])
                        g.dma_start(out=x3b[xpar][:],
                                    in_=bass.AP(x3g.tensor, xofs, [[D, 128], [1, D]])).then_inc(s_x3[xpar], 16)
                    else:
                        g.dma_start(out=x3b[xpar][:], in_=x3row_in[gp]).then_inc(s_x3[xpar], 16)
                if sl["last"] and sl["k"] == 7:
                    gp = sl["gidx"]
                    g.wait_ge(s_fin, gp + 1)
                    g.dma_start(out=x4row_out[gp], in_=x4b[:]).then_inc(s_st, 16)
        g.wait_ge(s_st, 16 * NGRP)

    @blk.vector
    def _(vector):
        NPAIR = NSLOT // 2
        if fused:
            vector.wait_ge(s_ld, 16 * 4)
            vector.wait_ge(s_qd, 16)
        else:
            vector.wait_ge(s_ld, 16 * 6)

        def v_produce(pp):
            s0 = 2 * pp
            qcol = q16[:, slots[s0]["gidx"]:slots[s0]["gidx"] + 1] if fused else q1c[:, s0:s0 + 1]
            q3v = (q16[:, 16 + slots[s0]["gidx"]:17 + slots[s0]["gidx"]] if fused
                   else q3c[:, s0:s0 + 1])
            vector.scalar_tensor_tensor(
                q2col[:, 0:2].rearrange("p (n t) -> p n t", n=1),
                itc[:, s0:s0 + 2].rearrange("p (n t) -> p n t", n=1),
                qcol,
                q3v.rearrange("p (n o) -> p n o", o=1).broadcast_to([128, 1, 2]),
                ALU.mult, ALU.subtract)
            vector.drain()
            vector.scalar_tensor_tensor(
                argT[:].rearrange("p (t f) -> p t f", f=128),
                iota2[:].rearrange("p (t f) -> p t f", f=128),
                qcol,
                q2col[:, 0:2].rearrange("p (t o) -> p t o", o=1).broadcast_to([128, 2, 128]),
                ALU.mult, ALU.add)
            vector.drain()
            if pp >= 2:
                vector.wait_ge(s_we, pp - 1)
            vector.tensor_tensor(sqV[pp % 2][:], argT[:], argT[:], ALU.mult).then_inc(s_q2, 1)
            vector.drain()

        v_produce(0)
        for pp in range(NPAIR):
            s0, s = 2 * pp, 2 * pp + 1
            sl = slots[s]
            if pp + 1 < NPAIR:
                v_produce(pp + 1)
            vector.wait_ge(s_we, pp + 1)
            vector.wait_ge(s_mk[pp % 2], 16 * (pp // 2 + 1))
            if pp >= 2:
                vector.wait_ge(s_tw, 2 * pp - 2)
            vector.tensor_tensor(wm[pp % 2][:], wraw[pp % 2][:], mkb[pp % 2][:], ALU.mult)
            vector.drain()
            vector.scalar_tensor_tensor(wsq_scr[:], wm[pp % 2][:, 0:128], 1.0, wm[pp % 2][:, 0:128],
                                        ALU.mult, ALU.mult,
                                        accum_out=ngrid[:, s0:s0 + 1])
            vector.drain()
            vector.scalar_tensor_tensor(wsq_scr[:], wm[pp % 2][:, 128:256], 1.0, wm[pp % 2][:, 128:256],
                                        ALU.mult, ALU.mult,
                                        accum_out=ngrid[:, s:s + 1]).then_inc(s_wm, 1)
            vector.drain()
            if sl["k"] == 7:
                sbi = sl["sb"]
                vector.wait_ge(s_mm, s + 1)
                for nh in range(2):
                    dst = accb[:, nh * 512:(nh + 1) * 512]
                    if sl["first"]:
                        vector.tensor_copy(dst, pCtx[sbi % 2][:, nh * 512:(nh + 1) * 512])
                    else:
                        vector.tensor_tensor(dst, dst, pCtx[sbi % 2][:, nh * 512:(nh + 1) * 512], ALU.add)
                vector.drain()
                vector.engine_nop().then_inc(s_acc, 2)
                if sl["last"]:
                    gp = sl["gidx"]
                    glen = 8 * sum(1 for i, a in enumerate(SB_OF_ACC) if a == sl["acc"])
                    vector.reduce_sum(n2[:], ngrid[:, s + 1 - glen:s + 1], AX)
                    vector.drain()
                    vector.engine_nop().then_inc(s_n2, 1)
                    vector.wait_ge(s_rn, gp + 1)
                    vector.tensor_scalar(rn[:], rn[:], 1e12, None, ALU.min)
                    vector.drain()
                    xpar = gp % 2
                    vector.wait_ge(s_x3[xpar], 16 * (gp // 2 + 1))
                    if gp > 0:
                        vector.wait_ge(s_st, 16 * gp)
                    for nh in range(2):
                        vector.scalar_tensor_tensor(
                            x4b[:, nh * 512:(nh + 1) * 512],
                            accb[:, nh * 512:(nh + 1) * 512], rn[:, 0:1],
                            x3b[xpar][:, nh * 512:(nh + 1) * 512], ALU.mult, ALU.add)
                    vector.drain()
                    vector.engine_nop().then_inc(s_fin, 1)

    @blk.scalar
    def _(scalar):
        def s_copy(m):
            scalar.wait_ge(s_tw, m + 1)
            if m >= 2:
                scalar.wait_ge(s_mm, m - 1)
            scalar.copy(wmT[m % 2][:], pW[m % 2][:]).then_inc(s_twe, 1)

        copied = set()
        for pp in range(NSLOT // 2):
            s0, s1 = 2 * pp, 2 * pp + 1
            sl1 = slots[s1]
            scalar.wait_ge(s_q2, pp + 1)
            if pp >= 2:
                scalar.wait_ge(s_wm, pp - 1)
            scalar.activation(wraw[pp % 2][:], sqV[pp % 2][:], AF.Exp, scale=-1.0).then_inc(s_we, 1)
            for m in (s0 - 2, s0 - 1):
                if m >= 0 and m not in copied:
                    s_copy(m)
                    copied.add(m)
            if sl1["k"] == 7 and sl1["last"]:
                for m in (s0, s1):
                    if m not in copied:
                        s_copy(m)
                        copied.add(m)
                gp = sl1["gidx"]
                scalar.wait_ge(s_n2, gp + 1)
                scalar.activation(rn[:], n2[:], AF.Ln)
                scalar.drain()
                scalar.activation(rn[:], rn[:], AF.Exp, scale=-0.5).then_inc(s_rn, 1)

    @blk.tensor
    def _(tensor):
        if fused:
            tensor.wait_ge(s_ld, 16 * 4)
        else:
            tensor.wait_ge(s_ld, 16 * 6)
        for s in range(NSLOT):
            sl = slots[s]
            par = s % 2
            tensor.wait_ge(s_wm, s // 2 + 1)
            if s >= 2:
                tensor.wait_ge(s_twe, s - 1)
            tensor.transpose(pW[par][:],
                             wm[(s // 2) % 2][:, (s % 2) * 128:(s % 2) * 128 + 128],
                             id_sb[:]).then_inc(s_tw, 1)
            tensor.wait_ge(s_twe, s + 1)
            tensor.wait_ge(s_lt[s % NLB], 16 * (s // NLB + 1))
            sbi = sl["sb"]
            if sl["k"] == 0 and sbi >= 2:
                tensor.wait_ge(s_acc, (sbi - 1) * 2)
            for nh in range(2):
                mm = tensor.matmul(
                    pCtx[sbi % 2][:, nh * 512:(nh + 1) * 512],
                    wmT[par][:],
                    ln3b[s % NLB][:, nh * 512:(nh + 1) * 512],
                    start=(sl["k"] == 0), stop=(sl["k"] == 7))
                if nh == 1:
                    mm.then_inc(s_mm, 1)

    if fused:
        p.pop_scope()
        p.pfx = ""
    return p


def host_prep_c2(core, ln3_full, q1, q3, x3):
    jts = [core, 15 - core, 16 + core, 31 - core]
    lens = [core + 1, 16 - core, 17 + core, 32 - core]
    ln3s = np.zeros((NSLOT, 128, D), dtype=BF)
    masks = np.zeros((NSLOT, 128, 128), dtype=BF)
    q1c = np.zeros((128, NSLOT), dtype=np.float32)
    q3c = np.full((128, NSLOT), 30.0, dtype=np.float32)
    itc = np.zeros((128, NSLOT), dtype=np.float32)
    x3row = np.zeros((16, 128, D), dtype=np.float32)
    tril = np.tril(np.ones((128, 128), np.float32)).astype(BF)
    ones = np.ones((128, 128), dtype=BF)
    s = 0
    for b in range(4):
        for sb_b in range(10):
            acc = SB_OF_ACC[sb_b]
            sb_local = sb_b - SB_OF_ACC.index(acc)
            jt = jts[acc]
            for k in range(8):
                it = sb_local * 8 + k
                if it < lens[acc]:
                    ln3s[s] = ln3_full[b * T + it * 128: b * T + it * 128 + 128]
                    masks[s] = tril if it == jt else ones
                    q1c[:, s] = q1[b, jt * 128: (jt + 1) * 128]
                    q3c[:, s] = q3[b, jt * 128: (jt + 1) * 128]
                    itc[:, s] = 128.0 * it
                s += 1
    for b in range(4):
        for a in range(4):
            x3row[b * 4 + a] = x3[b * T + jts[a] * 128: b * T + (jts[a] + 1) * 128]
    return {"ln3s": ln3s, "masks": masks, "q1c": q1c, "q3c": q3c, "itc": itc,
            "x3row": x3row, "iota": np.tile(np.arange(128, dtype=np.float32), (128, 1)),
            "ident": np.eye(128, dtype=BF)}


def run_c2(inputs, ln3_full, q1, q3, x3):
    p = build_c2()
    in_maps = [host_prep_c2(k, ln3_full, q1, q3, x3) for k in range(NCORES)]
    res = run_bass_kernel_spmd(p.nc, in_maps, core_ids=list(range(NCORES)))
    x4 = np.array(x3, dtype=np.float32, copy=True).reshape(TOK, D)
    for k in range(NCORES):
        jts = [k, 15 - k, 16 + k, 31 - k]
        rows = np.asarray(res.results[k]["x4row"])
        for b in range(4):
            for a in range(4):
                x4[b * T + jts[a] * 128: b * T + (jts[a] + 1) * 128] = rows[b * 4 + a]
    return x4



def build_l1f(p=None, blk=None, io=None):
    NPT = 4
    POS = 512
    ND8 = D // 128

    fused = io is not None
    if not fused:
        p = Prog()
    else:
        p.push_scope()
        p.pfx = "z_"
    x4row_in = io["x4s"] if fused else p.inp("x4row", [16, 128, D])
    wstack = io["wstack"] if fused else p.inp("wstack", [8 * D, D])
    bias_c = p.inp("bias_c", [128, 3 * ND8])
    bo_r = p.inp("bo_r", [1, D])
    inc_c = p.inp("inc_c", [128, ND8])
    ident = io["ident"] if fused else p.inp("ident", [128, 128], BF16)
    ones128 = io["ones128"] if fused else p.inp("ones128", [1, 128], BF16)
    out_rows = io["outrows"] if fused else p.out("outrows", [16, 128, D])

    wscr = io["wscr"] if fused else p.nc.dram_tensor(p.pfx + "wscr", [8 * D, D], BF16, kind="Internal").ap()

    xr = p.sb("xr", [128, 4 * D])
    ln4 = p.sb("ln4", [128, 4 * D], BF16)
    ln4T = p.sb("ln4T", [128, 4 * ND8 * POS], BF16)
    cst = p.sb("cst", [128, ND8 * POS])
    cbf = p.sb("cbf", [128, ND8 * POS], BF16)
    gf = p.sb("gf", [128, POS], BF16)
    gi = p.sb("gi", [128, POS], BF16)
    gg = p.sb("gg", [128, POS], BF16)
    t1g = p.sb("t1g", [128, POS], F32)
    if not fused:
        wstg = [p.sb(f"wstg{i}", [128, D], F32) for i in range(2)]
        wstgb = [p.sb(f"wstgb{i}", [128, D], BF16) for i in range(2)]
    wsl = [p.sb(f"wsl{i}", [128, ND8 * 128], BF16) for i in range(6)]
    wfin = [p.sb(f"wfin{i}", [128, ND8 * 512], BF16) for i in range(4)]
    bias_sb = p.sb("bias_sb", [128, 3 * ND8], F32)
    bo_sb = p.sb("bo_sb", [1, D], F32)
    bo_bf = p.sb("bo_bf", [1, D], BF16)
    inc_sb = p.sb("inc_sb", [128, ND8], F32)
    ones_t = p.sb("ones_t", [128, POS], F32)
    id_sb = p.sb("id_sb", [128, 128], BF16)
    ones_sb = p.sb("ones_sb", [1, 128], BF16)
    m_t = p.sb("m_t", [128, 4], F32)
    t1_t = p.sb("t1_t", [128, 4], F32)
    sq_t = p.sb("sq_t", [128, 4], F32)
    var_t = p.sb("var_t", [128, 4], F32)
    rstd_t = p.sb("rstd_t", [128, 4], F32)
    scr = p.sb("scr", [128, D], BF16)
    xout = p.sb("xout", [128, D], F32)

    pF = p.ps("pF", [128, POS], F32)
    pI = p.ps("pI", [128, POS], F32)
    pG = p.ps("pG", [128, POS], F32)
    pT = [p.ps(f"pT{i}", [128, 128], BF16) for i in range(2)]
    pH = [p.ps(f"pH{i}", [128, 512], F32) for i in range(2)]

    s_w = p.sem("s_w")
    if not fused:
        s_ws = [p.sem(f"s_ws{i}") for i in range(2)]
        s_wsc = p.sem("s_wsc")
        s_wst = [p.sem("s_wst0"), p.sem("s_wst1")]
    s_x = p.sem("s_x")
    s_xo = p.sem("s_xo")
    s_sq = p.sem("s_sq")
    s_var = p.sem("s_var")
    s_rstd = p.sem("s_rstd")
    s_ln = p.sem("s_ln")
    s_lnT = p.sem("s_lnT")
    s_lnTe = p.sem("s_lnTe")
    s_ci = p.sem("s_ci")
    s_sl = [p.sem(f"s_sl{i}") for i in range(6)]
    s_gm = p.sem("s_gm")
    s_ge = p.sem("s_ge")
    s_cu = p.sem("s_cu")
    s_cb = p.sem("s_cb")
    s_fw = [p.sem(f"s_fw{i}") for i in range(2)]
    s_wc2 = p.sem("s_wc2")
    s_hm = p.sem("s_hm")
    s_he = p.sem("s_he")
    s_fin = p.sem("s_fin")
    s_st = p.sem("s_st")

    if not fused:
        blk = p.block()

    @blk.sync
    def _(sync):
        if fused:
            sync.wait_ge(io["s_wpre"], 1)
        else:
            for i in range(64):
                par = i % 2
                if i >= 2:
                    sync.wait_ge(s_wsc, i - 1)
                sync.dma_start(out=wstg[par][:], in_=wstack[i * 128:(i + 1) * 128, :]).then_inc(s_ws[par], 16)
        cnt = [0] * 6
        for bstep in range(4):
            for ft in range(8):
                for j in range(6):
                    buf = j
                    g = (bstep * 8 + ft) * 6 + j
                    if g >= 6:
                        sync.wait_ge(s_gm, (bstep * 8 + ft))
                    sync.dma_start(
                        out=wsl[buf][:].rearrange("p (kt f) -> p kt f", f=128),
                        in_=wscr[j * D:(j + 1) * D, ft * 128:(ft + 1) * 128]
                        .rearrange("(kt p) f -> p kt f", p=128),
                    ).then_inc(s_sl[buf], 16)
        if not fused:
            sync.wait_ge(s_wst[0], 16 * 32)
            sync.wait_ge(s_wst[1], 16 * 32)
        for nh in range(2):
            sync.dma_start(
                out=wfin[nh][:].rearrange("p (kt f) -> p kt f", f=512),
                in_=wscr[6 * D:7 * D, nh * 512:(nh + 1) * 512]
                .rearrange("(kt p) f -> p kt f", p=128),
            ).then_inc(s_fw[0], 16)
            sync.dma_start(
                out=wfin[2 + nh][:].rearrange("p (kt f) -> p kt f", f=512),
                in_=wscr[7 * D:8 * D, nh * 512:(nh + 1) * 512]
                .rearrange("(kt p) f -> p kt f", p=128),
            ).then_inc(s_fw[0], 16)

    @blk.gpsimd
    def _(g):
        g.dma_start(out=bias_sb[:], in_=bias_c).then_inc(s_w, 16)
        g.dma_start(out=bo_sb[:], in_=bo_r).then_inc(s_w, 16)
        g.dma_start(out=inc_sb[:], in_=inc_c).then_inc(s_w, 16)
        g.dma_start(out=id_sb[:], in_=ident).then_inc(s_w, 16)
        g.dma_start(out=ones_sb[:], in_=ones128).then_inc(s_w, 16)
        if not fused:
            for i in range(64):
                g.wait_ge(s_wsc, i + 1)
                g.dma_start(out=wscr[i * 128:(i + 1) * 128, :], in_=wstgb[i % 2][:]).then_inc(s_wst[i % 2], 16)
        for bstep in range(4):
            g.wait_ge(s_ln, bstep * 4)
            g.dma_start(out=xr[:].rearrange("p (tt d) -> p tt d", d=D),
                        in_=x4row_in[bstep * 4:(bstep + 1) * 4]
                        .rearrange("tt p d -> p tt d")).then_inc(s_x, 16)
        for fb in range(4):
            for pt in range(NPT):
                gp = fb * NPT + pt
                if gp > 0:
                    g.wait_ge(s_st, 16 * gp)
                g.dma_start(out=xout[:], in_=x4row_in[fb * 4 + pt]).then_inc(s_xo, 16)
                g.wait_ge(s_fin, gp + 1)
                g.dma_start(out=out_rows[fb * 4 + pt], in_=xout[:]).then_inc(s_st, 16)
        g.wait_ge(s_st, 16 * 16)

    @blk.vector
    def _(vector):
        if not fused:
            for i in range(64):
                par = i % 2
                vector.wait_ge(s_ws[par], 16 * (i // 2 + 1))
                if i >= 2:
                    vector.wait_ge(s_wst[par], 16 * (i // 2))
                vector.tensor_copy(wstgb[par][:], wstg[par][:])
                vector.drain()
                vector.engine_nop().then_inc(s_wsc, 1)
        vector.wait_ge(s_w, 16 * 5)
        vector.tensor_copy(bo_bf[:], bo_sb[:])
        vector.drain()
        vector.engine_nop().then_inc(s_wc2, 1)
        vector.memset(ones_t[:], 1.0)
        vector.drain()
        for ft in range(ND8):
            vector.tensor_scalar(cst[:, ft * POS:(ft + 1) * POS], ones_t[:],
                                 inc_sb[:, ft:ft + 1], None, ALU.mult)
        vector.drain()
        vector.tensor_copy(cbf[:], cst[:])
        vector.drain()
        vector.engine_nop().then_inc(s_ci, 1)
        for bstep in range(4):
            vector.wait_ge(s_x, 16 * (bstep + 1))
            for tt in range(4):
                vector.reduce_sum(m_t[:, tt:tt + 1], xr[:, tt * D:(tt + 1) * D], AX)
            vector.drain()
            vector.tensor_scalar(m_t[:], m_t[:], 1.0 / D, None, ALU.mult)
            vector.drain()
            vector.tensor_tensor(t1_t[:], m_t[:], m_t[:], ALU.mult)
            vector.wait_ge(s_sq, bstep * 4 + 4)
            vector.tensor_scalar(var_t[:], sq_t[:], 1.0 / D, None, ALU.mult)
            vector.drain()
            vector.tensor_tensor(var_t[:], var_t[:], t1_t[:], ALU.subtract)
            vector.drain()
            vector.tensor_scalar(var_t[:], var_t[:], LN_EPS, None, ALU.add)
            vector.drain()
            vector.engine_nop().then_inc(s_var, 1)
            vector.wait_ge(s_rstd, bstep + 1)
            if bstep > 0:
                vector.wait_ge(s_lnTe, bstep * 4 * ND8)
            for tt in range(4):
                vector.tensor_scalar(
                    ln4[:, tt * D:(tt + 1) * D], xr[:, tt * D:(tt + 1) * D],
                    m_t[:, tt:tt + 1], rstd_t[:, tt:tt + 1], ALU.subtract, ALU.mult,
                ).then_inc(s_ln, 1)
            vector.drain()
            for ft in range(ND8):
                slot = bstep * 8 + ft
                vector.wait_ge(s_ge, 3 * (slot + 1))
                cs = cst[:, ft * POS:(ft + 1) * POS]
                vector.tensor_tensor(t1g[:], gi[:], gg[:], ALU.mult)
                vector.drain()
                vector.tensor_tensor(cs, cs, gf[:], ALU.mult)
                vector.drain()
                vector.tensor_tensor(cs, cs, t1g[:], ALU.add)
                vector.drain()
                vector.engine_nop().then_inc(s_cu, 1)
            vector.tensor_copy(cbf[:], cst[:])
            vector.drain()
            vector.engine_nop().then_inc(s_cb, 1)
        for fb in range(4):
            for pt in range(NPT):
                gp = fb * NPT + pt
                vector.wait_ge(s_xo, 16 * (gp + 1))
                vector.wait_ge(s_he, gp + 1)
                vector.tensor_tensor(xout[:], xout[:], scr[:], ALU.add)
                vector.drain()
                vector.engine_nop().then_inc(s_fin, 1)

    @blk.scalar
    def _(scalar):
        for bstep in range(4):
            scalar.wait_ge(s_x, 16 * (bstep + 1))
            for tt in range(4):
                scalar.activation(scr[:], xr[:, tt * D:(tt + 1) * D], AF.Square,
                                  accum_out=sq_t[:, tt:tt + 1]).then_inc(s_sq, 1)
                scalar.drain()
            scalar.wait_ge(s_var, bstep + 1)
            scalar.activation(rstd_t[:], var_t[:], AF.Ln)
            scalar.drain()
            scalar.activation(rstd_t[:], rstd_t[:], AF.Exp, scale=-0.5).then_inc(s_rstd, 1)
            for i in range(4 * ND8):
                scalar.wait_ge(s_lnT, bstep * 4 * ND8 + i + 1)
                tt, dt = divmod(i, ND8)
                scalar.copy(ln4T[:, (bstep * ND8 + dt) * POS + tt * 128:
                               (bstep * ND8 + dt) * POS + (tt + 1) * 128],
                            pT[i % 2][:]).then_inc(s_lnTe, 1)
            for ft in range(ND8):
                slot = bstep * 8 + ft
                scalar.wait_ge(s_gm, slot + 1)
                if slot >= 1:
                    scalar.wait_ge(s_cu, slot)
                scalar.activation(gf[:], pF[:], AF.Sigmoid,
                                  bias=bias_sb[:, ft:ft + 1]).then_inc(s_ge, 1)
                scalar.activation(gi[:], pI[:], AF.Sigmoid,
                                  bias=bias_sb[:, ND8 + ft:ND8 + ft + 1]).then_inc(s_ge, 1)
                scalar.activation(gg[:], pG[:], AF.Tanh,
                                  bias=bias_sb[:, 2 * ND8 + ft:2 * ND8 + ft + 1]).then_inc(s_ge, 1)
                scalar.drain()
        for fb in range(4):
            for pt in range(NPT):
                gp = fb * NPT + pt
                scalar.wait_ge(s_hm, gp * 2 + 2)
                if gp > 0:
                    scalar.wait_ge(s_fin, gp)
                scalar.activation(scr[:, 0:512], pH[0][:], AF.Tanh)
                scalar.drain()
                scalar.activation(scr[:, 512:1024], pH[1][:], AF.Tanh).then_inc(s_he, 1)
                scalar.drain()

    @blk.tensor
    def _(tensor):
        tensor.wait_ge(s_w, 16 * 5)
        for bstep in range(4):
            for i in range(4 * ND8):
                tt, dt = divmod(i, ND8)
                tensor.wait_ge(s_ln, bstep * 4 + tt + 1)
                g = bstep * 4 * ND8 + i
                if g >= 2:
                    tensor.wait_ge(s_lnTe, g - 1)
                tensor.transpose(pT[i % 2][:], ln4[:, tt * D + dt * 128: tt * D + (dt + 1) * 128],
                                 id_sb[:]).then_inc(s_lnT, 1)
            tensor.wait_ge(s_lnTe, (bstep + 1) * 4 * ND8)
            if bstep == 0:
                tensor.wait_ge(s_ci, 1)
            else:
                tensor.wait_ge(s_cb, bstep)
            for ft in range(ND8):
                slot = bstep * 8 + ft
                if slot >= 1:
                    tensor.wait_ge(s_ge, 3 * slot)
                for j, ps in ((0, pF), (1, pI), (2, pG)):
                    buf = j
                    tensor.wait_ge(s_sl[buf], 16 * (slot + 1))
                    for kt in range(ND8):
                        tensor.matmul(
                            ps[:],
                            wsl[buf][:, kt * 128:(kt + 1) * 128],
                            ln4T[:, (bstep * ND8 + kt) * POS:(bstep * ND8 + kt + 1) * POS],
                            start=(kt == 0), stop=False)
                for j, ps in ((3, pF), (4, pI), (5, pG)):
                    buf = j
                    tensor.wait_ge(s_sl[buf], 16 * (slot + 1))
                    for kt in range(ND8):
                        mm = tensor.matmul(
                            ps[:],
                            wsl[buf][:, kt * 128:(kt + 1) * 128],
                            cbf[:, kt * POS:(kt + 1) * POS],
                            start=False, stop=(kt == ND8 - 1))
                        if j == 5 and kt == ND8 - 1:
                            mm.then_inc(s_gm, 1)
        tensor.wait_ge(s_fw[0], 16 * 4)
        tensor.wait_ge(s_cb, 4)
        tensor.wait_ge(s_wc2, 1)
        for fb in range(4):
            for pt in range(NPT):
                gp = fb * NPT + pt
                for nh in range(2):
                    g = gp * 2 + nh
                    if g >= 2:
                        tensor.wait_ge(s_he, g // 2)
                    for kt in range(ND8):
                        tensor.matmul(
                            pH[nh][:],
                            ln4T[:, (fb * ND8 + kt) * POS + pt * 128:(fb * ND8 + kt) * POS + (pt + 1) * 128],
                            wfin[nh][:, kt * 512:(kt + 1) * 512],
                            start=(kt == 0), stop=False)
                    for kt in range(ND8):
                        tensor.matmul(
                            pH[nh][:],
                            cbf[:, kt * POS + pt * 128: kt * POS + (pt + 1) * 128],
                            wfin[2 + nh][:, kt * 512:(kt + 1) * 512],
                            start=False, stop=False)
                    tensor.matmul(pH[nh][:], ones_sb[:], bo_bf[:, nh * 512:(nh + 1) * 512],
                                  start=False, stop=True).then_inc(s_hm, 1)

    if fused:
        p.pop_scope()
        p.pfx = ""
    return p


def host_prep_l1f(inputs):
    ws = np.concatenate([
        _f32(inputs["L1_ffn_wf"]).T, _f32(inputs["L1_ffn_wi"]).T, _f32(inputs["L1_ffn_wc"]).T,
        _f32(inputs["L1_ffn_uf"]).T, _f32(inputs["L1_ffn_ui"]).T, _f32(inputs["L1_ffn_uc"]).T,
        _f32(inputs["L1_ffn_wo"]).T, _f32(inputs["L1_ffn_uo"]).T], axis=0)
    bias_c = np.concatenate([
        _f32(inputs["L1_ffn_bf"]).reshape(8, 128).T,
        _f32(inputs["L1_ffn_bi"]).reshape(8, 128).T,
        _f32(inputs["L1_ffn_bc"]).reshape(8, 128).T], axis=1)
    return {
        "wstack": ws, "bias_c": bias_c,
        "bo_r": _f32(inputs["L1_ffn_bo"]).reshape(1, D),
        "inc_c": _f32(inputs["L1_ffn_in_c"]).reshape(8, 128).T,
        "ident": np.eye(128, dtype=BF),
        "ones128": np.ones((1, 128), dtype=BF),
    }


def run_l1f(inputs, x4):
    p = build_l1f()
    com = host_prep_l1f(inputs)
    in_maps = []
    for k in range(NCORES):
        jts = [k, 15 - k, 16 + k, 31 - k]
        rows = np.zeros((16, 128, D), dtype=np.float32)
        for b in range(4):
            for a in range(4):
                rows[b * 4 + a] = x4[b * T + jts[a] * 128: b * T + (jts[a] + 1) * 128]
        m = dict(com)
        m["x4row"] = rows
        in_maps.append(m)
    res = run_bass_kernel_spmd(p.nc, in_maps, core_ids=list(range(NCORES)))
    out = np.zeros((TOK, D), dtype=np.float32)
    for k in range(NCORES):
        jts = [k, 15 - k, 16 + k, 31 - k]
        rows = np.asarray(res.results[k]["outrows"])
        for b in range(4):
            for a in range(4):
                out[b * T + jts[a] * 128: b * T + (jts[a] + 1) * 128] = rows[b * 4 + a]
    return out



def build_fused():
    p = Prog()
    ident = p.inp("ident", [128, 128], BF16)
    identf = p.inp("identf", [128, 128], F32)
    ones128 = p.inp("ones128", [1, 128], BF16)
    x2s = p.dram("x2s", [SHARD, D])
    wgstg = p.dram("wgstg", [D, WG_ROWS], BF16)
    wustg = p.dram("wustg", [D, WG_ROWS], BF16)
    wdstg = p.dram("wdstg", [FFN_I, WD_ROWS], BF16)
    wgg = p.dram("wgg", [NCORES * D, WG_ROWS], BF16)
    wug = p.dram("wug", [NCORES * D, WG_ROWS], BF16)
    wdg = p.dram("wdg", [NCORES * FFN_I, WD_ROWS], BF16)
    x3s = p.dram("x3s", [SHARD, D])
    x3g = p.dram("x3g", [NCORES * SHARD, D])
    ln3stg = p.dram("ln3stg", [SHARD, D], BF16)
    ln3g = p.dram("ln3g", [NCORES * SHARD, D], BF16)
    xgstg = p.dram("xgstg", [80, SHARD], BF16)
    xgg = p.dram("xgg", [NCORES * 80, SHARD], BF16)
    q1s = p.dram("q1s", [4, T])
    q3s = p.dram("q3s", [4, T])
    q16s = p.dram("q16s", [128, 32])
    x4s = p.dram("x4s", [16, 128, D])
    outrows = p.out("outrows", [16, 128, D])

    s_ph = p.sem_fixed("s_ph")
    s_clr = p.sem_fixed("s_clr")
    s_cc1 = p.sem_fixed("s_cc1")
    s_cc2 = p.sem_fixed("s_cc2")
    s_xt2 = p.sem_fixed("s_xt2")
    s_wpre = p.sem_fixed("s_wpre")
    wstack = p.inp("z_wstack", [8 * D, D])
    wscr = p.nc.dram_tensor("z_wscr", [8 * D, D], BF16, kind="Internal").ap()

    blk0 = p.block()
    xofs_reg = p.es.enter_context(p.nc.gpsimd.register("dyn_xofs"))
    base = dict(ident=ident, identf=identf, ones128=ones128, xofs_reg=xofs_reg)

    p.init_pools()

    def phase(idx, entry=None):
        p.begin_phase_sems()
        return PhasedBlk(blk0, s_ph, s_clr, idx, p.pool_range, entry)

    p.defer_pops = True
    build_l0a(p, phase(0), dict(base, x2=x2s))
    p.defer_pops = False
    build_l0f_wq(p, PlainBlk(blk0, s_ph), dict(base, wgstg=wgstg, wustg=wustg,
                                               wdstg=wdstg, wgg=wgg, wug=wug,
                                               wdg=wdg, s_cc=s_cc1))
    p.close_deferred()
    build_l0f(p, phase(1, entry=10), dict(base, x2=x2s, wgg=wgg, wug=wug, wdg=wdg, x3=x3s))
    build_c0(p, phase(2, entry=15), dict(base, x3=x3s, ln3stg=ln3stg, xgstg=xgstg,
                                         x3g=x3g, ln3g=ln3g, xgg=xgg, s_cc=s_cc2))
    build_c1(p, phase(3, entry=20), dict(base, q1s=q1s, q3s=q3s, q16s=q16s, xgg=xgg,
                                         s_cc=s_cc2, wstack=wstack, wscr=wscr, s_wpre=s_wpre))
    build_c2(p, phase(4, entry=25), dict(base, ln3g=ln3g, x3g=x3g, q16s=q16s,
                                         x4s=x4s, s_xt2=s_xt2, s_cc=s_cc2))
    build_l1f(p, phase(5, entry=30), dict(base, x4s=x4s, outrows=outrows,
                                          wstack=wstack, wscr=wscr, s_wpre=s_wpre))
    p.pool_mode = False
    return p


def host_prep_c2f(core):
    jts = [core, 15 - core, 16 + core, 31 - core]
    lens = [core + 1, 16 - core, 17 + core, 32 - core]
    masks = np.zeros((NSLOT, 128, 128), dtype=BF)
    itc = np.zeros((128, NSLOT), dtype=np.float32)
    tril = np.tril(np.ones((128, 128), np.float32)).astype(BF)
    ones = np.ones((128, 128), dtype=BF)

    def owner_row(b, trow):
        k = b * 2 + (0 if trow < SHARD else 1)
        return k * SHARD + (trow - (k % 2) * SHARD)

    s = 0
    for b in range(4):
        for sb_b in range(10):
            acc = SB_OF_ACC[sb_b]
            sb_local = sb_b - SB_OF_ACC.index(acc)
            jt = jts[acc]
            for k in range(8):
                it = sb_local * 8 + k
                if it < lens[acc]:
                    masks[s] = tril if it == jt else ones
                    itc[:, s] = 128.0 * it
                s += 1
    xtab = np.zeros((1, 16), dtype=np.int32)
    sel = np.zeros((32, 16), dtype=np.float32)
    for b in range(4):
        for a in range(4):
            xtab[0, b * 4 + a] = owner_row(b, jts[a] * 128) * D
            sel[jts[a], b * 4 + a] = 1.0
    return {
        "g_masks": masks, "g_itc": itc,
        "g_iota": np.tile(np.arange(128, dtype=np.float32), (128, 1)),
        "g_xtab": xtab, "m_sel": sel,
    }


def kernel_fused(inputs):
    x = _f32(inputs["L0_attn_x"]).reshape(TOK, D)
    a_com = {("a_" + k): v for k, v in host_prep_l0a(inputs).items()
             if k not in ("ident", "ones128")}
    wg, wu, wd = (_f32(inputs[k]) for k in ("L0_ffn_w_gate", "L0_ffn_w_up", "L0_ffn_w_down"))
    c_com = {("c_" + k): v for k, v in host_prep_c0(inputs).items()
             if k not in ("ident", "ones128", "identf")}
    m_com_full = host_prep_c1(inputs, [np.zeros((80, SHARD), BF)] * NCORES)
    m_com = {("m_" + k): v for k, v in m_com_full.items()
             if k not in ("identf", "xg_mega")}
    z_com = {("z_" + k): v for k, v in host_prep_l1f(inputs).items()
             if k not in ("ident", "ones128")}
    shared = {
        "ident": np.eye(128, dtype=BF),
        "identf": np.eye(128, dtype=np.float32),
        "ones128": np.ones((1, 128), dtype=BF),
    }
    prog = build_fused()
    in_maps = []
    for k in range(NCORES):
        m = dict(shared)
        m.update(a_com); m.update(c_com); m.update(m_com); m.update(z_com)
        m["a_x"] = np.ascontiguousarray(x[k * SHARD:(k + 1) * SHARD])
        m["q_wg"] = np.ascontiguousarray(wg[k * WG_ROWS:(k + 1) * WG_ROWS])
        m["q_wu"] = np.ascontiguousarray(wu[k * WG_ROWS:(k + 1) * WG_ROWS])
        m["q_wd"] = np.ascontiguousarray(wd[k * WD_ROWS:(k + 1) * WD_ROWS])
        m.update(host_prep_c2f(k))
        in_maps.append(m)
    res = run_bass_kernel_spmd(prog.nc, in_maps, core_ids=list(range(NCORES)))
    out = np.zeros((TOK, D), dtype=np.float32)
    for k in range(NCORES):
        jts = [k, 15 - k, 16 + k, 31 - k]
        rows = np.asarray(res.results[k]["outrows"])
        for b in range(4):
            for a in range(4):
                out[b * T + jts[a] * 128: b * T + (jts[a] + 1) * 128] = rows[b * 4 + a]
    return _f32(out.reshape(B, T, D))


# version 23 (final)
# speedup vs baseline: 1.0010x; 1.0010x over previous
import contextlib
import numpy as np
import ml_dtypes

import concourse.bass as bass
import concourse.mybir as mybir
from concourse.bass_utils import run_bass_kernel_spmd

F32 = mybir.dt.float32
BF16 = mybir.dt.bfloat16
I32 = mybir.dt.int32
AF = mybir.ActivationFunctionType
ALU = mybir.AluOpType
AX = mybir.AxisListType.X

NCORES = 8
B, T, D = 4, 4096, 1024
TOK = B * T
SHARD = TOK // NCORES
E = 2048
SST = 16
FFN_I = 2816
LN_EPS = 1e-5

BF = ml_dtypes.bfloat16


def _bf16(x):
    return np.ascontiguousarray(np.asarray(x)).astype(BF)


def _f32(x):
    return np.ascontiguousarray(np.asarray(x), dtype=np.float32)


class Prog:
    def __init__(self):
        self.nc = bass.Bass("TRN2", target_bir_lowering=False)
        self.es = contextlib.ExitStack()
        self._scopes = [self.es]
        self.pfx = ""
        self.defer_pops = False
        self._deferred = []

    def push_scope(self):
        s = contextlib.ExitStack()
        self._scopes.append(s)
        return s

    def pop_scope(self):
        s = self._scopes.pop()
        if self.defer_pops:
            self._deferred.append(s)
        else:
            s.close()

    def close_deferred(self):
        while self._deferred:
            self._deferred.pop().close()

    def inp(self, name, shape, dtype=F32):
        return self.nc.dram_tensor(self.pfx + name, list(shape), dtype, kind="ExternalInput").ap()

    def out(self, name, shape, dtype=F32):
        return self.nc.dram_tensor(self.pfx + name, list(shape), dtype, kind="ExternalOutput").ap()

    def dram(self, name, shape, dtype=F32, shared=False):
        if shared:
            return self.nc.dram_tensor(self.pfx + name, list(shape), dtype,
                                       kind="Internal", addr_space="Shared").ap()
        return self.nc.dram_tensor(self.pfx + name, list(shape), dtype, kind="Internal").ap()

    def sb(self, name, shape, dtype=F32):
        return self._scopes[-1].enter_context(self.nc.sbuf_tensor(self.pfx + name, list(shape), dtype))

    def ps(self, name, shape, dtype=F32):
        return self._scopes[-1].enter_context(self.nc.psum_tensor(self.pfx + name, list(shape), dtype))

    DMA_SEM_NAMES = {
        "s_x", "s_w", "s_wl0", "s_wl1", "s_wo0", "s_wo1", "s_st", "s_ld",
        "s_ldi", "s_wg0", "s_wg1", "s_wu0", "s_wu1", "s_wd", "s_lt0", "s_lt1",
        "s_lt2", "s_lt3", "s_lt4", "s_lt5", "s_x0", "s_x1", "s_st0", "s_st1",
        "s_mk0", "s_mk1", "s_x30", "s_x31", "s_qd", "s_ws0", "s_ws1",
        "s_wst0", "s_wst1", "s_xo", "s_fw0", "s_fw1", "s_sl0", "s_sl1",
        "s_sl2", "s_sl3", "s_sl4", "s_sl5",
    }

    def sem(self, name):
        if getattr(self, "pool_mode", False):
            is_dma = name in self.DMA_SEM_NAMES
            pool, key = (self.pool_dma, "d") if is_dma else (self.pool_eng, "e")
            cur = self.pool_cursors[key]
            assert cur < len(pool), f"sem pool {key} exhausted ({name})"
            self.pool_cursors[key] = cur + 1
            return pool[cur]
        return self.es.enter_context(self.nc.semaphore(self.pfx + name))

    def sem_fixed(self, name):
        return self.es.enter_context(self.nc.semaphore(name))

    def init_pools(self, n_dma=20, n_eng=26):
        self.pool_dma = [self.es.enter_context(self.nc.semaphore(f"poolD{i}"))
                         for i in range(n_dma)]
        self.pool_eng = [self.es.enter_context(self.nc.semaphore(f"poolE{i}"))
                         for i in range(n_eng)]
        nums = sorted(s.num for s in self.pool_dma + self.pool_eng)
        assert nums == list(range(nums[0], nums[0] + len(nums))), nums
        self.pool_range = range(nums[0], nums[0] + len(nums))
        self.pool_mode = True

    def begin_phase_sems(self):
        self.pool_cursors = {"d": 0, "e": 0}

    def block(self):
        return self.es.enter_context(self.nc.Block())




class PhasedBlk:

    ENGINES = ("sync", "gpsimd", "vector", "scalar", "tensor")

    def __init__(self, blk, s_ph, s_clr, phase, pool_range, entry=None):
        self._blk = blk
        self._s_ph = s_ph
        self._s_clr = s_clr
        self._phase = phase
        self._range = pool_range
        self._entry = 5 * phase if entry is None else entry

    def __getattr__(self, name):
        if name not in self.ENGINES:
            raise AttributeError(name)
        real = getattr(self._blk, name)

        def deco(fn):
            def body(eng):
                if self._entry > 0:
                    eng.wait_ge(self._s_ph, self._entry)
                if name == "gpsimd":
                    if self._phase > 0:
                        eng.dma_reset(self._range)
                        eng.sem_clear(self._range)
                    eng.engine_nop().then_inc(self._s_clr, 1)
                else:
                    eng.wait_ge(self._s_clr, self._phase + 1)
                fn(eng)
                if name == "sync":
                    eng.nop().then_inc(self._s_ph, 1)
                elif name == "gpsimd":
                    eng.engine_nop().then_inc(self._s_ph, 1)
                else:
                    eng.drain().then_inc(self._s_ph, 1)
            body.__name__ = fn.__name__
            return real(body)
        return deco


class PlainBlk:

    ENGINES = PhasedBlk.ENGINES

    def __init__(self, blk, s_ph):
        self._blk = blk
        self._s_ph = s_ph

    def __getattr__(self, name):
        if name not in self.ENGINES:
            raise AttributeError(name)
        real = getattr(self._blk, name)

        def deco(fn):
            def body(eng):
                fn(eng)
                if name == "sync":
                    eng.nop().then_inc(self._s_ph, 1)
                elif name == "gpsimd":
                    eng.engine_nop().then_inc(self._s_ph, 1)
                else:
                    eng.drain().then_inc(self._s_ph, 1)
            body.__name__ = fn.__name__
            return real(body)
        return deco



def build_l0a(p=None, blk=None, io=None):
    fused = io is not None
    CHUNK = 512
    NT = CHUNK // 128
    NCH = SHARD // CHUNK
    NF = 2 * E // 128
    NE = E // 128
    ND = D // 128
    NHD = D // 512

    if not fused:
        p = Prog()
    else:
        p.push_scope()
        p.pfx = "a_"
    x_in = p.inp("x", [SHARD, D])
    w_inT = p.inp("w_inT", [D, 2 * E])
    w_bT = p.inp("w_bT", [E, SST])
    w_cT = p.inp("w_cT", [SST, E])
    w_outT = p.inp("w_outT", [E, D])
    b_in_c = p.inp("b_in_c", [128, NF])
    d_c = p.inp("d_c", [128, NE])
    b_out_r = p.inp("b_out_r", [1, D])
    ident = io["ident"] if fused else p.inp("ident", [128, 128], BF16)
    ones128 = io["ones128"] if fused else p.inp("ones128", [1, 128], BF16)
    x2_out = io["x2"] if fused else p.out("x2", [SHARD, D])

    xrs = [p.sb(f"xr{i}", [128, NT * D]) for i in range(2)]
    ln = p.sb("ln", [128, NT * D], BF16)
    lnT = p.sb("lnT", [128, ND * CHUNK], BF16)
    xprojT = p.sb("xprojT", [128, NE * CHUNK], BF16)
    zsilT = p.sb("zsilT", [128, NE * CHUNK], BF16)
    yT = p.sb("yT", [128, NE * CHUNK], BF16)
    wA = [p.sb(f"wA{i}", [128, D], F32) for i in range(2)]
    wAb = [p.sb(f"wAb{i}", [128, D], BF16) for i in range(2)]
    wstg = [p.sb(f"wstg{i}", [128, D], F32) for i in range(2)]
    wOb = p.sb("wOb", [128, NE * D], BF16)
    wb_sb = p.sb("wb_sb", [128, NE * SST], F32)
    wb_bf = p.sb("wb_bf", [128, NE * SST], BF16)
    wc_sb = p.sb("wc_sb", [SST, E], F32)
    wc_bf = p.sb("wc_bf", [SST, E], BF16)
    bin_sb = p.sb("bin_sb", [128, NF], F32)
    d_sb = p.sb("d_sb", [128, NE], F32)
    bout_sb = p.sb("bout_sb", [1, D], F32)
    bout_bf = p.sb("bout_bf", [1, D], BF16)
    id_sb = p.sb("id_sb", [128, 128], BF16)
    ones_sb = p.sb("ones_sb", [1, 128], BF16)
    hsT = p.sb("hsT", [SST, CHUNK], F32)
    hsTb = p.sb("hsTb", [SST, CHUNK], BF16)
    carry = p.sb("carry", [SST, 1], F32)
    c09 = p.sb("c09", [SST, 1], F32)
    m_t = p.sb("m_t", [128, NT], F32)
    t1_t = p.sb("t1_t", [128, NT], F32)
    sq_t = p.sb("sq_t", [128, NT], F32)
    var_t = p.sb("var_t", [128, NT], F32)
    rstd_t = p.sb("rstd_t", [128, NT], F32)
    scr = p.sb("scr", [128, D], BF16)

    pT = [p.ps(f"pT{i}", [128, 128], BF16) for i in range(2)]
    pMM = [p.ps(f"pMM{i}", [128, 512], F32) for i in range(2)]
    pB = p.ps("pB", [SST, CHUNK], F32)
    pC = [p.ps(f"pC{i}", [128, CHUNK], F32) for i in range(2)]

    s_x = [p.sem("s_x0"), p.sem("s_x1")]
    s_w = p.sem("s_w")
    s_wl = [p.sem("s_wl0"), p.sem("s_wl1")]
    s_wc = p.sem("s_wc")
    s_wo = [p.sem("s_wo0"), p.sem("s_wo1")]
    s_woc = p.sem("s_woc")
    s_sq = p.sem("s_sq")
    s_var = p.sem("s_var")
    s_rstd = p.sem("s_rstd")
    s_ln = p.sem("s_ln")
    s_lnT = p.sem("s_lnT")
    s_lnTe = p.sem("s_lnTe")
    s_mm = p.sem("s_mm")
    s_ev = p.sem("s_ev")
    s_b = p.sem("s_b")
    s_h = p.sem("s_h")
    s_c = p.sem("s_c")
    s_y = p.sem("s_y")
    s_o = p.sem("s_o")
    s_r = p.sem("s_r")
    s_st = [p.sem("s_st0"), p.sem("s_st1")]

    if not fused:
        blk = p.block()

    @blk.sync
    def _(sync):
        for c in range(NCH):
            if c > 1:
                sync.wait_ge(s_r, (c - 1) * NT * 2)
                sync.wait_ge(s_st[c % 2], 16 * (c // 2) * NT)
            sync.dma_start(
                out=xrs[c % 2][:].rearrange("p (tt d) -> p tt d", d=D),
                in_=x_in[c * CHUNK:(c + 1) * CHUNK, :].rearrange("(tt p) d -> p tt d", p=128),
            ).then_inc(s_x[c % 2], 16)
            for fc in range(NF):
                g = c * NF + fc
                if g >= 2:
                    sync.wait_ge(s_wc, g - 1)
                sync.dma_start(
                    out=wA[fc % 2][:].rearrange("p (kt f) -> p kt f", f=128),
                    in_=w_inT[:, fc * 128:(fc + 1) * 128].rearrange("(kt p) f -> p kt f", p=128),
                ).then_inc(s_wl[fc % 2], 16)

    @blk.gpsimd
    def _(g):
        g.dma_start(out=wb_sb[:].rearrange("p (et s) -> p et s", s=SST),
                    in_=w_bT.rearrange("(et p) s -> p et s", p=128)).then_inc(s_w, 16)
        g.dma_start(out=wc_sb[:], in_=w_cT).then_inc(s_w, 16)
        g.dma_start(out=bin_sb[:], in_=b_in_c).then_inc(s_w, 16)
        g.dma_start(out=d_sb[:], in_=d_c).then_inc(s_w, 16)
        g.dma_start(out=bout_sb[:], in_=b_out_r).then_inc(s_w, 16)
        g.dma_start(out=id_sb[:], in_=ident).then_inc(s_w, 16)
        g.dma_start(out=ones_sb[:], in_=ones128).then_inc(s_w, 16)
        for ec in range(NE):
            if ec >= 2:
                g.wait_ge(s_woc, ec - 1)
            g.dma_start(
                out=wstg[ec % 2][:],
                in_=w_outT[ec * 128:(ec + 1) * 128, :],
            ).then_inc(s_wo[ec % 2], 16)
        for c in range(NCH):
            for mt in range(NT):
                g.wait_ge(s_r, (c * NT + mt + 1) * 2)
                g.dma_start(
                    out=x2_out[c * CHUNK + mt * 128: c * CHUNK + (mt + 1) * 128, :],
                    in_=xrs[c % 2][:, mt * D:(mt + 1) * D],
                ).then_inc(s_st[c % 2], 16)
        g.wait_ge(s_st[0], 16 * ((NCH + 1) // 2) * NT)
        g.wait_ge(s_st[1], 16 * (NCH // 2) * NT)

    @blk.vector
    def _(vector):
        vector.memset(carry[:], 0.0)
        vector.memset(c09[:], 0.9)
        vector.drain()
        for ec in range(NE):
            vector.wait_ge(s_wo[ec % 2], 16 * (ec // 2 + 1))
            vector.tensor_copy(wOb[:, ec * D:(ec + 1) * D], wstg[ec % 2][:])
            vector.drain()
            vector.engine_nop().then_inc(s_woc, 1)
        vector.wait_ge(s_w, 16 * 7)
        vector.tensor_scalar(wb_bf[:], wb_sb[:], 0.1, None, ALU.mult)
        vector.tensor_copy(wc_bf[:], wc_sb[:])
        vector.tensor_copy(bout_bf[:], bout_sb[:])
        vector.drain()
        for c in range(NCH):
            vector.wait_ge(s_x[c % 2], 16 * (c // 2 + 1))
            for tt in range(NT):
                vector.reduce_sum(m_t[:, tt:tt + 1], xrs[c % 2][:, tt * D:(tt + 1) * D], AX)
            vector.drain()
            vector.tensor_scalar(m_t[:], m_t[:], 1.0 / D, None, ALU.mult)
            vector.drain()
            vector.tensor_tensor(t1_t[:], m_t[:], m_t[:], ALU.mult)
            vector.wait_ge(s_sq, c * NT + NT)
            vector.tensor_scalar(var_t[:], sq_t[:], 1.0 / D, None, ALU.mult)
            vector.drain()
            vector.tensor_tensor(var_t[:], var_t[:], t1_t[:], ALU.subtract)
            vector.drain()
            vector.tensor_scalar(var_t[:], var_t[:], LN_EPS, None, ALU.add)
            vector.drain()
            vector.engine_nop().then_inc(s_var, 1)
            vector.wait_ge(s_rstd, c + 1)
            for tt in range(NT):
                vector.tensor_scalar(
                    ln[:, tt * D:(tt + 1) * D], xrs[c % 2][:, tt * D:(tt + 1) * D],
                    m_t[:, tt:tt + 1], rstd_t[:, tt:tt + 1], ALU.subtract, ALU.mult,
                ).then_inc(s_ln, 1)
            vector.drain()
            for fc in range(NF):
                g = c * NF + fc
                par = fc % 2
                vector.wait_ge(s_wl[fc % 2], 16 * (g // 2 + 1))
                if g >= 2:
                    vector.wait_ge(s_mm, g - 1)
                vector.tensor_copy(wAb[par][:], wA[par][:])
                vector.drain()
                vector.engine_nop().then_inc(s_wc, 1)
            vector.wait_ge(s_b, c + 1)
            vector.tensor_tensor_scan(
                hsT[:], c09[:, 0:1].broadcast_to([SST, CHUNK]), pB[:],
                carry[:, 0:1], ALU.mult, ALU.add)
            vector.drain()
            vector.tensor_copy(carry[:, 0:1], hsT[:, CHUNK - 1:CHUNK])
            vector.tensor_copy(hsTb[:], hsT[:])
            vector.drain()
            vector.engine_nop().then_inc(s_h, 1)
            vector.wait_ge(s_ev, c * NF + NF)
            for ec in range(NE):
                g = c * NE + ec
                vector.wait_ge(s_c, g + 1)
                sl = slice(ec * CHUNK, (ec + 1) * CHUNK)
                vector.scalar_tensor_tensor(
                    yT[:, sl], xprojT[:, sl], d_sb[:, ec:ec + 1], pC[g % 2][:],
                    ALU.mult, ALU.add)
                vector.drain()
                vector.tensor_tensor(yT[:, sl], yT[:, sl], zsilT[:, sl], ALU.mult)
                vector.drain()
                vector.engine_nop().then_inc(s_y, 1)
            for mt in range(NT):
                for nh in range(2):
                    g = (c * NT + mt) * 2 + nh
                    vector.wait_ge(s_o, g + 1)
                    sl = slice(mt * D + nh * 512, mt * D + (nh + 1) * 512)
                    vector.tensor_tensor(xrs[c % 2][:, sl], xrs[c % 2][:, sl], pMM[g % 2][:], ALU.add)
                    vector.drain()
                    vector.engine_nop().then_inc(s_r, 1)

    @blk.scalar
    def _(scalar):
        for c in range(NCH):
            scalar.wait_ge(s_x[c % 2], 16 * (c // 2 + 1))
            for tt in range(NT):
                scalar.activation(scr[:], xrs[c % 2][:, tt * D:(tt + 1) * D], AF.Square,
                                  accum_out=sq_t[:, tt:tt + 1]).then_inc(s_sq, 1)
                scalar.drain()
            scalar.wait_ge(s_var, c + 1)
            scalar.activation(rstd_t[:], var_t[:], AF.Ln)
            scalar.drain()
            scalar.activation(rstd_t[:], rstd_t[:], AF.Exp, scale=-0.5).then_inc(s_rstd, 1)
            for i in range(NT * ND):
                scalar.wait_ge(s_lnT, c * NT * ND + i + 1)
                tt, dt = divmod(i, ND)
                scalar.copy(lnT[:, dt * CHUNK + tt * 128: dt * CHUNK + (tt + 1) * 128],
                            pT[i % 2][:]).then_inc(s_lnTe, 1)
            for fc in range(NF):
                g = c * NF + fc
                scalar.wait_ge(s_mm, g + 1)
                dst = xprojT if fc < NE else zsilT
                eci = fc if fc < NE else fc - NE
                scalar.activation(dst[:, eci * CHUNK:(eci + 1) * CHUNK], pMM[fc % 2][:],
                                  AF.Silu, bias=bin_sb[:, fc:fc + 1]).then_inc(s_ev, 1)
                scalar.drain()

    @blk.tensor
    def _(tensor):
        tensor.wait_ge(s_w, 16 * 7)
        for c in range(NCH):
            for i in range(NT * ND):
                tt, dt = divmod(i, ND)
                tensor.wait_ge(s_ln, c * NT + tt + 1)
                g = c * NT * ND + i
                if g >= 2:
                    tensor.wait_ge(s_lnTe, g - 1)
                tensor.transpose(pT[i % 2][:], ln[:, tt * D + dt * 128: tt * D + (dt + 1) * 128],
                                 id_sb[:]).then_inc(s_lnT, 1)
            tensor.wait_ge(s_lnTe, (c + 1) * NT * ND)
            if c > 0:
                tensor.wait_ge(s_r, c * NT * 2)
            for fc in range(NF):
                g = c * NF + fc
                tensor.wait_ge(s_wc, g + 1)
                if g >= 2:
                    tensor.wait_ge(s_ev, g - 1)
                par = fc % 2
                for kt in range(ND):
                    mm = tensor.matmul(
                        pMM[par][:],
                        wAb[par][:, kt * 128:(kt + 1) * 128],
                        lnT[:, kt * CHUNK: kt * CHUNK + CHUNK],
                        start=(kt == 0), stop=(kt == ND - 1),
                    )
                    if kt == ND - 1:
                        mm.then_inc(s_mm, 1)
            tensor.wait_ge(s_ev, c * NF + NE)
            if c > 0:
                tensor.wait_ge(s_h, c)
            for ec in range(NE):
                mm = tensor.matmul(
                    pB[:],
                    wb_bf[:, ec * SST:(ec + 1) * SST],
                    xprojT[:, ec * CHUNK:(ec + 1) * CHUNK],
                    start=(ec == 0), stop=(ec == NE - 1),
                )
                if ec == NE - 1:
                    mm.then_inc(s_b, 1)
            tensor.wait_ge(s_h, c + 1)
            for ec in range(NE):
                g = c * NE + ec
                if g >= 2:
                    tensor.wait_ge(s_y, g - 1)
                tensor.matmul(
                    pC[g % 2][:],
                    wc_bf[:, ec * 128:(ec + 1) * 128],
                    hsTb[:],
                    start=True, stop=True,
                ).then_inc(s_c, 1)
            tensor.wait_ge(s_y, (c + 1) * NE)
            tensor.wait_ge(s_woc, NE)
            tensor.wait_ge(s_ev, (c + 1) * NF)
            for mt in range(NT):
                for nh in range(NHD):
                    g = (c * NT + mt) * NHD + nh
                    if g >= 2:
                        tensor.wait_ge(s_r, g - 1)
                    par = g % 2
                    for ec in range(NE):
                        tensor.matmul(
                            pMM[par][:],
                            yT[:, ec * CHUNK + mt * 128: ec * CHUNK + (mt + 1) * 128],
                            wOb[:, ec * D + nh * 512: ec * D + nh * 512 + 512],
                            start=(ec == 0), stop=False,
                        )
                    tensor.matmul(
                        pMM[par][:],
                        ones_sb[:],
                        bout_bf[:, nh * 512:(nh + 1) * 512],
                        start=False, stop=True,
                    ).then_inc(s_o, 1)

    if fused:
        p.pop_scope()
        p.pfx = ""
    return p


def host_prep_l0a(inputs):
    com = {
        "w_inT": _f32(inputs["L0_attn_W_in"].T),
        "w_bT": _f32(inputs["L0_attn_W_B"].T),
        "w_cT": _f32(inputs["L0_attn_W_C"].T),
        "w_outT": _f32(inputs["L0_attn_W_out"].T),
        "b_in_c": _f32(inputs["L0_attn_b_in"].reshape(2 * E // 128, 128).T),
        "d_c": _f32(inputs["L0_attn_D"].reshape(E // 128, 128).T),
        "b_out_r": _f32(inputs["L0_attn_b_out"].reshape(1, D)),
        "ident": np.eye(128, dtype=BF),
        "ones128": np.ones((1, 128), dtype=BF),
    }
    return com


def run_l0a(inputs):
    x = _f32(inputs["L0_attn_x"]).reshape(TOK, D)
    com = host_prep_l0a(inputs)
    p = build_l0a()
    in_maps = []
    for k in range(NCORES):
        m = dict(com)
        m["x"] = x[k * SHARD:(k + 1) * SHARD]
        in_maps.append(m)
    res = run_bass_kernel_spmd(p.nc, in_maps, core_ids=list(range(NCORES)))
    x2 = np.concatenate([np.asarray(r["x2"]) for r in res.results], axis=0)
    return x2.reshape(B, T, D)


def kernel(**inputs):
    return kernel_fused(inputs)


def kernel_unfused(**inputs):
    x2 = run_l0a(inputs).reshape(TOK, D)
    wq = run_l0f_wq(inputs)
    x3 = run_l0f(inputs, np.ascontiguousarray(x2), wq)
    ln3, xgs = run_c0(inputs, x3)
    q1, q3 = run_c1(inputs, xgs)
    x4 = run_c2(inputs, ln3, q1, q3, x3)
    out = run_l1f(inputs, x4)
    return _f32(out.reshape(B, T, D))



class QuantScratch:
    def __init__(self, p, name, F, nblk):
        self.F = F
        self.nblk = nblk
        self.bm = p.sb(f"{name}_bm", [128, nblk], F32)
        self.r = p.sb(f"{name}_r", [128, nblk], F32)
        self.sh = p.sb(f"{name}_sh", [128, nblk], F32)
        self.xr = p.sb(f"{name}_xr", [128, F], BF16)
        self.u = p.sb(f"{name}_u", [128, F], BF16)
        self.sg = p.sb(f"{name}_sg", [128, F], BF16)
        self.a = p.sb(f"{name}_a", [128, F], BF16)
        self.b = p.sb(f"{name}_b", [128, F], BF16)
        self.c = p.sb(f"{name}_c", [128, F], BF16)


def quant_v_pre(vector, q, x_ap, F):
    nblk = F // 16
    bm, r, sh, xr = q.bm[:, 0:nblk], q.r[:, 0:nblk], q.sh[:, 0:nblk], q.xr[:, 0:F]
    vector.tensor_reduce(
        bm.rearrange("p (n o) -> p n o", o=1),
        x_ap.rearrange("p (n b) -> p n b", b=16),
        AX, ALU.max, apply_absolute_value=True)
    vector.drain()
    vector.tensor_scalar(bm, bm, 1.0 / 6.0, 1e-12, ALU.mult, ALU.max)
    vector.drain()
    vector.reciprocal(r, bm)
    vector.tensor_scalar(sh, bm, 0.5, None, ALU.mult)
    vector.drain()
    rb = r.rearrange("p (n o) -> p n o", o=1).broadcast_to([128, nblk, 16])
    vector.tensor_tensor(xr.rearrange("p (n b) -> p n b", b=16), x_ap.rearrange("p (n b) -> p n b", b=16), rb, ALU.mult)
    vector.drain()


def quant_s_mid(scalar, q, F):
    scalar.activation(q.u[:, 0:F], q.xr[:, 0:F], AF.Abs)
    return scalar.activation(q.sg[:, 0:F], q.xr[:, 0:F], AF.Sign)


def quant_v_chain(vector, q, out_ap, F):
    nblk = F // 16
    u = q.u[:, 0:F]
    a, b_, cc, sg = q.a[:, 0:F], q.b[:, 0:F], q.c[:, 0:F], q.sg[:, 0:F]
    vector.tensor_single_scalar(a, u, 0.25, ALU.is_gt)
    vector.tensor_single_scalar(b_, u, 5.0, ALU.is_gt)
    vector.drain()
    vector.scalar_tensor_tensor(a, u, 0.75, a, ALU.is_gt, ALU.add)
    vector.scalar_tensor_tensor(cc, u, 3.5, b_, ALU.is_gt, ALU.add)
    vector.drain()
    vector.scalar_tensor_tensor(a, u, 1.25, a, ALU.is_gt, ALU.add)
    vector.scalar_tensor_tensor(cc, u, 2.5, cc, ALU.is_gt, ALU.add)
    vector.drain()
    vector.scalar_tensor_tensor(a, u, 1.75, a, ALU.is_gt, ALU.add)
    vector.tensor_tensor(b_, cc, b_, ALU.add)
    vector.drain()
    vector.scalar_tensor_tensor(cc, b_, 2.0, a, ALU.mult, ALU.add)
    vector.drain()
    vector.tensor_tensor(cc, cc, sg, ALU.mult)
    vector.drain()
    shb = q.sh[:, 0:nblk].rearrange("p (n o) -> p n o", o=1).broadcast_to([128, nblk, 16])
    vector.tensor_tensor(out_ap.rearrange("p (n b) -> p n b", b=16), cc.rearrange("p (n b) -> p n b", b=16), shb, ALU.mult)
    vector.drain()



WG_ROWS = FFN_I // NCORES
WD_ROWS = D // NCORES


def build_l0f_wq(p=None, blk=None, io=None):
    fused = io is not None
    if not fused:
        p = Prog()
    else:
        p.push_scope()
        p.pfx = "q_"
    wg_in = p.inp("wg", [WG_ROWS, D])
    wu_in = p.inp("wu", [WG_ROWS, D])
    wd_in = p.inp("wd", [WD_ROWS, FFN_I])
    ident = io["ident"] if fused else p.inp("ident", [128, 128], BF16)
    wgT_out = io["wgstg"] if fused else p.out("wgT", [D, WG_ROWS], BF16)
    wuT_out = io["wustg"] if fused else p.out("wuT", [D, WG_ROWS], BF16)
    wdT_out = io["wdstg"] if fused else p.out("wdT", [FFN_I, WD_ROWS], BF16)

    FH = FFN_I // 2
    xt = p.sb("xt", [128, FH], F32)
    xq = p.sb("xq", [128, FH], BF16)
    q = QuantScratch(p, "q", FH, FH // 16)
    id_sb = p.sb("id_sb", [128, 128], BF16)
    outT = p.sb("outT", [128, 128 * 11], BF16)
    pT = [p.ps("pT0", [128, 128], BF16)] * 2

    s_ld = p.sem("s_ld")
    s_v1 = p.sem("s_v1")
    s_s1 = p.sem("s_s1")
    s_q = p.sem("s_q")
    s_tr = p.sem("s_tr")
    s_te = p.sem("s_te")
    s_st = p.sem("s_st")

    tiles = []
    for wsrc, wout in ((wg_in, wgT_out), (wu_in, wuT_out)):
        for i, rows in enumerate((128, 128, 96)):
            tiles.append((wsrc, i * 128, rows, D, 0, wout))
    tiles.append((wd_in, 0, 128, FH, 0, wdT_out))
    tiles.append((wd_in, 0, 128, FH, FH, wdT_out))

    s_init = p.sem("s_init")
    s_ldi = p.sem("s_ldi")

    if not fused:
        blk = p.block()

    @blk.sync
    def _(sync):
        sync.wait_ge(s_init, 1)
        sync.dma_start(out=id_sb[:], in_=ident).then_inc(s_ldi, 16)
        for i, (src, r0, rows, F, c0, _) in enumerate(tiles):
            if i >= 1:
                sync.wait_ge(s_q, i)
            sync.dma_start(out=xt[0:rows, 0:F], in_=src[r0:r0 + rows, c0:c0 + F]).then_inc(s_ld, 16)

    @blk.vector
    def _(vector):
        vector.memset(xt[:], 0.0)
        vector.drain()
        vector.engine_nop().then_inc(s_init, 1)
        for i, (src, r0, rows, F, c0, _) in enumerate(tiles):
            vector.wait_ge(s_ld, 16 * (i + 1))
            quant_v_pre(vector, q, xt[:, 0:F], F)
            vector.engine_nop().then_inc(s_v1, 1)
            vector.wait_ge(s_s1, i + 1)
            quant_v_chain(vector, q, xq[:, 0:F], F)
            vector.engine_nop().then_inc(s_q, 1)

    @blk.scalar
    def _(scalar):
        for i, (src, r0, rows, F, c0, wout) in enumerate(tiles):
            scalar.wait_ge(s_v1, i + 1)
            quant_s_mid(scalar, q, F).then_inc(s_s1, 1)
            ntr = F // 128
            nst = sum((t[3] // 128 + 10) // 11 for t in tiles[:i])
            if i >= 1:
                scalar.wait_ge(s_st, 16 * nst)
            for j in range(ntr):
                if j > 0 and j % 11 == 0:
                    scalar.wait_ge(s_st, 16 * (nst + j // 11))
                scalar.wait_ge(s_tr, sum(t[3] // 128 for t in tiles[:i]) + j + 1)
                scalar.copy(outT[0:128, (j % 11) * 128:(j % 11) * 128 + rows],
                            pT[j % 2][:, 0:rows]).then_inc(s_te, 1)

    @blk.tensor
    def _(tensor):
        tensor.wait_ge(s_ldi, 16)
        n = 0
        for i, (src, r0, rows, F, c0, wout) in enumerate(tiles):
            tensor.wait_ge(s_q, i + 1)
            ntr = F // 128
            for j in range(ntr):
                if n >= 1:
                    tensor.wait_ge(s_te, n)
                tensor.transpose(pT[0][:, 0:rows], xq[0:rows, j * 128:(j + 1) * 128],
                                 id_sb[0:rows, 0:rows]).then_inc(s_tr, 1)
                n += 1

    @blk.gpsimd
    def _(g):
        nst_total = 0
        for i, (src, r0, rows, F, c0, wout) in enumerate(tiles):
            ntr = F // 128
            base = sum(t[3] // 128 for t in tiles[:i])
            for g0 in range(0, ntr, 11):
                ng = min(11, ntr - g0)
                g.wait_ge(s_te, base + g0 + ng)
                g.dma_start(
                    out=wout[c0 + g0 * 128:c0 + (g0 + ng) * 128, r0:r0 + rows].rearrange("(j c) r -> c j r", c=128),
                    in_=outT[0:128, 0:ng * 128].rearrange("p (j f) -> p j f", f=128)[:, :, 0:rows],
                ).then_inc(s_st, 16)
                nst_total += 1
        g.wait_ge(s_st, 16 * nst_total)
        if fused:
            s_cc = io["s_cc"]
            g.collective_compute("AllGather", ALU.bypass,
                                 replica_groups=[list(range(NCORES))],
                                 ins=[io["wgstg"].opt()], outs=[io["wgg"].opt()]).then_inc(s_cc, 1)
            g.collective_compute("AllGather", ALU.bypass,
                                 replica_groups=[list(range(NCORES))],
                                 ins=[io["wustg"].opt()], outs=[io["wug"].opt()]).then_inc(s_cc, 1)
            g.collective_compute("AllGather", ALU.bypass,
                                 replica_groups=[list(range(NCORES))],
                                 ins=[io["wdstg"].opt()], outs=[io["wdg"].opt()]).then_inc(s_cc, 1)
            g.wait_ge(s_cc, 3)

    if fused:
        p.pop_scope()
        p.pfx = ""
    return p



def build_l0f(p=None, blk=None, io=None):
    fused = io is not None
    CHUNK = 512
    NT = CHUNK // 128
    NCH = SHARD // CHUNK
    ND = D // 128
    NI = FFN_I // 128
    IC_W = [512] * 5 + [256]
    NIC = len(IC_W)

    if not fused:
        p = Prog()
    else:
        p.push_scope()
        p.pfx = "f_"
    if fused:
        x_in = io["x2"]
        wgg, wug, wdg = io["wgg"], io["wug"], io["wdg"]
        x3_out = io["x3"]
    else:
        x_in = p.inp("x2", [SHARD, D])
        wgT_in = p.inp("wgT", [D, FFN_I], BF16)
        wuT_in = p.inp("wuT", [D, FFN_I], BF16)
        wdT_in = p.inp("wdT", [FFN_I, D], BF16)
        x3_out = p.out("x3", [SHARD, D])
    ident = io["ident"] if fused else p.inp("ident", [128, 128], BF16)

    xr = p.sb("xr", [128, NT * D])
    ln2 = p.sb("ln2", [128, NT * D], BF16)
    xq = p.sb("xq", [128, NT * D], BF16)
    xqT = p.sb("xqT", [128, ND * CHUNK], BF16)
    h = p.sb("h", [128, NT * FFN_I], BF16)
    hqT = p.sb("hqT", [128, NI * CHUNK], BF16)
    wgS = [p.sb(f"wgS{i}", [128, ND * 512], BF16) for i in range(2)]
    wuS = [p.sb(f"wuS{i}", [128, ND * 512], BF16) for i in range(2)]
    wdS = p.sb("wdS", [128, NI * D], BF16)
    id_sb = p.sb("id_sb", [128, 128], BF16)
    tsil = p.sb("tsil", [128, 512], F32)
    m_t = p.sb("m_t", [128, NT], F32)
    t1_t = p.sb("t1_t", [128, NT], F32)
    sq_t = p.sb("sq_t", [128, NT], F32)
    var_t = p.sb("var_t", [128, NT], F32)
    rstd_t = p.sb("rstd_t", [128, NT], F32)
    scr = p.sb("scr", [128, D], BF16)

    class QS:
        pass
    _bm = p.sb("q_bm", [128, FFN_I // 16], F32)
    _r = p.sb("q_r", [128, FFN_I // 16], F32)
    _a = p.sb("q_a", [128, FFN_I], BF16)
    _b = p.sb("q_b", [128, FFN_I], BF16)
    _c = p.sb("q_c", [128, FFN_I], BF16)
    _u = p.sb("q_u", [128, FFN_I], BF16)
    _sg = p.sb("q_sg", [128, FFN_I], BF16)
    qs = []
    for i in range(2):
        q_ = QS()
        q_.bm, q_.r, q_.a, q_.b, q_.c = _bm, _r, _a, _b, _c
        q_.u, q_.sg = _u, _sg
        q_.sh = p.sb(f"q_sh{i}", [128, FFN_I // 16], F32)
        q_.xr = p.sb(f"q_xr{i}", [128, FFN_I], BF16)
        qs.append(q_)

    pG = [p.ps(f"pG{i}", [128, 512], F32) for i in range(2)]
    pU = [p.ps(f"pU{i}", [128, 512], F32) for i in range(2)]
    pDD = [p.ps(f"pDD{i}", [128, 512], F32) for i in range(2)]
    pT = [p.ps(f"pT{i}", [128, 128], BF16) for i in range(2)]

    s_x = p.sem("s_x")
    s_wg = [p.sem("s_wg0"), p.sem("s_wg1")]
    s_wu = [p.sem("s_wu0"), p.sem("s_wu1")]
    s_wd = p.sem("s_wd")
    s_sq = p.sem("s_sq")
    s_var = p.sem("s_var")
    s_rstd = p.sem("s_rstd")
    s_ln = p.sem("s_ln")
    s_qv1 = p.sem("s_qv1")
    s_qs1 = p.sem("s_qs1")
    s_qc = p.sem("s_qc")
    s_txq = p.sem("s_txq")
    s_txe = p.sem("s_txe")
    s_gu = p.sem("s_gu")
    s_sil = p.sem("s_sil")
    s_hm = p.sem("s_hm")
    s_dmm = p.sem("s_dmm")
    s_res = p.sem("s_res")
    s_st = p.sem("s_st")

    NTR_X = NT * ND
    NTR_H = NT * NI
    NTR = NTR_X + NTR_H
    NGU = NIC * NT
    NDN = NT * 2

    if not fused:
        blk = p.block()

    def _pieces(off, w):
        out = []
        i = off
        while i < off + w:
            r = i // WG_ROWS
            c0 = i - r * WG_ROWS
            c1 = min(WG_ROWS, c0 + (off + w - i))
            out.append((r, c0, c1))
            i += c1 - c0
        return out

    @blk.sync
    def _(sync):
        for c in range(NCH):
            if c > 0:
                sync.wait_ge(s_res, c * NDN)
                sync.wait_ge(s_st, 16 * c * NT)
            sync.dma_start(
                out=xr[:].rearrange("p (tt d) -> p tt d", d=D),
                in_=x_in[c * CHUNK:(c + 1) * CHUNK, :].rearrange("(tt p) d -> p tt d", p=128),
            ).then_inc(s_x, 16)
            off = 0
            for ic in range(NIC):
                w = IC_W[ic]
                par = ic % 2
                g = c * NIC + ic
                if g >= 2:
                    sync.wait_ge(s_gu, (g - 1) * NT)
                if fused:
                    dst0 = 0
                    for (r, c0, c1) in _pieces(off, w):
                        pw_ = c1 - c0
                        sync.dma_start(
                            out=wgS[par][:].rearrange("p (kt f) -> p kt f", f=512)[:, :, dst0:dst0 + pw_],
                            in_=wgg[r * D:(r + 1) * D, c0:c1].rearrange("(kt p) f -> p kt f", p=128),
                        ).then_inc(s_wg[par], 16)
                        sync.dma_start(
                            out=wuS[par][:].rearrange("p (kt f) -> p kt f", f=512)[:, :, dst0:dst0 + pw_],
                            in_=wug[r * D:(r + 1) * D, c0:c1].rearrange("(kt p) f -> p kt f", p=128),
                        ).then_inc(s_wu[par], 16)
                        dst0 += pw_
                else:
                    sync.dma_start(
                        out=wgS[par][:].rearrange("p (kt f) -> p kt f", f=512)[:, :, 0:w],
                        in_=wgT_in[:, off:off + w].rearrange("(kt p) f -> p kt f", p=128),
                    ).then_inc(s_wg[par], 16)
                    sync.dma_start(
                        out=wuS[par][:].rearrange("p (kt f) -> p kt f", f=512)[:, :, 0:w],
                        in_=wuT_in[:, off:off + w].rearrange("(kt p) f -> p kt f", p=128),
                    ).then_inc(s_wu[par], 16)
                off += w

    @blk.gpsimd
    def _(g):
        g.dma_start(out=id_sb[:], in_=ident).then_inc(s_wd, 16)
        if fused:
            for r in range(NCORES):
                g.dma_start(
                    out=wdS[:].rearrange("p (kt d) -> p kt d", d=D)[:, :, r * 128:(r + 1) * 128],
                    in_=wdg[r * FFN_I:(r + 1) * FFN_I, :].rearrange("(kt p) c -> p kt c", p=128),
                ).then_inc(s_wd, 16)
        else:
            g.dma_start(out=wdS[:].rearrange("p (kt d) -> p kt d", d=D),
                        in_=wdT_in.rearrange("(kt p) d -> p kt d", p=128)).then_inc(s_wd, 16)
        for c in range(NCH):
            for mt in range(NT):
                g.wait_ge(s_res, c * NDN + (mt + 1) * 2)
                g.dma_start(
                    out=x3_out[c * CHUNK + mt * 128: c * CHUNK + (mt + 1) * 128, :],
                    in_=xr[:, mt * D:(mt + 1) * D],
                ).then_inc(s_st, 16)
        g.wait_ge(s_st, 16 * NCH * NT)

    @blk.vector
    def _(vector):
        for c in range(NCH):
            vector.wait_ge(s_x, 16 * (c + 1))
            for tt in range(NT):
                vector.reduce_sum(m_t[:, tt:tt + 1], xr[:, tt * D:(tt + 1) * D], AX)
            vector.drain()
            vector.tensor_scalar(m_t[:], m_t[:], 1.0 / D, None, ALU.mult)
            vector.drain()
            vector.tensor_tensor(t1_t[:], m_t[:], m_t[:], ALU.mult)
            vector.wait_ge(s_sq, c * NT + NT)
            vector.tensor_scalar(var_t[:], sq_t[:], 1.0 / D, None, ALU.mult)
            vector.drain()
            vector.tensor_tensor(var_t[:], var_t[:], t1_t[:], ALU.subtract)
            vector.drain()
            vector.tensor_scalar(var_t[:], var_t[:], LN_EPS, None, ALU.add)
            vector.drain()
            vector.engine_nop().then_inc(s_var, 1)
            vector.wait_ge(s_rstd, c + 1)
            for tt in range(NT):
                vector.tensor_scalar(
                    ln2[:, tt * D:(tt + 1) * D], xr[:, tt * D:(tt + 1) * D],
                    m_t[:, tt:tt + 1], rstd_t[:, tt:tt + 1], ALU.subtract, ALU.mult,
                ).then_inc(s_ln, 1)
            vector.drain()
            def v_q_pre(ts_, ap, F):
                if ts_ >= 2:
                    vector.wait_ge(s_qs1, ts_ - 1)
                quant_v_pre(vector, qs[ts_ % 2], ap, F)
                vector.engine_nop().then_inc(s_qv1, 1)

            def v_q_chain(ts_, ap, F):
                vector.wait_ge(s_qs1, ts_ + 1)
                quant_v_chain(vector, qs[ts_ % 2], ap, F)
                vector.engine_nop().then_inc(s_qc, 1)

            for tt in range(NT):
                ts_ = c * 8 + tt
                v_q_pre(ts_, ln2[:, tt * D:(tt + 1) * D], D)
                if tt >= 1:
                    v_q_chain(ts_ - 1, xq[:, (tt - 1) * D:tt * D], D)
            v_q_chain(c * 8 + NT - 1, xq[:, (NT - 1) * D:NT * D], D)
            for ic in range(NIC):
                w = IC_W[ic]
                ioff = sum(IC_W[:ic])
                for mt in range(NT):
                    slot = c * NGU + ic * NT + mt
                    vector.wait_ge(s_sil, slot + 1)
                    vector.tensor_tensor(
                        h[:, mt * FFN_I + ioff: mt * FFN_I + ioff + w],
                        tsil[:, 0:w], pU[slot % 2][:, 0:w], ALU.mult)
                    vector.drain()
                    vector.engine_nop().then_inc(s_hm, 1)
            for mt in range(NT):
                ts_ = c * 8 + NT + mt
                v_q_pre(ts_, h[:, mt * FFN_I:(mt + 1) * FFN_I], FFN_I)
                if mt >= 1:
                    v_q_chain(ts_ - 1, h[:, (mt - 1) * FFN_I:mt * FFN_I], FFN_I)
            v_q_chain(c * 8 + 2 * NT - 1, h[:, (NT - 1) * FFN_I:NT * FFN_I], FFN_I)
            for mt in range(NT):
                for nh in range(2):
                    slot = c * NDN + mt * 2 + nh
                    vector.wait_ge(s_dmm, slot + 1)
                    sl = slice(mt * D + nh * 512, mt * D + (nh + 1) * 512)
                    vector.tensor_tensor(xr[:, sl], xr[:, sl], pDD[slot % 2][:], ALU.add)
                    vector.drain()
                    vector.engine_nop().then_inc(s_res, 1)

    @blk.scalar
    def _(scalar):
        for c in range(NCH):
            scalar.wait_ge(s_x, 16 * (c + 1))
            for tt in range(NT):
                scalar.activation(scr[:], xr[:, tt * D:(tt + 1) * D], AF.Square,
                                  accum_out=sq_t[:, tt:tt + 1]).then_inc(s_sq, 1)
                scalar.drain()
            scalar.wait_ge(s_var, c + 1)
            scalar.activation(rstd_t[:], var_t[:], AF.Ln)
            scalar.drain()
            scalar.activation(rstd_t[:], rstd_t[:], AF.Exp, scale=-0.5).then_inc(s_rstd, 1)
            for tt in range(NT):
                ts_ = c * 8 + tt
                scalar.wait_ge(s_qv1, ts_ + 1)
                if ts_ >= 1:
                    scalar.wait_ge(s_qc, ts_)
                quant_s_mid(scalar, qs[ts_ % 2], D).then_inc(s_qs1, 1)
            for i in range(NTR_X):
                n = c * NTR + i
                scalar.wait_ge(s_txq, n + 1)
                tt, dt = divmod(i, ND)
                scalar.copy(xqT[:, dt * CHUNK + tt * 128: dt * CHUNK + (tt + 1) * 128],
                            pT[n % 2][:]).then_inc(s_txe, 1)
            for ic in range(NIC):
                w = IC_W[ic]
                for mt in range(NT):
                    slot = c * NGU + ic * NT + mt
                    scalar.wait_ge(s_gu, slot + 1)
                    if slot >= 1:
                        scalar.wait_ge(s_hm, slot)
                    scalar.activation(tsil[:, 0:w], pG[slot % 2][:, 0:w], AF.Silu).then_inc(s_sil, 1)
                    scalar.drain()
            for mt in range(NT):
                ts_ = c * 8 + NT + mt
                scalar.wait_ge(s_qv1, ts_ + 1)
                if ts_ >= 1:
                    scalar.wait_ge(s_qc, ts_)
                quant_s_mid(scalar, qs[ts_ % 2], FFN_I).then_inc(s_qs1, 1)
            for i in range(NTR_H):
                n = c * NTR + NTR_X + i
                scalar.wait_ge(s_txq, n + 1)
                mt, kt = divmod(i, NI)
                scalar.copy(hqT[:, kt * CHUNK + mt * 128: kt * CHUNK + (mt + 1) * 128],
                            pT[n % 2][:]).then_inc(s_txe, 1)

    @blk.tensor
    def _(tensor):
        tensor.wait_ge(s_wd, 16 * (1 + NCORES) if fused else 32)
        for c in range(NCH):
            for i in range(NTR_X):
                tt, dt = divmod(i, ND)
                n = c * NTR + i
                tensor.wait_ge(s_qc, c * 8 + tt + 1)
                if n >= 2:
                    tensor.wait_ge(s_txe, n - 1)
                tensor.transpose(pT[n % 2][:], xq[:, tt * D + dt * 128: tt * D + (dt + 1) * 128],
                                 id_sb[:]).then_inc(s_txq, 1)
            tensor.wait_ge(s_txe, c * NTR + NTR_X)
            for ic in range(NIC):
                w = IC_W[ic]
                par = ic % 2
                g = c * NIC + ic
                if fused:
                    npieces = 0
                    for gi in range(g + 1):
                        if gi % 2 == g % 2:
                            ici = gi % NIC
                            npieces += len(_pieces(sum(IC_W[:ici]), IC_W[ici]))
                    tensor.wait_ge(s_wg[par], 16 * npieces)
                    tensor.wait_ge(s_wu[par], 16 * npieces)
                else:
                    tensor.wait_ge(s_wg[par], 16 * (g // 2 + 1))
                    tensor.wait_ge(s_wu[par], 16 * (g // 2 + 1))
                for mt in range(NT):
                    slot = c * NGU + ic * NT + mt
                    if slot >= 2:
                        tensor.wait_ge(s_sil, slot - 1)
                        tensor.wait_ge(s_hm, slot - 1)
                    for kt in range(ND):
                        tensor.matmul(
                            pG[slot % 2][:, 0:w],
                            xqT[:, kt * CHUNK + mt * 128: kt * CHUNK + (mt + 1) * 128],
                            wgS[par][:, kt * 512: kt * 512 + w],
                            start=(kt == 0), stop=(kt == ND - 1))
                    for kt in range(ND):
                        mm = tensor.matmul(
                            pU[slot % 2][:, 0:w],
                            xqT[:, kt * CHUNK + mt * 128: kt * CHUNK + (mt + 1) * 128],
                            wuS[par][:, kt * 512: kt * 512 + w],
                            start=(kt == 0), stop=(kt == ND - 1))
                        if kt == ND - 1:
                            mm.then_inc(s_gu, 1)
            for i in range(NTR_H):
                mt, kt = divmod(i, NI)
                n = c * NTR + NTR_X + i
                tensor.wait_ge(s_qc, c * 8 + NT + mt + 1)
                tensor.wait_ge(s_txe, n - 1)
                tensor.transpose(pT[n % 2][:], h[:, mt * FFN_I + kt * 128: mt * FFN_I + (kt + 1) * 128],
                                 id_sb[:]).then_inc(s_txq, 1)
            tensor.wait_ge(s_txe, (c + 1) * NTR)
            for mt in range(NT):
                for nh in range(2):
                    slot = c * NDN + mt * 2 + nh
                    if slot >= 2:
                        tensor.wait_ge(s_res, slot - 1)
                    for kt in range(NI):
                        mm = tensor.matmul(
                            pDD[slot % 2][:],
                            hqT[:, kt * CHUNK + mt * 128: kt * CHUNK + (mt + 1) * 128],
                            wdS[:, kt * D + nh * 512: kt * D + nh * 512 + 512],
                            start=(kt == 0), stop=(kt == NI - 1))
                        if kt == NI - 1:
                            mm.then_inc(s_dmm, 1)

    if fused:
        p.pop_scope()
        p.pfx = ""
    return p


def run_l0f(inputs, x2, wq):
    p = build_l0f()
    ident = np.eye(128, dtype=BF)
    in_maps = []
    for k in range(NCORES):
        in_maps.append({
            "x2": np.ascontiguousarray(x2[k * SHARD:(k + 1) * SHARD]),
            "wgT": np.ascontiguousarray(wq["wgT"]), "wuT": np.ascontiguousarray(wq["wuT"]),
            "wdT": np.ascontiguousarray(wq["wdT"]), "ident": ident})
    res = run_bass_kernel_spmd(p.nc, in_maps, core_ids=list(range(NCORES)))
    return np.concatenate([np.asarray(r["x3"]) for r in res.results], axis=0)


def run_l0f_wq(inputs):
    p = build_l0f_wq()
    wg, wu, wd = (_f32(inputs[k]) for k in ("L0_ffn_w_gate", "L0_ffn_w_up", "L0_ffn_w_down"))
    ident = np.eye(128, dtype=BF)
    in_maps = []
    for k in range(NCORES):
        in_maps.append({
            "wg": wg[k * WG_ROWS:(k + 1) * WG_ROWS], "wu": wu[k * WG_ROWS:(k + 1) * WG_ROWS],
            "wd": wd[k * WD_ROWS:(k + 1) * WD_ROWS], "ident": ident})
    res = run_bass_kernel_spmd(p.nc, in_maps, core_ids=list(range(NCORES)))
    return {
        "wgT": np.concatenate([np.asarray(r["wgT"]) for r in res.results], axis=1),
        "wuT": np.concatenate([np.asarray(r["wuT"]) for r in res.results], axis=1),
        "wdT": np.concatenate([np.asarray(r["wdT"]) for r in res.results], axis=1),
    }



def build_c0(p=None, blk=None, io=None):
    fused = io is not None
    CHUNK = 512
    NT = CHUNK // 128
    NCH = SHARD // CHUNK
    ND = D // 128

    if not fused:
        p = Prog()
    else:
        p.push_scope()
        p.pfx = "c_"
    x_in = io["x3"] if fused else p.inp("x3", [SHARD, D])
    wih = p.inp("wihT", [D, 80])
    brow = p.inp("brow", [2, 80])
    ident = io["ident"] if fused else p.inp("ident", [128, 128], BF16)
    identf = io["identf"] if fused else p.inp("identf", [128, 128], F32)
    ones128 = io["ones128"] if fused else p.inp("ones128", [1, 128], BF16)
    ln3_out = io["ln3stg"] if fused else p.out("ln3", [SHARD, D], BF16)
    xgT_out = io["xgstg"] if fused else p.out("xgT", [80, SHARD], BF16)

    xr = p.sb("xr", [128, NT * D])
    ln = p.sb("ln", [128, NT * D], BF16)
    lnT = p.sb("lnT", [128, ND * CHUNK], BF16)
    wih_sb = p.sb("wih_sb", [128, ND * 80], F32)
    wih_bf = p.sb("wih_bf", [128, ND * 80], BF16)
    brow_sb = [p.sb("brow_sb0", [1, 80], F32), p.sb("brow_sb1", [1, 80], F32)]
    brow_bf = [p.sb("brow_bf0", [1, 80], BF16), p.sb("brow_bf1", [1, 80], BF16)]
    id_sb = p.sb("id_sb", [128, 128], BF16)
    idf_sb = p.sb("idf_sb", [128, 128], F32)
    ones_sb = p.sb("ones_sb", [1, 128], BF16)
    xgtm = p.sb("xgtm", [128, NT * 80], F32)
    xgT = p.sb("xgT_sb", [80, SHARD], BF16)
    m_t = p.sb("m_t", [128, NT], F32)
    t1_t = p.sb("t1_t", [128, NT], F32)
    sq_t = p.sb("sq_t", [128, NT], F32)
    var_t = p.sb("var_t", [128, NT], F32)
    rstd_t = p.sb("rstd_t", [128, NT], F32)
    scr = p.sb("scr", [128, D], BF16)

    pT = [p.ps(f"pT{i}", [128, 128], BF16) for i in range(2)]
    pXG = [p.ps(f"pXG{i}", [128, 80], F32) for i in range(2)]
    pXT = [p.ps(f"pXT{i}", [80, 128], F32) for i in range(2)]

    s_x = p.sem("s_x")
    s_w = p.sem("s_w")
    s_wc = p.sem("s_wc")
    s_sq = p.sem("s_sq")
    s_var = p.sem("s_var")
    s_rstd = p.sem("s_rstd")
    s_ln = p.sem("s_ln")
    s_lnT = p.sem("s_lnT")
    s_lnTe = p.sem("s_lnTe")
    s_xg = p.sem("s_xg")
    s_xge = p.sem("s_xge")
    s_xt = p.sem("s_xt")
    s_xte = p.sem("s_xte")
    s_st = p.sem("s_st")

    if not fused:
        blk = p.block()

    @blk.sync
    def _(sync):
        sync.dma_start(out=wih_sb[:].rearrange("p (kt f) -> p kt f", f=80),
                       in_=wih.rearrange("(kt p) f -> p kt f", p=128)).then_inc(s_w, 16)
        sync.dma_start(out=brow_sb[0][:], in_=brow[0:1, :]).then_inc(s_w, 16)
        sync.dma_start(out=brow_sb[1][:], in_=brow[1:2, :]).then_inc(s_w, 16)
        sync.dma_start(out=id_sb[:], in_=ident).then_inc(s_w, 16)
        sync.dma_start(out=idf_sb[:], in_=identf).then_inc(s_w, 16)
        sync.dma_start(out=ones_sb[:], in_=ones128).then_inc(s_w, 16)
        for c in range(NCH):
            if c > 0:
                sync.wait_ge(s_xge, c * NT)
            sync.dma_start(
                out=xr[:].rearrange("p (tt d) -> p tt d", d=D),
                in_=x_in[c * CHUNK:(c + 1) * CHUNK, :].rearrange("(tt p) d -> p tt d", p=128),
            ).then_inc(s_x, 16)

    @blk.gpsimd
    def _(g):
        for c in range(NCH):
            for tt in range(NT):
                g.wait_ge(s_ln, c * NT + tt + 1)
                g.dma_start(
                    out=ln3_out[c * CHUNK + tt * 128: c * CHUNK + (tt + 1) * 128, :],
                    in_=ln[:, tt * D:(tt + 1) * D],
                ).then_inc(s_st, 16)
        g.wait_ge(s_xte, NCH * NT)
        g.dma_start(out=xgT_out, in_=xgT[:]).then_inc(s_st, 16)
        g.wait_ge(s_st, 16 * (NCH * NT + 1))
        if fused:
            s_cc = io["s_cc"]
            g.collective_compute("AllGather", ALU.bypass,
                                 replica_groups=[list(range(NCORES))],
                                 ins=[io["xgstg"].opt()], outs=[io["xgg"].opt()]).then_inc(s_cc, 1)
            g.collective_compute("AllGather", ALU.bypass,
                                 replica_groups=[list(range(NCORES))],
                                 ins=[io["ln3stg"].opt()], outs=[io["ln3g"].opt()]).then_inc(s_cc, 1)
            g.collective_compute("AllGather", ALU.bypass,
                                 replica_groups=[list(range(NCORES))],
                                 ins=[io["x3"].opt()], outs=[io["x3g"].opt()]).then_inc(s_cc, 1)

    @blk.vector
    def _(vector):
        vector.wait_ge(s_w, 16 * 6)
        vector.tensor_copy(wih_bf[:], wih_sb[:])
        vector.tensor_copy(brow_bf[0][:], brow_sb[0][:])
        vector.tensor_copy(brow_bf[1][:], brow_sb[1][:])
        vector.drain()
        vector.engine_nop().then_inc(s_wc, 1)
        for c in range(NCH):
            vector.wait_ge(s_x, 16 * (c + 1))
            for tt in range(NT):
                vector.reduce_sum(m_t[:, tt:tt + 1], xr[:, tt * D:(tt + 1) * D], AX)
            vector.drain()
            vector.tensor_scalar(m_t[:], m_t[:], 1.0 / D, None, ALU.mult)
            vector.drain()
            vector.tensor_tensor(t1_t[:], m_t[:], m_t[:], ALU.mult)
            vector.wait_ge(s_sq, c * NT + NT)
            vector.tensor_scalar(var_t[:], sq_t[:], 1.0 / D, None, ALU.mult)
            vector.drain()
            vector.tensor_tensor(var_t[:], var_t[:], t1_t[:], ALU.subtract)
            vector.drain()
            vector.tensor_scalar(var_t[:], var_t[:], LN_EPS, None, ALU.add)
            vector.drain()
            vector.engine_nop().then_inc(s_var, 1)
            vector.wait_ge(s_rstd, c + 1)
            if c > 0:
                vector.wait_ge(s_st, 16 * c * NT)
            for tt in range(NT):
                vector.tensor_scalar(
                    ln[:, tt * D:(tt + 1) * D], xr[:, tt * D:(tt + 1) * D],
                    m_t[:, tt:tt + 1], rstd_t[:, tt:tt + 1], ALU.subtract, ALU.mult,
                ).then_inc(s_ln, 1)
            vector.drain()
            for mt in range(NT):
                slot = c * NT + mt
                vector.wait_ge(s_xg, slot + 1)
                par = slot % 2
                vector.tensor_copy(xgtm[:, mt * 80:mt * 80 + 20], pXG[par][:, 0:20])
                vector.tensor_scalar(xgtm[:, mt * 80 + 20:mt * 80 + 80], pXG[par][:, 20:80],
                                     0.5, None, ALU.mult)
                vector.drain()
                vector.engine_nop().then_inc(s_xge, 1)

    @blk.scalar
    def _(scalar):
        for c in range(NCH):
            scalar.wait_ge(s_x, 16 * (c + 1))
            for tt in range(NT):
                scalar.activation(scr[:], xr[:, tt * D:(tt + 1) * D], AF.Square,
                                  accum_out=sq_t[:, tt:tt + 1]).then_inc(s_sq, 1)
                scalar.drain()
            scalar.wait_ge(s_var, c + 1)
            scalar.activation(rstd_t[:], var_t[:], AF.Ln)
            scalar.drain()
            scalar.activation(rstd_t[:], rstd_t[:], AF.Exp, scale=-0.5).then_inc(s_rstd, 1)
            for i in range(NT * ND):
                scalar.wait_ge(s_lnT, c * NT * ND + i + 1)
                tt, dt = divmod(i, ND)
                scalar.copy(lnT[:, dt * CHUNK + tt * 128: dt * CHUNK + (tt + 1) * 128],
                            pT[i % 2][:]).then_inc(s_lnTe, 1)
            for mt in range(NT):
                slot = c * NT + mt
                scalar.wait_ge(s_xt, slot + 1)
                scalar.copy(xgT[:, (c * NT + mt) * 128:(c * NT + mt + 1) * 128],
                            pXT[slot % 2][:]).then_inc(s_xte, 1)

    @blk.tensor
    def _(tensor):
        tensor.wait_ge(s_w, 16 * 6)
        tensor.wait_ge(s_wc, 1)
        for c in range(NCH):
            for i in range(NT * ND):
                tt, dt = divmod(i, ND)
                tensor.wait_ge(s_ln, c * NT + tt + 1)
                g = c * NT * ND + i
                if g >= 2:
                    tensor.wait_ge(s_lnTe, g - 1)
                tensor.transpose(pT[i % 2][:], ln[:, tt * D + dt * 128: tt * D + (dt + 1) * 128],
                                 id_sb[:]).then_inc(s_lnT, 1)
            tensor.wait_ge(s_lnTe, (c + 1) * NT * ND)
            for mt in range(NT):
                slot = c * NT + mt
                if slot >= 2:
                    tensor.wait_ge(s_xge, slot - 1)
                par = slot % 2
                for kt in range(ND):
                    tensor.matmul(
                        pXG[par][:],
                        lnT[:, kt * CHUNK + mt * 128: kt * CHUNK + (mt + 1) * 128],
                        wih_bf[:, kt * 80:(kt + 1) * 80],
                        start=(kt == 0), stop=False)
                tensor.matmul(pXG[par][:], ones_sb[:], brow_bf[0][:], start=False, stop=False)
                tensor.matmul(pXG[par][:], ones_sb[:], brow_bf[1][:],
                              start=False, stop=True).then_inc(s_xg, 1)
            for mt in range(NT):
                slot = c * NT + mt
                tensor.wait_ge(s_xge, slot + 1)
                if slot >= 2:
                    tensor.wait_ge(s_xte, slot - 1)
                tensor.transpose(pXT[slot % 2][:], xgtm[:, mt * 80:(mt + 1) * 80],
                                 idf_sb[:]).then_inc(s_xt, 1)

    if fused:
        p.pop_scope()
        p.pfx = ""
    return p



LST_T = T
NTILE = T // 128


def build_c1(p=None, blk=None, io=None):
    fused = io is not None
    if not fused:
        p = Prog()
    else:
        p.push_scope()
        p.pfx = "m_"
    if not fused:
        xg_mega = p.inp("xg_mega", [80, 4 * (LST_T + 1)], BF16)
    lwxh = p.inp("lwx", [20, 80])
    lwxx = p.inp("lwxx", [80, 80])
    wms = p.inp("wms", [20, 36])
    bms = p.inp("bms", [1, 36])
    invj = p.inp("invj", [32, 128])
    tp1sT = p.inp("tp1sT", [128, 32])
    identf = io["identf"] if fused else p.inp("identf", [128, 128], F32)
    q1_out = io["q1s"] if fused else p.out("q1", [4, LST_T])
    q3_out = io["q3s"] if fused else p.out("q3", [4, LST_T])
    if fused:
        sel_in = p.inp("sel", [32, 16])
        q16s_out = io["q16s"]
    if fused:
        col = lambda t, b: b * (LST_T + 1) + t
    else:
        col = lambda t, b: 4 * t + b
    RHS_COLS = 4 * (LST_T + 1)

    rhs = p.sb("rhs", [20, 4 * (LST_T + 1)], BF16)
    rhx = p.sb("rhx", [80, 4 * (LST_T + 1)], BF16)
    lwx_sb = p.sb("lwx_sb", [20, 80], F32)
    lwxx_sb = p.sb("lwxx_sb", [80, 80], F32)
    lwbh = p.sb("lwbh", [20, 80], BF16)
    lwbx = p.sb("lwbx", [80, 80], BF16)
    wms_sb = p.sb("wms_sb", [20, 36], F32)
    wmsb = p.sb("wmsb", [20, 36], BF16)
    bms_sb = p.sb("bms_sb", [1, 36], F32)
    bmsb = p.sb("bmsb", [1, 36], BF16)
    invj_sb = p.sb("invj_sb", [32, 128], F32)
    tp1_sb = p.sb("tp1_sb", [128, 32], F32)
    idf_sb = p.sb("idf_sb", [128, 128], F32)
    ones512 = p.sb("ones512", [1, 512], BF16)
    tau0 = p.sb("tau0", [20, 16], F32)
    tau1 = p.sb("tau1", [20, 16], F32)
    tau2 = p.sb("tau2", [20, 16], F32)
    tau3 = p.sb("tau3", [20, 16], F32)
    u_t = p.sb("u_t", [20, 4], F32)
    v_t = p.sb("v_t", [20, 4], F32)
    s2 = p.sb("s2", [20, 4], F32)
    th = p.sb("th", [20, 4], F32)
    pj_sb = p.sb("pj_sb", [36, 512], F32)
    grids = [p.sb(f"grid{i}", [128, 32 * 36], F32) for i in range(4)]
    t2g = p.sb("t2g", [128, 32], F32)
    bbg = p.sb("bbg", [128, 32], F32)
    aS = p.sb("aS", [32, 128], F32)
    bbS = p.sb("bbS", [32, 128], F32)
    sigS = p.sb("sigS", [32, 128], F32)
    muL = p.sb("muL", [32, 128], F32)
    PS = p.sb("PS", [32, 128], F32)
    ABt0 = p.sb("ABt0", [1, 32], F32)
    ABt1 = p.sb("ABt1", [1, 32], F32)
    mue = p.sb("mue", [1, 32], F32)
    muin_r = p.sb("muin_r", [1, 32], F32)
    muin = p.sb("muin", [32, 1], F32)
    kS = p.sb("kS", [32, 128], F32)
    q1S = p.sb("q1S", [32, 128], F32)
    q3S = p.sb("q3S", [32, 128], F32)
    if fused:
        sel_sb = p.sb("sel_sb", [32, 16], F32)
        q16sb = p.sb("q16sb", [128, 32], F32)
        pQ = p.ps("pQ", [128, 8], F32)
        s_q16 = p.sem("s_q16")
        wstg = [p.sb(f"wstg{i}", [128, D], F32) for i in range(2)]
        wstgb = [p.sb(f"wstgb{i}", [128, D], BF16) for i in range(2)]
        s_ws = [p.sem(f"s_ws{i}") for i in range(2)]
        s_wsc = p.sem("s_wsc")
        s_wst = [p.sem("s_wst0"), p.sem("s_wst1")]
        wstack, wscr, s_wpre = io["wstack"], io["wscr"], io["s_wpre"]

    ps0 = p.ps("ps0", [20, 16], F32)
    ps1 = p.ps("ps1", [20, 16], F32)
    pPJ = p.ps("pPJ", [36, 512], F32)
    pTR = p.ps("pTR", [128, 128], F32)

    s_ld = p.sem("s_ld")
    s_ldi = p.sem("s_ldi")
    s_wp = p.sem("s_wp")
    s_T = p.sem("s_T")
    s_S1 = p.sem("s_S1")
    s_V1 = p.sem("s_V1")
    s_S2 = p.sem("s_S2")
    s_V2 = p.sem("s_V2")
    s_pj = p.sem("s_pj")
    s_pje = p.sem("s_pje")
    s_tr = p.sem("s_tr")
    s_tre = p.sem("s_tre")
    s_gv = p.sem("s_gv")
    s_st = p.sem("s_st")

    taus = [tau0, tau1, tau2, tau3]
    pss = [ps0, ps1]
    NCHP = LST_T // 512

    ins_T_mm, ins_T_tr, ins_V_relu, ins_S_sig, ins_S_cp = {}, {}, {}, {}, {}
    for ch_ in range(7):
        for b_ in range(4):
            ci_ = ch_ * 4 + b_
            tb_ = (ch_ + 1) * 512 + 4 * b_
            ins_T_mm[tb_] = (b_, ch_, ci_)
            ins_V_relu[tb_ + 2] = ci_
            ins_S_sig[tb_ + 2] = ci_
            ins_S_cp.setdefault(tb_ + 3, []).extend([(b_, ch_, ci_, 0), (b_, ch_, ci_, 1)])
            ins_S_cp.setdefault(tb_ + 4, []).extend([(b_, ch_, ci_, 2), (b_, ch_, ci_, 3)])
            ins_T_tr.setdefault(tb_ + 3, []).extend([(b_, ch_, ci_, 0), (b_, ch_, ci_, 1)])
            ins_T_tr.setdefault(tb_ + 4, []).extend([(b_, ch_, ci_, 2), (b_, ch_, ci_, 3)])

    def proj_rv(b, ch):
        if fused:
            return rhs[0:20, b * (LST_T + 1) + ch * 512 + 1: b * (LST_T + 1) + (ch + 1) * 512 + 1]
        return rhs[0:20, :].rearrange("p (s c) -> p s c", c=4)[:, ch * 512 + 1:(ch + 1) * 512 + 1, b]

    if not fused:
        blk = p.block()

    @blk.sync
    def _(sync):
        sync.dma_start(out=lwx_sb[:], in_=lwxh).then_inc(s_ld, 16)
        sync.dma_start(out=lwxx_sb[:], in_=lwxx).then_inc(s_ld, 16)
        sync.dma_start(out=wms_sb[:], in_=wms).then_inc(s_ld, 16)
        sync.dma_start(out=bms_sb[:], in_=bms).then_inc(s_ld, 16)
        sync.dma_start(out=invj_sb[:], in_=invj).then_inc(s_ld, 16)
        sync.dma_start(out=tp1_sb[:], in_=tp1sT).then_inc(s_ld, 16)
        sync.dma_start(out=idf_sb[:], in_=identf).then_inc(s_ld, 16)
        if fused:
            sync.dma_start(out=sel_sb[:], in_=sel_in).then_inc(s_ld, 16)
            sync.wait_ge(io["s_cc"], 1)
            for k in range(NCORES):
                b, t0 = k // 2, (k % 2) * SHARD
                sync.dma_start(out=rhx[:, col(t0, b):col(t0, b) + SHARD],
                               in_=io["xgg"][k * 80:(k + 1) * 80, :]).then_inc(s_ldi, 16)
            for i in range(64):
                if i >= 2:
                    sync.wait_ge(s_wsc, i - 1)
                sync.dma_start(out=wstg[i % 2][:],
                               in_=wstack[i * 128:(i + 1) * 128, :]).then_inc(s_ws[i % 2], 16)
        else:
            sync.dma_start(out=rhx[:, 0:4 * LST_T], in_=xg_mega[:, 0:4 * LST_T]).then_inc(s_ld, 16)

    @blk.vector
    def _(vector):
        vector.wait_ge(s_ld, 16 * 8)
        vector.memset(rhs[0:20, :], 0.0)
        vector.memset(s2[:], 0.0)
        vector.memset(pj_sb[:], 0.0)
        vector.memset(ones512[:], 1.0)
        vector.tensor_scalar(lwx_sb[0:20, 0:20], lwx_sb[0:20, 0:20], 0.5, None, ALU.mult)
        vector.tensor_scalar(lwx_sb[0:20, 20:80], lwx_sb[0:20, 20:80], 0.25, None, ALU.mult)
        vector.drain()
        vector.tensor_copy(lwbh[:], lwx_sb[:])
        vector.tensor_copy(lwbx[:], lwxx_sb[:])
        vector.tensor_scalar(wms_sb[:], wms_sb[:], 0.5, None, ALU.mult)
        vector.drain()
        vector.tensor_copy(wmsb[:], wms_sb[:])
        vector.tensor_copy(bmsb[:], bms_sb[:])
        vector.drain()
        vector.engine_nop().then_inc(s_wp, 1)
        vector.engine_nop().then_inc(s_V2, 1)
        for t in range(LST_T):
            tau = taus[t % 4]
            vector.wait_ge(s_S1, t + 1)
            vector.scalar_tensor_tensor(u_t[:], tau[:, 4:8], 1.0, tau[:, 0:4], ALU.add, ALU.mult)
            vector.scalar_tensor_tensor(v_t[:], tau[:, 8:12], 1.0, s2[:], ALU.add, ALU.mult)
            vector.drain()
            vector.scalar_tensor_tensor(s2[:], v_t[:], 0.5, u_t[:], ALU.mult, ALU.add).then_inc(s_V1, 1)
            vector.wait_ge(s_S2, t + 1)
            if fused:
                _hd = rhs[0:20, :].rearrange("p (b tt) -> p b tt", b=4)[:, :, t + 1]
            else:
                _hd = rhs[0:20, 4 * (t + 1):4 * (t + 2)]
            vector.scalar_tensor_tensor(_hd,
                                        tau[:, 12:16], 1.0, th[:], ALU.add, ALU.mult).then_inc(s_V2, 1)
            vector.drain()
            if fused and t >= 128 and (t - 128) % 32 == 0 and (t - 128) // 32 < 64:
                i = (t - 128) // 32
                par = i % 2
                vector.wait_ge(s_ws[par], 16 * (i // 2 + 1))
                if i >= 2:
                    vector.wait_ge(s_wst[par], 16 * (i // 2))
                vector.tensor_copy(wstgb[par][:], wstg[par][:]).then_inc(s_wsc, 1)
            if t in ins_V_relu:
                ci = ins_V_relu[t]
                vector.wait_ge(s_pj, ci + 1)
                vector.tensor_relu(pj_sb[0:3, :], pPJ[0:3, :])
                vector.drain()
                vector.engine_nop().then_inc(s_pje, 1)
        for b in range(4):
            ci = 28 + b
            vector.wait_ge(s_pj, ci + 1)
            vector.tensor_relu(pj_sb[0:3, :], pPJ[0:3, :])
            vector.drain()
            vector.engine_nop().then_inc(s_pje, 1)
        for b in range(4):
            vector.wait_ge(s_tre, 116 + 4 * b)
            mu1g = grids[b][:].rearrange("p (c v) -> p c v", v=36)[:, :, 1]
            mu2g = grids[b][:].rearrange("p (c v) -> p c v", v=36)[:, :, 2]
            vector.tensor_tensor(t2g[:], mu2g, tp1_sb[:], ALU.mult)
            vector.drain()
            vector.scalar_tensor_tensor(bbg[:], mu1g, 1.0 / LST_T, t2g[:], ALU.mult, ALU.add)
            vector.drain()
            vector.engine_nop().then_inc(s_gv, 1)
            vector.wait_ge(s_tre, 128 + 6 * b + 3)
            vector.tensor_tensor_scan(muL[:], aS[:], bbS[:], 0.0, ALU.mult, ALU.add)
            vector.drain()
            vector.tensor_tensor_scan(PS[:], aS[:], aS[:], 1.0, ALU.mult, ALU.bypass)
            vector.drain()
            vector.engine_nop().then_inc(s_gv, 1)
            vector.wait_ge(s_tre, 128 + 6 * b + 5)
            vector.tensor_tensor_scan(mue[:], ABt0[:], ABt1[:], 0.0, ALU.mult, ALU.add)
            vector.drain()
            vector.memset(muin_r[:], 0.0)
            vector.drain()
            vector.tensor_copy(muin_r[0:1, 1:32], mue[0:1, 0:31])
            vector.drain()
            vector.engine_nop().then_inc(s_gv, 1)
            vector.wait_ge(s_tre, 128 + 6 * b + 6)
            vector.scalar_tensor_tensor(muL[:], PS[:], muin[:, 0:1], muL[:], ALU.mult, ALU.add)
            vector.drain()
            vector.reciprocal(kS[:], sigS[:])
            vector.drain()
            vector.tensor_scalar(kS[:], kS[:], 0.7071067811865476, None, ALU.mult)
            vector.drain()
            vector.wait_ge(s_st, 32 * b)
            if fused and b > 0:
                vector.wait_ge(s_q16, 2 * b)
            vector.tensor_tensor(q1S[:], kS[:], invj_sb[:], ALU.mult)
            vector.tensor_tensor(q3S[:], muL[:], kS[:], ALU.mult)
            vector.drain()
            vector.engine_nop().then_inc(s_gv, 1)
        if fused:
            vector.wait_ge(s_wst[0], 16 * 32)
            vector.wait_ge(s_wst[1], 16 * 32)
            vector.engine_nop().then_inc(s_wpre, 1)

    @blk.scalar
    def _(scalar):
        for t in range(LST_T):
            tau = taus[t % 4]
            ps = pss[t % 2]
            scalar.wait_ge(s_T, t + 1)
            if t >= 4:
                scalar.wait_ge(s_V2, t - 2)
            scalar.activation(tau[:], ps[:], AF.Tanh).then_inc(s_S1, 1)
            scalar.wait_ge(s_V1, t + 1)
            scalar.activation(th[:], s2[:], AF.Tanh, scale=0.5).then_inc(s_S2, 1)
            if t in ins_S_sig:
                ci = ins_S_sig[t]
                scalar.wait_ge(s_pj, ci + 1)
                scalar.activation(pj_sb[32:33, :], pPJ[32:33, :], AF.Sigmoid).then_inc(s_pje, 1)
            if t in ins_S_cp:
                for (b_, ch_, ci, sub) in ins_S_cp[t]:
                    n = ci * 4 + sub
                    scalar.wait_ge(s_tr, n + 1)
                    i_g = ch_ * 4 + sub
                    scalar.copy(grids[b_][:, i_g * 36:i_g * 36 + 36],
                                pTR[:, 0:36]).then_inc(s_tre, 1)
        for b in range(4):
            ci = 28 + b
            scalar.wait_ge(s_pj, ci + 1)
            scalar.activation(pj_sb[32:33, :], pPJ[32:33, :], AF.Sigmoid).then_inc(s_pje, 1)
            for sub in range(4):
                n = ci * 4 + sub
                scalar.wait_ge(s_tr, n + 1)
                i_g = 7 * 4 + sub
                scalar.copy(grids[b][:, i_g * 36:i_g * 36 + 36],
                            pTR[:, 0:36]).then_inc(s_tre, 1)
        for b in range(4):
            for i, dst in enumerate((aS, bbS, sigS)):
                scalar.wait_ge(s_tr, 128 + 6 * b + i + 1)
                scalar.copy(dst[:], pTR[0:32, :]).then_inc(s_tre, 1)
            scalar.wait_ge(s_tr, 128 + 6 * b + 4)
            scalar.copy(ABt0[:], pTR[0:1, 0:32]).then_inc(s_tre, 1)
            scalar.wait_ge(s_tr, 128 + 6 * b + 5)
            scalar.copy(ABt1[:], pTR[0:1, 0:32]).then_inc(s_tre, 1)
            scalar.wait_ge(s_tr, 128 + 6 * b + 6)
            scalar.copy(muin[:], pTR[0:32, 0:1]).then_inc(s_tre, 1)
            if fused:
                scalar.wait_ge(s_q16, 2 * b + 1)
                scalar.copy(q16sb[:, b * 4:(b + 1) * 4], pQ[:, 0:4])
                scalar.copy(q16sb[:, 16 + b * 4:16 + (b + 1) * 4], pQ[:, 4:8]).then_inc(s_q16, 1)

    @blk.tensor
    def _(tensor):
        tensor.wait_ge(s_ld, 16 * 8)
        if fused:
            tensor.wait_ge(s_ldi, 16 * NCORES)
        tensor.wait_ge(s_wp, 1)
        tensor.wait_ge(s_V2, 1)

        def rvh(t):
            if fused:
                return rhs[0:20, :].rearrange("p (b tt) -> p b tt", b=4)[:, :, t]
            return rhs[0:20, 4 * t:4 * (t + 1)]

        def rvx(t):
            if fused:
                return rhx[:, :].rearrange("p (b tt) -> p b tt", b=4)[:, :, t]
            return rhx[:, 4 * t:4 * (t + 1)]

        for G in range(4):
            tensor.matmul(pss[0][:, G * 4:(G + 1) * 4], lwbx[:, G * 20:(G + 1) * 20], rvx(0),
                          start=(G == 0), stop=False, skip_group_check=True)
        for t in range(LST_T):
            ps = pss[t % 2]
            tensor.wait_ge(s_V2, t + 1)
            for G in range(4):
                mm = tensor.matmul(ps[:, G * 4:(G + 1) * 4], lwbh[:, G * 20:(G + 1) * 20], rvh(t),
                                   start=False, stop=(G == 3), skip_group_check=True)
                if G == 3:
                    mm.then_inc(s_T, 1)
            if t + 1 < LST_T:
                if t >= 1:
                    tensor.wait_ge(s_S1, t)
                psn = pss[(t + 1) % 2]
                for G in range(4):
                    tensor.matmul(psn[:, G * 4:(G + 1) * 4], lwbx[:, G * 20:(G + 1) * 20], rvx(t + 1),
                                  start=(G == 0), stop=False, skip_group_check=True)
            if t in ins_T_mm:
                b_, ch_, ci = ins_T_mm[t]
                if ci >= 1:
                    tensor.wait_ge(s_pje, 2 * ci)
                tensor.matmul(pPJ[:], wmsb[:], proj_rv(b_, ch_), start=True, stop=False)
                tensor.matmul(pPJ[:], bmsb[:], ones512[:], start=False, stop=True).then_inc(s_pj, 1)
            if t in ins_T_tr:
                for (b_, ch_, ci, sub) in ins_T_tr[t]:
                    n = ci * 4 + sub
                    tensor.wait_ge(s_pje, 2 * ci + 2)
                    if n >= 1:
                        tensor.wait_ge(s_tre, n)
                    tensor.transpose(pTR[:, 0:36], pj_sb[:, sub * 128:(sub + 1) * 128],
                                     idf_sb[0:36, 0:36]).then_inc(s_tr, 1)
        tensor.wait_ge(s_V2, LST_T + 1)
        for b in range(4):
            ci = 28 + b
            tensor.wait_ge(s_pje, 2 * ci)
            tensor.matmul(pPJ[:], wmsb[:], proj_rv(b, 7), start=True, stop=False)
            tensor.matmul(pPJ[:], bmsb[:], ones512[:], start=False, stop=True).then_inc(s_pj, 1)
            tensor.wait_ge(s_pje, 2 * ci + 2)
            for sub in range(4):
                n = ci * 4 + sub
                tensor.wait_ge(s_tre, n)
                tensor.transpose(pTR[:, 0:36], pj_sb[:, sub * 128:(sub + 1) * 128],
                                 idf_sb[0:36, 0:36]).then_inc(s_tr, 1)
        for b in range(4):
            tensor.wait_ge(s_gv, 4 * b + 1)
            ag = grids[b][:].rearrange("p (c v) -> p c v", v=36)[:, :, 0]
            sgg = grids[b][:].rearrange("p (c v) -> p c v", v=36)[:, :, 32]
            for i, src in enumerate((ag, bbg[:], sgg)):
                n = 128 + 6 * b + i
                tensor.wait_ge(s_tre, n)
                tensor.transpose(pTR[0:32, :], src, idf_sb[:]).then_inc(s_tr, 1)
            tensor.wait_ge(s_gv, 4 * b + 2)
            tensor.wait_ge(s_tre, 128 + 6 * b + 3)
            tensor.transpose(pTR[0:1, 0:32], PS[:, 127:128], idf_sb[0:32, 0:32]).then_inc(s_tr, 1)
            tensor.wait_ge(s_tre, 128 + 6 * b + 4)
            tensor.transpose(pTR[0:1, 0:32], muL[:, 127:128], idf_sb[0:32, 0:32]).then_inc(s_tr, 1)
            tensor.wait_ge(s_gv, 4 * b + 3)
            tensor.wait_ge(s_tre, 128 + 6 * b + 5)
            tensor.transpose(pTR[0:32, 0:1], muin_r[:], idf_sb[0:1, 0:1]).then_inc(s_tr, 1)
            if fused:
                tensor.wait_ge(s_gv, 4 * b + 4)
                if b > 0:
                    tensor.wait_ge(s_q16, 2 * b)
                tensor.matmul(pQ[:, 0:4], q1S[:], sel_sb[:, b * 4:(b + 1) * 4],
                              start=True, stop=True, skip_group_check=True)
                tensor.matmul(pQ[:, 4:8], q3S[:], sel_sb[:, b * 4:(b + 1) * 4],
                              start=True, stop=True, skip_group_check=True).then_inc(s_q16, 1)

    @blk.gpsimd
    def _(g):
        if fused:
            for i in range(64):
                g.wait_ge(s_wsc, i + 1)
                g.dma_start(out=wscr[i * 128:(i + 1) * 128, :],
                            in_=wstgb[i % 2][:]).then_inc(s_wst[i % 2], 16)
        for b in range(4):
            g.wait_ge(s_gv, 4 * b + 4)
            g.dma_start(out=q1_out[b:b + 1, :].rearrange("o (pp f) -> (o pp) f", f=128),
                        in_=q1S[:]).then_inc(s_st, 16)
            g.dma_start(out=q3_out[b:b + 1, :].rearrange("o (pp f) -> (o pp) f", f=128),
                        in_=q3S[:]).then_inc(s_st, 16)
        if fused:
            g.wait_ge(s_q16, 8)
            g.dma_start(out=io["q16s"], in_=q16sb[:]).then_inc(s_st, 16)
            g.wait_ge(s_st, 16 * 9)
        else:
            g.wait_ge(s_st, 16 * 8)

    if fused:
        p.pop_scope()
        p.pfx = ""
    return p


PERM_GIFO = np.concatenate([np.arange(40, 60), np.arange(0, 20), np.arange(20, 40), np.arange(60, 80)])


def host_prep_c0(inputs):
    wih = _f32(inputs["L1_attn_W_ih"])[PERM_GIFO]
    return {
        "wihT": _f32(wih.T),
        "brow": _f32(np.stack([inputs["L1_attn_b_ih"][PERM_GIFO], inputs["L1_attn_b_hh"][PERM_GIFO]])),
        "ident": np.eye(128, dtype=BF),
        "identf": np.eye(128, dtype=np.float32),
        "ones128": np.ones((1, 128), dtype=BF),
    }


def run_c0(inputs, x3):
    p = build_c0()
    com = host_prep_c0(inputs)
    in_maps = []
    for k in range(NCORES):
        m = dict(com)
        m["x3"] = np.ascontiguousarray(x3[k * SHARD:(k + 1) * SHARD])
        in_maps.append(m)
    res = run_bass_kernel_spmd(p.nc, in_maps, core_ids=list(range(NCORES)))
    ln3 = np.concatenate([np.asarray(r["ln3"]) for r in res.results], axis=0)
    xgs = [np.asarray(r["xgT"]) for r in res.results]
    return ln3, xgs


def host_prep_c1(inputs, xgs):
    xg_mega = np.zeros((80, 4 * (T + 1)), dtype=BF)
    for k in range(NCORES):
        b, t0 = k // 2, (k % 2) * SHARD
        cols = 4 * (t0 + np.arange(SHARD)) + b
        xg_mega[:, cols] = xgs[k]
    W_hh = _f32(inputs["L1_attn_W_hh"])
    lwx = np.zeros((20, 80), dtype=np.float32)
    orig_off = {"i": 0, "f": 20, "g": 40, "o": 60}
    col_base = {"g": 0, "i": 20, "f": 40, "o": 60}
    for gname in ("g", "i", "f", "o"):
        for j in range(20):
            lwx[0:20, col_base[gname] + j] = W_hh[orig_off[gname] + j, :]
    lwxx = np.eye(80, dtype=np.float32)
    wms = np.zeros((20, 36), dtype=np.float32)
    wms[:, 0:3] = _f32(inputs["L1_attn_W_mu"]).T
    wms[:, 32] = _f32(inputs["L1_attn_W_sig"])[0]
    bms = np.zeros((1, 36), dtype=np.float32)
    bms[0, 0:3] = _f32(inputs["L1_attn_b_mu"])
    bms[0, 32] = float(inputs["L1_attn_b_sig"][0])
    pj = np.arange(32)[:, None] * 128 + np.arange(128)[None, :]
    invj = (1.0 / (pj + 1.0)).astype(np.float32)
    tc = np.arange(32)[None, :] * 128 + np.arange(128)[:, None]
    tp1sT = ((tc + 1.0) / T).astype(np.float32)
    return {
        "xg_mega": xg_mega, "lwx": lwx, "lwxx": lwxx, "wms": wms, "bms": bms,
        "invj": invj, "tp1sT": tp1sT, "identf": np.eye(128, dtype=np.float32),
    }


def run_c1(inputs, xgs):
    p = build_c1()
    com = host_prep_c1(inputs, xgs)
    in_maps = [dict(com) for _ in range(NCORES)]
    res = run_bass_kernel_spmd(p.nc, in_maps, core_ids=list(range(NCORES)))
    return np.asarray(res.results[0]["q1"]), np.asarray(res.results[0]["q3"])



NSB_PAT = (1, 2, 3, 4)
SB_OF_ACC = [0, 1, 1, 2, 2, 2, 3, 3, 3, 3]
NSLOT_B = 80
NSLOT = 4 * NSLOT_B


def build_c2(p=None, blk=None, io=None):
    fused = io is not None
    if not fused:
        p = Prog()
    else:
        p.push_scope()
        p.pfx = "g_"
    masks = p.inp("masks", [NSLOT, 128, 128], BF16)
    itc_in = p.inp("itc", [128, NSLOT])
    iota_in = p.inp("iota", [128, 128])
    if fused:
        xtab_in = p.inp("xtab", [1, 16], I32)
        ln3g, x3g = io["ln3g"], io["x3g"]
        q16s_d = io["q16s"]
        x4row_out = io["x4s"]
        ident = io["ident"]
        identf = io["identf"]
    else:
        ln3s = p.inp("ln3s", [NSLOT, 128, D], BF16)
        q1c_in = p.inp("q1c", [128, NSLOT])
        q3c_in = p.inp("q3c", [128, NSLOT])
        x3row_in = p.inp("x3row", [16, 128, D])
        ident = p.inp("ident", [128, 128], BF16)
        x4row_out = p.out("x4row", [16, 128, D])

    NLB = 6
    ln3b = [p.sb(f"ln3b{i}", [128, D], BF16) for i in range(NLB)]
    if fused:
        xtab = p.sb("xtab_sb", [1, 16], I32)
        q16 = p.sb("q16", [128, 32], F32)
        s_qd = p.sem("s_qd")

    mkb = [p.sb(f"mkb{i}", [128, 256], BF16) for i in range(2)]
    q1c = p.sb("q1c_sb", [128, NSLOT], F32)
    q3c = p.sb("q3c_sb", [128, NSLOT], F32)
    itc = p.sb("itc_sb", [128, NSLOT], F32)
    iota2 = p.sb("iota2_sb", [128, 256], F32)
    id_sb = p.sb("id_sb", [128, 128], BF16)
    q2col = p.sb("q2col", [128, 2], F32)
    argT = [p.sb(f"argT{i}", [128, 256], F32) for i in range(2)]
    sqV = p.sb("sqV", [128, 256], F32)
    wraw = [p.sb(f"wraw{i}", [128, 256], BF16) for i in range(2)]
    wm = [p.sb(f"wm{i}", [128, 256], BF16) for i in range(2)]
    wmT = [p.sb(f"wmT{i}", [128, 128], BF16) for i in range(2)]
    wsq_scr = p.sb("wsq_scr", [128, 128], BF16)
    ngrid = p.sb("ngrid", [128, NSLOT], F32)
    accb = p.sb("accb", [128, D], F32)
    x3b = [p.sb(f"x3b{i}", [128, D], F32) for i in range(2)]
    x4b = p.sb("x4b", [128, D], F32)
    n2 = p.sb("n2", [128, 1], F32)
    rn = p.sb("rn", [128, 1], F32)

    pW = [p.ps(f"pW{i}", [128, 128], BF16) for i in range(2)]
    pCtx = [p.ps(f"pCtx{i}", [128, D], F32) for i in range(2)]

    s_ld = p.sem("s_ld")
    s_lt = [p.sem(f"s_lt{i}") for i in range(NLB)]
    s_mk = [p.sem(f"s_mk{i}") for i in range(2)]
    s_x3 = [p.sem(f"s_x3{i}") for i in range(2)]
    s_q2 = p.sem("s_q2")
    s_ag = p.sem("s_ag")
    s_we = p.sem("s_we")
    s_wm = p.sem("s_wm")
    s_tw = p.sem("s_tw")
    s_twe = p.sem("s_twe")
    s_mm = p.sem("s_mm")
    s_acc = p.sem("s_acc")
    s_n2 = p.sem("s_n2")
    s_rn = p.sem("s_rn")
    s_fin = p.sem("s_fin")
    s_st = p.sem("s_st")

    slots = []
    for b in range(4):
        for sb_b in range(10):
            acc = SB_OF_ACC[sb_b]
            first_sb = (sb_b == 0 or SB_OF_ACC[sb_b - 1] != acc)
            last_sb = (sb_b == 9 or SB_OF_ACC[sb_b + 1] != acc)
            sb_local = sb_b - SB_OF_ACC.index(acc)
            for k in range(8):
                slots.append(dict(b=b, sb=b * 10 + sb_b, k=k, acc=acc,
                                  first=first_sb, last=last_sb,
                                  gidx=b * 4 + acc, it=sb_local * 8 + k))
    NGRP = 16

    def _ln3row(b, sl):
        trow = sl["it"] * 128
        kown = b * 2 + (0 if trow < SHARD else 1)
        return kown * SHARD + (trow - (kown % 2) * SHARD)

    if not fused:
        blk = p.block()

    @blk.sync
    def _(sync):
        if fused:
            sync.dma_start(out=itc[:], in_=itc_in).then_inc(s_ld, 16)
            sync.dma_start(out=iota2[:, 0:128], in_=iota_in).then_inc(s_ld, 16)
            sync.dma_start(out=iota2[:, 128:256], in_=iota_in).then_inc(s_ld, 16)
            sync.dma_start(out=id_sb[:], in_=ident).then_inc(s_ld, 16)
            sync.dma_start(out=q16[:], in_=q16s_d).then_inc(s_qd, 16)
            sync.wait_ge(io["s_cc"], 2)
            for s in range(0, NSLOT, 2):
                par = s % NLB
                if s >= NLB:
                    sync.wait_ge(s_mm, s - NLB + 1)
                sl = slots[s]
                row = _ln3row(sl["b"], sl)
                sync.dma_start(out=ln3b[par][:], in_=ln3g[row:row + 128, :]).then_inc(s_lt[par], 16)
        else:
            sync.dma_start(out=q1c[:], in_=q1c_in).then_inc(s_ld, 16)
            sync.dma_start(out=q3c[:], in_=q3c_in).then_inc(s_ld, 16)
            sync.dma_start(out=itc[:], in_=itc_in).then_inc(s_ld, 16)
            sync.dma_start(out=iota2[:, 0:128], in_=iota_in).then_inc(s_ld, 16)
            sync.dma_start(out=iota2[:, 128:256], in_=iota_in).then_inc(s_ld, 16)
            sync.dma_start(out=id_sb[:], in_=ident).then_inc(s_ld, 16)
            for s in range(0, NSLOT, 2):
                par = s % NLB
                if s >= NLB:
                    sync.wait_ge(s_mm, s - NLB + 1)
                sync.dma_start(out=ln3b[par][:], in_=ln3s[s]).then_inc(s_lt[par], 16)

    @blk.gpsimd
    def _(g):
        with contextlib.ExitStack() as ges:
            if fused:
                s_xt2 = io["s_xt2"]
                g.dma_start(out=xtab[:], in_=xtab_in).then_inc(s_xt2, 16)
                g.wait_ge(s_xt2, 16)
                g.wait_ge(io["s_cc"], 3)
                xofs = io["xofs_reg"]
            for s in range(NSLOT):
                sl = slots[s]
                if s % 2 == 0:
                    pp = s // 2
                    if pp >= 2:
                        g.wait_ge(s_wm, pp - 1)
                    g.dma_start(out=mkb[pp % 2][:].rearrange("p (t f) -> p t f", f=128),
                                in_=masks[s:s + 2].rearrange("t p f -> p t f")).then_inc(s_mk[pp % 2], 16)
                if s % 2 == 1:
                    lpar = s % NLB
                    if s >= NLB:
                        g.wait_ge(s_mm, s - NLB + 1)
                    if fused:
                        lrow = _ln3row(sl["b"], sl)
                        g.dma_start(out=ln3b[lpar][:],
                                    in_=io["ln3g"][lrow:lrow + 128, :]).then_inc(s_lt[lpar], 16)
                    else:
                        g.dma_start(out=ln3b[lpar][:], in_=ln3s[s]).then_inc(s_lt[lpar], 16)
                if sl["last"] and sl["k"] == 0:
                    gp = sl["gidx"]
                    xpar = gp % 2
                    if gp >= 2:
                        g.wait_ge(s_fin, gp - 1)
                    if fused:
                        g.reg_load(xofs, xtab[0:1, gp:gp + 1])
                        g.dma_start(out=x3b[xpar][:],
                                    in_=bass.AP(x3g.tensor, xofs, [[D, 128], [1, D]])).then_inc(s_x3[xpar], 16)
                    else:
                        g.dma_start(out=x3b[xpar][:], in_=x3row_in[gp]).then_inc(s_x3[xpar], 16)
                if sl["last"] and sl["k"] == 7:
                    gp = sl["gidx"]
                    g.wait_ge(s_fin, gp + 1)
                    g.dma_start(out=x4row_out[gp], in_=x4b[:]).then_inc(s_st, 16)
        g.wait_ge(s_st, 16 * NGRP)

    @blk.vector
    def _(vector):
        NPAIR = NSLOT // 2
        if fused:
            vector.wait_ge(s_ld, 16 * 4)
            vector.wait_ge(s_qd, 16)
        else:
            vector.wait_ge(s_ld, 16 * 6)

        def v_produce(pp):
            s0 = 2 * pp
            qcol = q16[:, slots[s0]["gidx"]:slots[s0]["gidx"] + 1] if fused else q1c[:, s0:s0 + 1]
            q3v = (q16[:, 16 + slots[s0]["gidx"]:17 + slots[s0]["gidx"]] if fused
                   else q3c[:, s0:s0 + 1])
            vector.scalar_tensor_tensor(
                q2col[:, 0:2].rearrange("p (n t) -> p n t", n=1),
                itc[:, s0:s0 + 2].rearrange("p (n t) -> p n t", n=1),
                qcol,
                q3v.rearrange("p (n o) -> p n o", o=1).broadcast_to([128, 1, 2]),
                ALU.mult, ALU.subtract)
            vector.drain()
            if pp >= 2:
                vector.wait_ge(s_q2, pp - 1)
            vector.scalar_tensor_tensor(
                argT[pp % 2][:].rearrange("p (t f) -> p t f", f=128),
                iota2[:].rearrange("p (t f) -> p t f", f=128),
                qcol,
                q2col[:, 0:2].rearrange("p (t o) -> p t o", o=1).broadcast_to([128, 2, 128]),
                ALU.mult, ALU.add).then_inc(s_ag, 1)
            vector.drain()

        v_produce(0)
        for pp in range(NPAIR):
            s0, s = 2 * pp, 2 * pp + 1
            sl = slots[s]
            if pp + 1 < NPAIR:
                v_produce(pp + 1)
            vector.wait_ge(s_we, pp + 1)
            vector.wait_ge(s_mk[pp % 2], 16 * (pp // 2 + 1))
            if pp >= 2:
                vector.wait_ge(s_tw, 2 * pp - 2)
            vector.tensor_tensor(wm[pp % 2][:], wraw[pp % 2][:], mkb[pp % 2][:], ALU.mult)
            vector.drain()
            vector.scalar_tensor_tensor(wsq_scr[:], wm[pp % 2][:, 0:128], 1.0, wm[pp % 2][:, 0:128],
                                        ALU.mult, ALU.mult,
                                        accum_out=ngrid[:, s0:s0 + 1])
            vector.drain()
            vector.scalar_tensor_tensor(wsq_scr[:], wm[pp % 2][:, 128:256], 1.0, wm[pp % 2][:, 128:256],
                                        ALU.mult, ALU.mult,
                                        accum_out=ngrid[:, s:s + 1]).then_inc(s_wm, 1)
            vector.drain()
            if sl["k"] == 7:
                sbi = sl["sb"]
                vector.wait_ge(s_mm, s + 1)
                for nh in range(2):
                    dst = accb[:, nh * 512:(nh + 1) * 512]
                    if sl["first"]:
                        vector.tensor_copy(dst, pCtx[sbi % 2][:, nh * 512:(nh + 1) * 512])
                    else:
                        vector.tensor_tensor(dst, dst, pCtx[sbi % 2][:, nh * 512:(nh + 1) * 512], ALU.add)
                vector.drain()
                vector.engine_nop().then_inc(s_acc, 2)
                if sl["last"]:
                    gp = sl["gidx"]
                    glen = 8 * sum(1 for i, a in enumerate(SB_OF_ACC) if a == sl["acc"])
                    vector.reduce_sum(n2[:], ngrid[:, s + 1 - glen:s + 1], AX)
                    vector.drain()
                    vector.engine_nop().then_inc(s_n2, 1)
                    vector.wait_ge(s_rn, gp + 1)
                    vector.tensor_scalar(rn[:], rn[:], 1e12, None, ALU.min)
                    vector.drain()
                    xpar = gp % 2
                    vector.wait_ge(s_x3[xpar], 16 * (gp // 2 + 1))
                    if gp > 0:
                        vector.wait_ge(s_st, 16 * gp)
                    for nh in range(2):
                        vector.scalar_tensor_tensor(
                            x4b[:, nh * 512:(nh + 1) * 512],
                            accb[:, nh * 512:(nh + 1) * 512], rn[:, 0:1],
                            x3b[xpar][:, nh * 512:(nh + 1) * 512], ALU.mult, ALU.add)
                    vector.drain()
                    vector.engine_nop().then_inc(s_fin, 1)

    @blk.scalar
    def _(scalar):
        def s_copy(m):
            scalar.wait_ge(s_tw, m + 1)
            if m >= 2:
                scalar.wait_ge(s_mm, m - 1)
            scalar.copy(wmT[m % 2][:], pW[m % 2][:]).then_inc(s_twe, 1)

        copied = set()
        for pp in range(NSLOT // 2):
            s0, s1 = 2 * pp, 2 * pp + 1
            sl1 = slots[s1]
            scalar.wait_ge(s_ag, pp + 1)
            if pp >= 2:
                scalar.wait_ge(s_wm, pp - 1)
            if pp >= 1:
                scalar.drain()
            scalar.activation(sqV[:], argT[pp % 2][:], AF.Square).then_inc(s_q2, 1)
            scalar.drain()
            scalar.activation(wraw[pp % 2][:], sqV[:], AF.Exp, scale=-1.0).then_inc(s_we, 1)
            for m in (s0 - 2, s0 - 1):
                if m >= 0 and m not in copied:
                    s_copy(m)
                    copied.add(m)
            if sl1["k"] == 7 and sl1["last"]:
                for m in (s0, s1):
                    if m not in copied:
                        s_copy(m)
                        copied.add(m)
                gp = sl1["gidx"]
                scalar.wait_ge(s_n2, gp + 1)
                scalar.activation(rn[:], n2[:], AF.Ln)
                scalar.drain()
                scalar.activation(rn[:], rn[:], AF.Exp, scale=-0.5).then_inc(s_rn, 1)

    @blk.tensor
    def _(tensor):
        if fused:
            tensor.wait_ge(s_ld, 16 * 4)
        else:
            tensor.wait_ge(s_ld, 16 * 6)
        for s in range(NSLOT):
            sl = slots[s]
            par = s % 2
            tensor.wait_ge(s_wm, s // 2 + 1)
            if s >= 2:
                tensor.wait_ge(s_twe, s - 1)
            tensor.transpose(pW[par][:],
                             wm[(s // 2) % 2][:, (s % 2) * 128:(s % 2) * 128 + 128],
                             id_sb[:]).then_inc(s_tw, 1)
            tensor.wait_ge(s_twe, s + 1)
            tensor.wait_ge(s_lt[s % NLB], 16 * (s // NLB + 1))
            sbi = sl["sb"]
            if sl["k"] == 0 and sbi >= 2:
                tensor.wait_ge(s_acc, (sbi - 1) * 2)
            for nh in range(2):
                mm = tensor.matmul(
                    pCtx[sbi % 2][:, nh * 512:(nh + 1) * 512],
                    wmT[par][:],
                    ln3b[s % NLB][:, nh * 512:(nh + 1) * 512],
                    start=(sl["k"] == 0), stop=(sl["k"] == 7))
                if nh == 1:
                    mm.then_inc(s_mm, 1)

    if fused:
        p.pop_scope()
        p.pfx = ""
    return p


def host_prep_c2(core, ln3_full, q1, q3, x3):
    jts = [core, 15 - core, 16 + core, 31 - core]
    lens = [core + 1, 16 - core, 17 + core, 32 - core]
    ln3s = np.zeros((NSLOT, 128, D), dtype=BF)
    masks = np.zeros((NSLOT, 128, 128), dtype=BF)
    q1c = np.zeros((128, NSLOT), dtype=np.float32)
    q3c = np.full((128, NSLOT), 30.0, dtype=np.float32)
    itc = np.zeros((128, NSLOT), dtype=np.float32)
    x3row = np.zeros((16, 128, D), dtype=np.float32)
    tril = np.tril(np.ones((128, 128), np.float32)).astype(BF)
    ones = np.ones((128, 128), dtype=BF)
    s = 0
    for b in range(4):
        for sb_b in range(10):
            acc = SB_OF_ACC[sb_b]
            sb_local = sb_b - SB_OF_ACC.index(acc)
            jt = jts[acc]
            for k in range(8):
                it = sb_local * 8 + k
                if it < lens[acc]:
                    ln3s[s] = ln3_full[b * T + it * 128: b * T + it * 128 + 128]
                    masks[s] = tril if it == jt else ones
                    q1c[:, s] = q1[b, jt * 128: (jt + 1) * 128]
                    q3c[:, s] = q3[b, jt * 128: (jt + 1) * 128]
                    itc[:, s] = 128.0 * it
                s += 1
    for b in range(4):
        for a in range(4):
            x3row[b * 4 + a] = x3[b * T + jts[a] * 128: b * T + (jts[a] + 1) * 128]
    return {"ln3s": ln3s, "masks": masks, "q1c": q1c, "q3c": q3c, "itc": itc,
            "x3row": x3row, "iota": np.tile(np.arange(128, dtype=np.float32), (128, 1)),
            "ident": np.eye(128, dtype=BF)}


def run_c2(inputs, ln3_full, q1, q3, x3):
    p = build_c2()
    in_maps = [host_prep_c2(k, ln3_full, q1, q3, x3) for k in range(NCORES)]
    res = run_bass_kernel_spmd(p.nc, in_maps, core_ids=list(range(NCORES)))
    x4 = np.array(x3, dtype=np.float32, copy=True).reshape(TOK, D)
    for k in range(NCORES):
        jts = [k, 15 - k, 16 + k, 31 - k]
        rows = np.asarray(res.results[k]["x4row"])
        for b in range(4):
            for a in range(4):
                x4[b * T + jts[a] * 128: b * T + (jts[a] + 1) * 128] = rows[b * 4 + a]
    return x4



def build_l1f(p=None, blk=None, io=None):
    NPT = 4
    POS = 512
    ND8 = D // 128

    fused = io is not None
    if not fused:
        p = Prog()
    else:
        p.push_scope()
        p.pfx = "z_"
    x4row_in = io["x4s"] if fused else p.inp("x4row", [16, 128, D])
    wstack = io["wstack"] if fused else p.inp("wstack", [8 * D, D])
    bias_c = p.inp("bias_c", [128, 3 * ND8])
    bo_r = p.inp("bo_r", [1, D])
    inc_c = p.inp("inc_c", [128, ND8])
    ident = io["ident"] if fused else p.inp("ident", [128, 128], BF16)
    ones128 = io["ones128"] if fused else p.inp("ones128", [1, 128], BF16)
    out_rows = io["outrows"] if fused else p.out("outrows", [16, 128, D])

    wscr = io["wscr"] if fused else p.nc.dram_tensor(p.pfx + "wscr", [8 * D, D], BF16, kind="Internal").ap()

    xr = p.sb("xr", [128, 4 * D])
    ln4 = p.sb("ln4", [128, 4 * D], BF16)
    ln4T = p.sb("ln4T", [128, 4 * ND8 * POS], BF16)
    cst = p.sb("cst", [128, ND8 * POS])
    cbf = p.sb("cbf", [128, ND8 * POS], BF16)
    gf = p.sb("gf", [128, POS], BF16)
    gi = p.sb("gi", [128, POS], BF16)
    gg = p.sb("gg", [128, POS], BF16)
    t1g = p.sb("t1g", [128, POS], F32)
    if not fused:
        wstg = [p.sb(f"wstg{i}", [128, D], F32) for i in range(2)]
        wstgb = [p.sb(f"wstgb{i}", [128, D], BF16) for i in range(2)]
    wsl = [p.sb(f"wsl{i}", [128, ND8 * 128], BF16) for i in range(6)]
    wfin = [p.sb(f"wfin{i}", [128, ND8 * 512], BF16) for i in range(4)]
    bias_sb = p.sb("bias_sb", [128, 3 * ND8], F32)
    bo_sb = p.sb("bo_sb", [1, D], F32)
    bo_bf = p.sb("bo_bf", [1, D], BF16)
    inc_sb = p.sb("inc_sb", [128, ND8], F32)
    ones_t = p.sb("ones_t", [128, POS], F32)
    id_sb = p.sb("id_sb", [128, 128], BF16)
    ones_sb = p.sb("ones_sb", [1, 128], BF16)
    m_t = p.sb("m_t", [128, 4], F32)
    t1_t = p.sb("t1_t", [128, 4], F32)
    sq_t = p.sb("sq_t", [128, 4], F32)
    var_t = p.sb("var_t", [128, 4], F32)
    rstd_t = p.sb("rstd_t", [128, 4], F32)
    scr = p.sb("scr", [128, D], BF16)
    xout = p.sb("xout", [128, D], F32)

    pF = p.ps("pF", [128, POS], F32)
    pI = p.ps("pI", [128, POS], F32)
    pG = p.ps("pG", [128, POS], F32)
    pT = [p.ps(f"pT{i}", [128, 128], BF16) for i in range(2)]
    pH = [p.ps(f"pH{i}", [128, 512], F32) for i in range(2)]

    s_w = p.sem("s_w")
    if not fused:
        s_ws = [p.sem(f"s_ws{i}") for i in range(2)]
        s_wsc = p.sem("s_wsc")
        s_wst = [p.sem("s_wst0"), p.sem("s_wst1")]
    s_x = p.sem("s_x")
    s_xo = p.sem("s_xo")
    s_sq = p.sem("s_sq")
    s_var = p.sem("s_var")
    s_rstd = p.sem("s_rstd")
    s_ln = p.sem("s_ln")
    s_lnT = p.sem("s_lnT")
    s_lnTe = p.sem("s_lnTe")
    s_ci = p.sem("s_ci")
    s_sl = [p.sem(f"s_sl{i}") for i in range(6)]
    s_gm = p.sem("s_gm")
    s_ge = p.sem("s_ge")
    s_cu = p.sem("s_cu")
    s_cb = p.sem("s_cb")
    s_fw = [p.sem(f"s_fw{i}") for i in range(2)]
    s_wc2 = p.sem("s_wc2")
    s_hm = p.sem("s_hm")
    s_he = p.sem("s_he")
    s_fin = p.sem("s_fin")
    s_st = p.sem("s_st")

    if not fused:
        blk = p.block()

    @blk.sync
    def _(sync):
        if fused:
            sync.wait_ge(io["s_wpre"], 1)
        else:
            for i in range(64):
                par = i % 2
                if i >= 2:
                    sync.wait_ge(s_wsc, i - 1)
                sync.dma_start(out=wstg[par][:], in_=wstack[i * 128:(i + 1) * 128, :]).then_inc(s_ws[par], 16)
        cnt = [0] * 6
        for bstep in range(4):
            for ft in range(8):
                for j in range(6):
                    buf = j
                    g = (bstep * 8 + ft) * 6 + j
                    if g >= 6:
                        sync.wait_ge(s_gm, (bstep * 8 + ft))
                    sync.dma_start(
                        out=wsl[buf][:].rearrange("p (kt f) -> p kt f", f=128),
                        in_=wscr[j * D:(j + 1) * D, ft * 128:(ft + 1) * 128]
                        .rearrange("(kt p) f -> p kt f", p=128),
                    ).then_inc(s_sl[buf], 16)
        if not fused:
            sync.wait_ge(s_wst[0], 16 * 32)
            sync.wait_ge(s_wst[1], 16 * 32)
        for nh in range(2):
            sync.dma_start(
                out=wfin[nh][:].rearrange("p (kt f) -> p kt f", f=512),
                in_=wscr[6 * D:7 * D, nh * 512:(nh + 1) * 512]
                .rearrange("(kt p) f -> p kt f", p=128),
            ).then_inc(s_fw[0], 16)
            sync.dma_start(
                out=wfin[2 + nh][:].rearrange("p (kt f) -> p kt f", f=512),
                in_=wscr[7 * D:8 * D, nh * 512:(nh + 1) * 512]
                .rearrange("(kt p) f -> p kt f", p=128),
            ).then_inc(s_fw[0], 16)

    @blk.gpsimd
    def _(g):
        g.dma_start(out=bias_sb[:], in_=bias_c).then_inc(s_w, 16)
        g.dma_start(out=bo_sb[:], in_=bo_r).then_inc(s_w, 16)
        g.dma_start(out=inc_sb[:], in_=inc_c).then_inc(s_w, 16)
        g.dma_start(out=id_sb[:], in_=ident).then_inc(s_w, 16)
        g.dma_start(out=ones_sb[:], in_=ones128).then_inc(s_w, 16)
        if not fused:
            for i in range(64):
                g.wait_ge(s_wsc, i + 1)
                g.dma_start(out=wscr[i * 128:(i + 1) * 128, :], in_=wstgb[i % 2][:]).then_inc(s_wst[i % 2], 16)
        for bstep in range(4):
            g.wait_ge(s_ln, bstep * 4)
            g.dma_start(out=xr[:].rearrange("p (tt d) -> p tt d", d=D),
                        in_=x4row_in[bstep * 4:(bstep + 1) * 4]
                        .rearrange("tt p d -> p tt d")).then_inc(s_x, 16)
        for fb in range(4):
            for pt in range(NPT):
                gp = fb * NPT + pt
                if gp > 0:
                    g.wait_ge(s_st, 16 * gp)
                g.dma_start(out=xout[:], in_=x4row_in[fb * 4 + pt]).then_inc(s_xo, 16)
                g.wait_ge(s_fin, gp + 1)
                g.dma_start(out=out_rows[fb * 4 + pt], in_=xout[:]).then_inc(s_st, 16)
        g.wait_ge(s_st, 16 * 16)

    @blk.vector
    def _(vector):
        if not fused:
            for i in range(64):
                par = i % 2
                vector.wait_ge(s_ws[par], 16 * (i // 2 + 1))
                if i >= 2:
                    vector.wait_ge(s_wst[par], 16 * (i // 2))
                vector.tensor_copy(wstgb[par][:], wstg[par][:])
                vector.drain()
                vector.engine_nop().then_inc(s_wsc, 1)
        vector.wait_ge(s_w, 16 * 5)
        vector.tensor_copy(bo_bf[:], bo_sb[:])
        vector.drain()
        vector.engine_nop().then_inc(s_wc2, 1)
        vector.memset(ones_t[:], 1.0)
        vector.drain()
        for ft in range(ND8):
            vector.tensor_scalar(cst[:, ft * POS:(ft + 1) * POS], ones_t[:],
                                 inc_sb[:, ft:ft + 1], None, ALU.mult)
        vector.drain()
        vector.tensor_copy(cbf[:], cst[:])
        vector.drain()
        vector.engine_nop().then_inc(s_ci, 1)
        for bstep in range(4):
            vector.wait_ge(s_x, 16 * (bstep + 1))
            for tt in range(4):
                vector.reduce_sum(m_t[:, tt:tt + 1], xr[:, tt * D:(tt + 1) * D], AX)
            vector.drain()
            vector.tensor_scalar(m_t[:], m_t[:], 1.0 / D, None, ALU.mult)
            vector.drain()
            vector.tensor_tensor(t1_t[:], m_t[:], m_t[:], ALU.mult)
            vector.wait_ge(s_sq, bstep * 4 + 4)
            vector.tensor_scalar(var_t[:], sq_t[:], 1.0 / D, None, ALU.mult)
            vector.drain()
            vector.tensor_tensor(var_t[:], var_t[:], t1_t[:], ALU.subtract)
            vector.drain()
            vector.tensor_scalar(var_t[:], var_t[:], LN_EPS, None, ALU.add)
            vector.drain()
            vector.engine_nop().then_inc(s_var, 1)
            vector.wait_ge(s_rstd, bstep + 1)
            if bstep > 0:
                vector.wait_ge(s_lnTe, bstep * 4 * ND8)
            for tt in range(4):
                vector.tensor_scalar(
                    ln4[:, tt * D:(tt + 1) * D], xr[:, tt * D:(tt + 1) * D],
                    m_t[:, tt:tt + 1], rstd_t[:, tt:tt + 1], ALU.subtract, ALU.mult,
                ).then_inc(s_ln, 1)
            vector.drain()
            for ft in range(ND8):
                slot = bstep * 8 + ft
                vector.wait_ge(s_ge, 3 * (slot + 1))
                cs = cst[:, ft * POS:(ft + 1) * POS]
                vector.tensor_tensor(t1g[:], gi[:], gg[:], ALU.mult)
                vector.drain()
                vector.tensor_tensor(cs, cs, gf[:], ALU.mult)
                vector.drain()
                vector.tensor_tensor(cs, cs, t1g[:], ALU.add)
                vector.drain()
                vector.engine_nop().then_inc(s_cu, 1)
            vector.tensor_copy(cbf[:], cst[:])
            vector.drain()
            vector.engine_nop().then_inc(s_cb, 1)
        for fb in range(4):
            for pt in range(NPT):
                gp = fb * NPT + pt
                vector.wait_ge(s_xo, 16 * (gp + 1))
                vector.wait_ge(s_he, gp + 1)
                vector.tensor_tensor(xout[:], xout[:], scr[:], ALU.add)
                vector.drain()
                vector.engine_nop().then_inc(s_fin, 1)

    @blk.scalar
    def _(scalar):
        for bstep in range(4):
            scalar.wait_ge(s_x, 16 * (bstep + 1))
            for tt in range(4):
                scalar.activation(scr[:], xr[:, tt * D:(tt + 1) * D], AF.Square,
                                  accum_out=sq_t[:, tt:tt + 1]).then_inc(s_sq, 1)
                scalar.drain()
            scalar.wait_ge(s_var, bstep + 1)
            scalar.activation(rstd_t[:], var_t[:], AF.Ln)
            scalar.drain()
            scalar.activation(rstd_t[:], rstd_t[:], AF.Exp, scale=-0.5).then_inc(s_rstd, 1)
            for i in range(4 * ND8):
                scalar.wait_ge(s_lnT, bstep * 4 * ND8 + i + 1)
                tt, dt = divmod(i, ND8)
                scalar.copy(ln4T[:, (bstep * ND8 + dt) * POS + tt * 128:
                               (bstep * ND8 + dt) * POS + (tt + 1) * 128],
                            pT[i % 2][:]).then_inc(s_lnTe, 1)
            for ft in range(ND8):
                slot = bstep * 8 + ft
                scalar.wait_ge(s_gm, slot + 1)
                if slot >= 1:
                    scalar.wait_ge(s_cu, slot)
                scalar.activation(gf[:], pF[:], AF.Sigmoid,
                                  bias=bias_sb[:, ft:ft + 1]).then_inc(s_ge, 1)
                scalar.activation(gi[:], pI[:], AF.Sigmoid,
                                  bias=bias_sb[:, ND8 + ft:ND8 + ft + 1]).then_inc(s_ge, 1)
                scalar.activation(gg[:], pG[:], AF.Tanh,
                                  bias=bias_sb[:, 2 * ND8 + ft:2 * ND8 + ft + 1]).then_inc(s_ge, 1)
                scalar.drain()
        for fb in range(4):
            for pt in range(NPT):
                gp = fb * NPT + pt
                scalar.wait_ge(s_hm, gp * 2 + 2)
                if gp > 0:
                    scalar.wait_ge(s_fin, gp)
                scalar.activation(scr[:, 0:512], pH[0][:], AF.Tanh)
                scalar.drain()
                scalar.activation(scr[:, 512:1024], pH[1][:], AF.Tanh).then_inc(s_he, 1)
                scalar.drain()

    @blk.tensor
    def _(tensor):
        tensor.wait_ge(s_w, 16 * 5)
        for bstep in range(4):
            for i in range(4 * ND8):
                tt, dt = divmod(i, ND8)
                tensor.wait_ge(s_ln, bstep * 4 + tt + 1)
                g = bstep * 4 * ND8 + i
                if g >= 2:
                    tensor.wait_ge(s_lnTe, g - 1)
                tensor.transpose(pT[i % 2][:], ln4[:, tt * D + dt * 128: tt * D + (dt + 1) * 128],
                                 id_sb[:]).then_inc(s_lnT, 1)
            tensor.wait_ge(s_lnTe, (bstep + 1) * 4 * ND8)
            if bstep == 0:
                tensor.wait_ge(s_ci, 1)
            else:
                tensor.wait_ge(s_cb, bstep)
            for ft in range(ND8):
                slot = bstep * 8 + ft
                if slot >= 1:
                    tensor.wait_ge(s_ge, 3 * slot)
                for j, ps in ((0, pF), (1, pI), (2, pG)):
                    buf = j
                    tensor.wait_ge(s_sl[buf], 16 * (slot + 1))
                    for kt in range(ND8):
                        tensor.matmul(
                            ps[:],
                            wsl[buf][:, kt * 128:(kt + 1) * 128],
                            ln4T[:, (bstep * ND8 + kt) * POS:(bstep * ND8 + kt + 1) * POS],
                            start=(kt == 0), stop=False)
                for j, ps in ((3, pF), (4, pI), (5, pG)):
                    buf = j
                    tensor.wait_ge(s_sl[buf], 16 * (slot + 1))
                    for kt in range(ND8):
                        mm = tensor.matmul(
                            ps[:],
                            wsl[buf][:, kt * 128:(kt + 1) * 128],
                            cbf[:, kt * POS:(kt + 1) * POS],
                            start=False, stop=(kt == ND8 - 1))
                        if j == 5 and kt == ND8 - 1:
                            mm.then_inc(s_gm, 1)
        tensor.wait_ge(s_fw[0], 16 * 4)
        tensor.wait_ge(s_cb, 4)
        tensor.wait_ge(s_wc2, 1)
        for fb in range(4):
            for pt in range(NPT):
                gp = fb * NPT + pt
                for nh in range(2):
                    g = gp * 2 + nh
                    if g >= 2:
                        tensor.wait_ge(s_he, g // 2)
                    for kt in range(ND8):
                        tensor.matmul(
                            pH[nh][:],
                            ln4T[:, (fb * ND8 + kt) * POS + pt * 128:(fb * ND8 + kt) * POS + (pt + 1) * 128],
                            wfin[nh][:, kt * 512:(kt + 1) * 512],
                            start=(kt == 0), stop=False)
                    for kt in range(ND8):
                        tensor.matmul(
                            pH[nh][:],
                            cbf[:, kt * POS + pt * 128: kt * POS + (pt + 1) * 128],
                            wfin[2 + nh][:, kt * 512:(kt + 1) * 512],
                            start=False, stop=False)
                    tensor.matmul(pH[nh][:], ones_sb[:], bo_bf[:, nh * 512:(nh + 1) * 512],
                                  start=False, stop=True).then_inc(s_hm, 1)

    if fused:
        p.pop_scope()
        p.pfx = ""
    return p


def host_prep_l1f(inputs):
    ws = np.concatenate([
        _f32(inputs["L1_ffn_wf"]).T, _f32(inputs["L1_ffn_wi"]).T, _f32(inputs["L1_ffn_wc"]).T,
        _f32(inputs["L1_ffn_uf"]).T, _f32(inputs["L1_ffn_ui"]).T, _f32(inputs["L1_ffn_uc"]).T,
        _f32(inputs["L1_ffn_wo"]).T, _f32(inputs["L1_ffn_uo"]).T], axis=0)
    bias_c = np.concatenate([
        _f32(inputs["L1_ffn_bf"]).reshape(8, 128).T,
        _f32(inputs["L1_ffn_bi"]).reshape(8, 128).T,
        _f32(inputs["L1_ffn_bc"]).reshape(8, 128).T], axis=1)
    return {
        "wstack": ws, "bias_c": bias_c,
        "bo_r": _f32(inputs["L1_ffn_bo"]).reshape(1, D),
        "inc_c": _f32(inputs["L1_ffn_in_c"]).reshape(8, 128).T,
        "ident": np.eye(128, dtype=BF),
        "ones128": np.ones((1, 128), dtype=BF),
    }


def run_l1f(inputs, x4):
    p = build_l1f()
    com = host_prep_l1f(inputs)
    in_maps = []
    for k in range(NCORES):
        jts = [k, 15 - k, 16 + k, 31 - k]
        rows = np.zeros((16, 128, D), dtype=np.float32)
        for b in range(4):
            for a in range(4):
                rows[b * 4 + a] = x4[b * T + jts[a] * 128: b * T + (jts[a] + 1) * 128]
        m = dict(com)
        m["x4row"] = rows
        in_maps.append(m)
    res = run_bass_kernel_spmd(p.nc, in_maps, core_ids=list(range(NCORES)))
    out = np.zeros((TOK, D), dtype=np.float32)
    for k in range(NCORES):
        jts = [k, 15 - k, 16 + k, 31 - k]
        rows = np.asarray(res.results[k]["outrows"])
        for b in range(4):
            for a in range(4):
                out[b * T + jts[a] * 128: b * T + (jts[a] + 1) * 128] = rows[b * 4 + a]
    return out



def build_fused():
    p = Prog()
    ident = p.inp("ident", [128, 128], BF16)
    identf = p.inp("identf", [128, 128], F32)
    ones128 = p.inp("ones128", [1, 128], BF16)
    x2s = p.dram("x2s", [SHARD, D])
    wgstg = p.dram("wgstg", [D, WG_ROWS], BF16)
    wustg = p.dram("wustg", [D, WG_ROWS], BF16)
    wdstg = p.dram("wdstg", [FFN_I, WD_ROWS], BF16)
    wgg = p.dram("wgg", [NCORES * D, WG_ROWS], BF16)
    wug = p.dram("wug", [NCORES * D, WG_ROWS], BF16)
    wdg = p.dram("wdg", [NCORES * FFN_I, WD_ROWS], BF16)
    x3s = p.dram("x3s", [SHARD, D])
    x3g = p.dram("x3g", [NCORES * SHARD, D])
    ln3stg = p.dram("ln3stg", [SHARD, D], BF16)
    ln3g = p.dram("ln3g", [NCORES * SHARD, D], BF16)
    xgstg = p.dram("xgstg", [80, SHARD], BF16)
    xgg = p.dram("xgg", [NCORES * 80, SHARD], BF16)
    q1s = p.dram("q1s", [4, T])
    q3s = p.dram("q3s", [4, T])
    q16s = p.dram("q16s", [128, 32])
    x4s = p.dram("x4s", [16, 128, D])
    outrows = p.out("outrows", [16, 128, D])

    s_ph = p.sem_fixed("s_ph")
    s_clr = p.sem_fixed("s_clr")
    s_cc1 = p.sem_fixed("s_cc1")
    s_cc2 = p.sem_fixed("s_cc2")
    s_xt2 = p.sem_fixed("s_xt2")
    s_wpre = p.sem_fixed("s_wpre")
    wstack = p.inp("z_wstack", [8 * D, D])
    wscr = p.nc.dram_tensor("z_wscr", [8 * D, D], BF16, kind="Internal").ap()

    blk0 = p.block()
    xofs_reg = p.es.enter_context(p.nc.gpsimd.register("dyn_xofs"))
    base = dict(ident=ident, identf=identf, ones128=ones128, xofs_reg=xofs_reg)

    p.init_pools()

    def phase(idx, entry=None):
        p.begin_phase_sems()
        return PhasedBlk(blk0, s_ph, s_clr, idx, p.pool_range, entry)

    p.defer_pops = True
    build_l0a(p, phase(0), dict(base, x2=x2s))
    p.defer_pops = False
    build_l0f_wq(p, PlainBlk(blk0, s_ph), dict(base, wgstg=wgstg, wustg=wustg,
                                               wdstg=wdstg, wgg=wgg, wug=wug,
                                               wdg=wdg, s_cc=s_cc1))
    p.close_deferred()
    build_l0f(p, phase(1, entry=10), dict(base, x2=x2s, wgg=wgg, wug=wug, wdg=wdg, x3=x3s))
    build_c0(p, phase(2, entry=15), dict(base, x3=x3s, ln3stg=ln3stg, xgstg=xgstg,
                                         x3g=x3g, ln3g=ln3g, xgg=xgg, s_cc=s_cc2))
    build_c1(p, phase(3, entry=20), dict(base, q1s=q1s, q3s=q3s, q16s=q16s, xgg=xgg,
                                         s_cc=s_cc2, wstack=wstack, wscr=wscr, s_wpre=s_wpre))
    build_c2(p, phase(4, entry=25), dict(base, ln3g=ln3g, x3g=x3g, q16s=q16s,
                                         x4s=x4s, s_xt2=s_xt2, s_cc=s_cc2))
    build_l1f(p, phase(5, entry=30), dict(base, x4s=x4s, outrows=outrows,
                                          wstack=wstack, wscr=wscr, s_wpre=s_wpre))
    p.pool_mode = False
    return p


def host_prep_c2f(core):
    jts = [core, 15 - core, 16 + core, 31 - core]
    lens = [core + 1, 16 - core, 17 + core, 32 - core]
    masks = np.zeros((NSLOT, 128, 128), dtype=BF)
    itc = np.zeros((128, NSLOT), dtype=np.float32)
    tril = np.tril(np.ones((128, 128), np.float32)).astype(BF)
    ones = np.ones((128, 128), dtype=BF)

    def owner_row(b, trow):
        k = b * 2 + (0 if trow < SHARD else 1)
        return k * SHARD + (trow - (k % 2) * SHARD)

    s = 0
    for b in range(4):
        for sb_b in range(10):
            acc = SB_OF_ACC[sb_b]
            sb_local = sb_b - SB_OF_ACC.index(acc)
            jt = jts[acc]
            for k in range(8):
                it = sb_local * 8 + k
                if it < lens[acc]:
                    masks[s] = tril if it == jt else ones
                    itc[:, s] = 128.0 * it
                s += 1
    xtab = np.zeros((1, 16), dtype=np.int32)
    sel = np.zeros((32, 16), dtype=np.float32)
    for b in range(4):
        for a in range(4):
            xtab[0, b * 4 + a] = owner_row(b, jts[a] * 128) * D
            sel[jts[a], b * 4 + a] = 1.0
    return {
        "g_masks": masks, "g_itc": itc,
        "g_iota": np.tile(np.arange(128, dtype=np.float32), (128, 1)),
        "g_xtab": xtab, "m_sel": sel,
    }


def kernel_fused(inputs):
    x = _f32(inputs["L0_attn_x"]).reshape(TOK, D)
    a_com = {("a_" + k): v for k, v in host_prep_l0a(inputs).items()
             if k not in ("ident", "ones128")}
    wg, wu, wd = (_f32(inputs[k]) for k in ("L0_ffn_w_gate", "L0_ffn_w_up", "L0_ffn_w_down"))
    c_com = {("c_" + k): v for k, v in host_prep_c0(inputs).items()
             if k not in ("ident", "ones128", "identf")}
    m_com_full = host_prep_c1(inputs, [np.zeros((80, SHARD), BF)] * NCORES)
    m_com = {("m_" + k): v for k, v in m_com_full.items()
             if k not in ("identf", "xg_mega")}
    z_com = {("z_" + k): v for k, v in host_prep_l1f(inputs).items()
             if k not in ("ident", "ones128")}
    shared = {
        "ident": np.eye(128, dtype=BF),
        "identf": np.eye(128, dtype=np.float32),
        "ones128": np.ones((1, 128), dtype=BF),
    }
    prog = build_fused()
    in_maps = []
    for k in range(NCORES):
        m = dict(shared)
        m.update(a_com); m.update(c_com); m.update(m_com); m.update(z_com)
        m["a_x"] = np.ascontiguousarray(x[k * SHARD:(k + 1) * SHARD])
        m["q_wg"] = np.ascontiguousarray(wg[k * WG_ROWS:(k + 1) * WG_ROWS])
        m["q_wu"] = np.ascontiguousarray(wu[k * WG_ROWS:(k + 1) * WG_ROWS])
        m["q_wd"] = np.ascontiguousarray(wd[k * WD_ROWS:(k + 1) * WD_ROWS])
        m.update(host_prep_c2f(k))
        in_maps.append(m)
    res = run_bass_kernel_spmd(prog.nc, in_maps, core_ids=list(range(NCORES)))
    out = np.zeros((TOK, D), dtype=np.float32)
    for k in range(NCORES):
        jts = [k, 15 - k, 16 + k, 31 - k]
        rows = np.asarray(res.results[k]["outrows"])
        for b in range(4):
            for a in range(4):
                out[b * T + jts[a] * 128: b * T + (jts[a] + 1) * 128] = rows[b * 4 + a]
    return _f32(out.reshape(B, T, D))
